# version 01
# baseline (speedup 1.0000x reference)
import contextlib
import numpy as np
import concourse.bass as bass
import concourse.mybir as mybir
from concourse.bass_utils import run_bass_kernel_spmd

F32 = mybir.dt.float32
BF16 = mybir.dt.bfloat16
AF = mybir.ActivationFunctionType
ALU = mybir.AluOpType
AX = mybir.AxisListType

NCORES = 8
BPC = 2
S = 4096
D = 1024
H = 8
HID = 4096
NT = S // 128
QB = 512
NQB = S // QB
NKT = S // 128
SCALE = 1.0 / float(np.sqrt(160.0))
EPS = 1e-5


class SemCtr:
    def __init__(self, nc, es, name):
        self.sem = es.enter_context(nc.semaphore(name))
        self.n = 0
        self.last = {}

    def inc(self, instr, k=1):
        instr.then_inc(self.sem, k)
        self.n += k
        return self.n

    def wait(self, eng, key, at=None):
        v = self.n if at is None else at
        if v <= 0:
            return
        if self.last.get(key, -1) >= v:
            return
        eng.wait_ge(self.sem, v)
        self.last[key] = v


def build_program():
    import os
    DBG_SEQS = int(os.environ.get("KMK_SEQS", BPC))
    DBG_NT = int(os.environ.get("KMK_NT", NT))
    DBG_NQB = int(os.environ.get("KMK_NQB", NQB))
    DBG_PAIRS = int(os.environ.get("KMK_PAIRS", 4))
    DBG_FFN = int(os.environ.get("KMK_FFN", 1))
    DBG_DUMP = int(os.environ.get("KMK_DUMP", 0))
    DBG_PREP = int(os.environ.get("KMK_PREP", 99))
    nc = bass.Bass("TRN2", target_bir_lowering=False)

    dx = nc.dram_tensor("x", [BPC, S, D], F32, kind="ExternalInput")
    dy = nc.dram_tensor("y", [BPC, S, D], F32, kind="ExternalOutput")
    dwcat = nc.dram_tensor("wcat", [D, 96], F32, kind="ExternalInput")
    dbcat = nc.dram_tensor("bcat", [96, 1], F32, kind="ExternalInput")
    dwuq = nc.dram_tensor("wuq", [32, 1024], F32, kind="ExternalInput")
    dwuk = nc.dram_tensor("wuk", [32, 1024], F32, kind="ExternalInput")
    dwuv = nc.dram_tensor("wuv", [32, 1024], F32, kind="ExternalInput")
    dwqr = nc.dram_tensor("wqr", [32, 256], F32, kind="ExternalInput")
    dbuq = nc.dram_tensor("buq8", [8, 128], F32, kind="ExternalInput")
    dbqr = nc.dram_tensor("bqr8", [8, 32], F32, kind="ExternalInput")
    dbuv = nc.dram_tensor("buv8", [8, 128], F32, kind="ExternalInput")
    dwo = nc.dram_tensor("wo", [1024, 1024], F32, kind="ExternalInput")
    dbo = nc.dram_tensor("bo", [1, 1024], F32, kind="ExternalInput")
    dg1 = nc.dram_tensor("g1_8", [8, 128], F32, kind="ExternalInput")
    db1ln = nc.dram_tensor("b1ln8", [8, 128], F32, kind="ExternalInput")
    db1 = nc.dram_tensor("b1row", [1, 4096], F32, kind="ExternalInput")
    dg2 = nc.dram_tensor("g2_32", [32, 128], F32, kind="ExternalInput")
    db2ln = nc.dram_tensor("b2ln32", [32, 128], F32, kind="ExternalInput")
    db2 = nc.dram_tensor("b2row", [1, 1024], F32, kind="ExternalInput")
    dw1 = nc.dram_tensor("w1", [HID, D], F32, kind="ExternalInput")
    dw2 = nc.dram_tensor("w2", [D, HID], F32, kind="ExternalInput")
    if DBG_DUMP:
        ddump = nc.dram_tensor("dump", [128, 8192], F32, kind="ExternalOutput")

    from concourse.hw_specs import get_activation_tables
    act_sets = list(get_activation_tables(nc.m.arch).keys())
    ID_NLEXP = act_sets.index("natural_log_exp_and_others")
    ID_GELU = act_sets.index("gelu_and_others")

    es = contextlib.ExitStack()
    with es:
        E = es.enter_context
        T1T = E(nc.sbuf_tensor("T1T", [128, 8 * 4096], BF16))
        W2T = E(nc.sbuf_tensor("W2T", [128, 32 * 1024], BF16))
        CKV2 = E(nc.sbuf_tensor("CKV2", [128, S], BF16))
        cqT = E(nc.sbuf_tensor("cqT", [32, S], BF16))
        ckvkm = E(nc.sbuf_tensor("ckvkm", [128, 33 * NKT], BF16))
        z1T = E(nc.sbuf_tensor("z1T", [128, 2048], BF16))
        PT0 = E(nc.sbuf_tensor("PT0", [128, 1024], BF16))
        PT1 = E(nc.sbuf_tensor("PT1", [128, 1024], BF16))
        Q2 = E(nc.sbuf_tensor("Q2", [128, QB], BF16))
        OVT = E(nc.sbuf_tensor("OVT", [128, 1024], BF16))
        WVO = E(nc.sbuf_tensor("WVO", [128, 2048], BF16))
        wcat8 = E(nc.sbuf_tensor("wcat8", [128, 8 * 96], BF16))
        wqkr = E(nc.sbuf_tensor("wqkr", [32, 8 * 64], BF16))
        qbias = E(nc.sbuf_tensor("qbias", [128, 4], F32))
        battnbc = E(nc.sbuf_tensor("battnbc", [128, 1024], BF16))
        b2bc = E(nc.sbuf_tensor("b2bc", [128, 1024], BF16))
        beff1T = E(nc.sbuf_tensor("beff1T", [128, 32], F32))
        identbf = E(nc.sbuf_tensor("identbf", [128, 128], BF16))
        identf32 = E(nc.sbuf_tensor("identf32", [32, 32], F32))
        arena = E(nc.sbuf_tensor("arena", [128, 17408], BF16))
        xping = E(nc.sbuf_tensor("xping", [128, 512], F32))
        xpong = E(nc.sbuf_tensor("xpong", [128, 512], F32))
        bconst = E(nc.sbuf_tensor("bconst", [128, 1], F32))
        onesbf = E(nc.sbuf_tensor("onesbf", [128, 1], BF16))
        onescf = E(nc.sbuf_tensor("onescf", [128, 1], F32))
        onesrow = E(nc.sbuf_tensor("onesrow", [1, 128], F32))
        negsc = E(nc.sbuf_tensor("negsc", [1, 128], F32))
        epsv = E(nc.sbuf_tensor("epsv", [128, 1], F32))
        biasC = E(nc.sbuf_tensor("biasC", [128, 1], F32))
        Mk2 = E(nc.sbuf_tensor("Mk2", [1, 1], F32))
        ms = E(nc.sbuf_tensor("ms", [1, 1], F32))
        thrv = E(nc.sbuf_tensor("thrv", [128, 1], F32))
        nthrv = E(nc.sbuf_tensor("nthrv", [128, 1], F32))
        mvec = E(nc.sbuf_tensor("mvec", [128, 1], F32))
        gcols = E(nc.sbuf_tensor("gcols", [128, 32], F32))
        blncols = E(nc.sbuf_tensor("blncols", [128, 32], BF16))
        stat = E(nc.sbuf_tensor("stat", [128, 8], F32))
        rstd2c = E(nc.sbuf_tensor("rstd2c", [128, 2], F32))

        tmp96 = Q2[:, 0:128]
        P0 = E(nc.psum_tensor("P0", [128, 2048], F32))
        P1 = E(nc.psum_tensor("P1", [128, 2048], F32))
        pbf0 = P0.bitcast(BF16)
        pbf1 = P1.bitcast(BF16)

        cPE = SemCtr(nc, es, "cPE")
        cV = SemCtr(nc, es, "cV")
        cS = SemCtr(nc, es, "cS")
        cG = SemCtr(nc, es, "cG")
        cLD = SemCtr(nc, es, "cLD")
        cGD = SemCtr(nc, es, "cGD")

        block = E(nc.Block())

        a1g = arena[:, 0:8192]
        x1stage = arena[:, 8192:12288].bitcast(F32)
        outbuf = arena[:, 12288:16384].bitcast(F32)
        z2buf = arena[:, 16384:16896]
        mu2bc = arena[:, 16896:17408].bitcast(F32)
        wchunk = arena[:, 0:8192].bitcast(F32)
        bocur = arena[:, 0:8192].bitcast(F32)[0:1, 0:4096]
        brow2 = arena[:, 8192:17408].bitcast(F32)[0:1, 4096:4608]
        trow = arena[:, 8192:17408]
        trowf = arena[:, 8192:17408].bitcast(F32)
        Q2sq = z1T[:, 0:512]
        recs = z1T.bitcast(F32)[:, 512:1024]
        rowf = PT1.bitcast(F32)
        ln2scr = PT0.bitcast(F32)

        def load_act(s, set_id):
            s.add_instruction(mybir.InstLoadActFuncSet(
                name=nc.get_next_instruction_name(),
                act_func_set_id=set_id, ins=[], outs=[]))

        @block.gpsimd
        def _(g):
            g.iota(wchunk[:, 0:128], pattern=[[1, 128]], base=0,
                   channel_multiplier=0, allow_small_or_imprecise_dtypes=True)
            ins = g.iota(wchunk[:, 128:129], pattern=[[0, 1]], base=0,
                         channel_multiplier=1, allow_small_or_imprecise_dtypes=True)
            cG.inc(ins)
            cGD.inc(g.dma_start(wcat8[:].rearrange("p (c n) -> p c n", c=8),
                                dwcat.ap().rearrange("(c p) n -> p c n", p=128)), 16)
            cGD.inc(g.dma_start(trow[0:32, 0:1024], dwuq.ap()), 16)
            cGD.inc(g.dma_start(trow[0:32, 1024:2048], dwuk.ap()), 16)
            cGD.inc(g.dma_start(trow[0:32, 2048:3072], dwuv.ap()), 16)
            cGD.inc(g.dma_start(trow[0:32, 3072:3328], dwqr.ap()), 16)
            cGD.inc(g.dma_start(trow[0:8, 3328:3456], dbuq.ap()), 16)
            cGD.inc(g.dma_start(trow[0:8, 3456:3584], dbuv.ap()), 16)

        @block.sync
        def _(sy):
            cLD.inc(sy.dma_start(bconst[0:96, :], dbcat.ap()), 16)
            cLD.inc(sy.dma_start(trowf[0:8, 3208:3240], dbqr.ap()), 16)

        @block.vector
        def _(v):
            cG.wait(v, "v")
            ins = v.tensor_scalar(wchunk[:, 256:384], wchunk[:, 0:128],
                                  wchunk[:, 128:129], None, op0=ALU.is_equal)
            cV.inc(ins)
            cV.wait(v, "v")
            v.tensor_copy(identbf[:], wchunk[:, 256:384])
            v.tensor_copy(identf32[:], wchunk[0:32, 256:288])
            v.memset(onesbf[:], 1.0)
            v.memset(onescf[:], 1.0)
            v.memset(onesrow[:], 1.0)
            v.memset(negsc[:], -SCALE)
            v.memset(epsv[:], EPS)
            ins = v.memset(ckvkm[:, 32:33 * NKT:33], 1.0)
            cV.inc(ins)


        if DBG_PREP <= 1:
            @block.gpsimd
            def _(g):
                cGD.wait(g, "gfinal")
            return nc
        WUQT = trow[:, 3584:3840]
        WUKT = trow[:, 3840:4096]
        WUVT = trow[:, 4096:4352]
        buqc = trow[:, 4352:4360]
        buvc = trow[:, 4360:4368]

        @block.tensor
        def _(t):
            cGD.wait(t, "t")
            for h in range(H):
                cV.wait(t, "t")
                t.transpose(pbf0[:, 0:32], trow[0:32, h * 128:(h + 1) * 128],
                            identbf[0:32, 0:32])
                t.transpose(pbf0[:, 1024:1056], trow[0:32, 1024 + h * 128:1024 + (h + 1) * 128],
                            identbf[0:32, 0:32])
                ins = t.transpose(pbf0[:, 2048:2080], trow[0:32, 2048 + h * 128:2048 + (h + 1) * 128],
                                  identbf[0:32, 0:32])
                cPE.inc(ins)

                @block.vector
                def _(v, h=h):
                    cPE.wait(v, "v")
                    v.tensor_copy(WUQT[:, h * 32:(h + 1) * 32], pbf0[:, 0:32])
                    v.tensor_copy(WUKT[:, h * 32:(h + 1) * 32], pbf0[:, 1024:1056])
                    ins = v.tensor_copy(WUVT[:, h * 32:(h + 1) * 32], pbf0[:, 2048:2080])
                    cV.inc(ins)

        @block.tensor
        def _(t):
            cV.wait(t, "t")
            cLD.wait(t, "t")
            t.transpose(pbf0[:, 0:8], trow[0:8, 3328:3456], identbf[0:8, 0:8])
            ins = t.transpose(pbf0[:, 1024:1032], trow[0:8, 3456:3584], identbf[0:8, 0:8])
            cPE.inc(ins)
            ins = t.transpose(P0[0:32, 1536:1544], trowf[0:8, 3208:3240], identf32[0:8, 0:8])
            cPE.inc(ins)

        @block.vector
        def _(v):
            cPE.wait(v, "v")
            v.tensor_copy(buqc[:], pbf0[:, 0:8])
            ins = v.tensor_copy(buvc[:], pbf0[:, 1024:1032])
            cV.inc(ins)
            for h in range(H):
                r0 = 32 if h % 2 == 0 else 96
                ins = v.tensor_copy(qbias[r0:r0 + 32, h // 2:h // 2 + 1],
                                    P0[0:32, 1536 + h:1537 + h])
            cV.inc(ins)


        if DBG_PREP <= 2:
            @block.gpsimd
            def _(g):
                cGD.wait(g, "gfinal")
            return nc
        @block.tensor
        def _(t):
            for h in range(H):
                cV.wait(t, "t")
                t.matmul(P1[0:32, 0:32], WUQT[:, h * 32:(h + 1) * 32],
                         WUKT[:, h * 32:(h + 1) * 32], start=True, stop=True)
                ins = t.matmul(P1[0:32, 512:513], WUKT[:, h * 32:(h + 1) * 32],
                               buqc[:, h:h + 1], start=True, stop=True)
                cPE.inc(ins)

                @block.vector
                def _(v, h=h):
                    cPE.wait(v, "v")
                    v.tensor_copy(wqkr[:, h * 64:h * 64 + 32], P1[0:32, 0:32])
                    r0 = 0 if h % 2 == 0 else 64
                    v.tensor_copy(qbias[r0:r0 + 32, h // 2:h // 2 + 1], P1[0:32, 512:513])
                    ins = v.tensor_copy(wqkr[:, h * 64 + 32:h * 64 + 64],
                                        trow[0:32, 3072 + h * 32:3072 + (h + 1) * 32])
                    cV.inc(ins)


        if DBG_PREP <= 3:
            @block.gpsimd
            def _(g):
                cGD.wait(g, "gfinal")
            return nc
        @block.sync
        def _(sy):
            cV.wait(sy, "sy")
            ins = sy.dma_start(bocur[0:1, 0:1024], dbo.ap())
            cLD.inc(ins, 16)

        for c in range(8):
            @block.gpsimd
            def _(g, c=c):
                cPE.wait(g, "g")
                ins = g.dma_start(trow[:, 4368 + (c % 2) * 1024: 4368 + (c % 2) * 1024 + 1024],
                                  dwo.ap()[c * 128:(c + 1) * 128, :])
                cGD.inc(ins, 16)

            @block.tensor
            def _(t, c=c):
                cGD.wait(t, "t")
                cV.wait(t, "t")
                wob = trow[:, 4368 + (c % 2) * 1024: 4368 + (c % 2) * 1024 + 1024]
                t.matmul(P1[0:32, 1024:1536], WUVT[:, c * 32:(c + 1) * 32], wob[:, 0:512],
                         start=True, stop=True)
                t.matmul(P1[0:32, 1536:2048], WUVT[:, c * 32:(c + 1) * 32], wob[:, 512:1024],
                         start=True, stop=True)
                t.matmul(P0[0:1, 0:512], buvc[:, c:c + 1], wob[:, 0:512],
                         start=(c == 0), stop=(c == 7))
                ins = t.matmul(P0[0:1, 512:1024], buvc[:, c:c + 1], wob[:, 512:1024],
                               start=(c == 0), stop=(c == 7))
                cPE.inc(ins)

            @block.vector
            def _(v, c=c):
                cPE.wait(v, "v")
                r0 = 32 * (c % 4)
                ins = v.tensor_copy(WVO[r0:r0 + 32, (c // 4) * 1024:(c // 4) * 1024 + 1024],
                                    P1[0:32, 1024:2048])
                cV.inc(ins)

        @block.vector
        def _(v):
            cPE.wait(v, "v")
            cLD.wait(v, "v")
            ins = v.tensor_tensor(bocur[0:1, 0:1024], P0[0:1, 0:1024], bocur[0:1, 0:1024],
                                  op=ALU.add)
            cV.inc(ins)

        @block.tensor
        def _(t):
            cV.wait(t, "t")
            t.matmul(P1[:, 0:512], onesrow[:], bocur[0:1, 0:512], start=True, stop=True)
            ins = t.matmul(P1[:, 512:1024], onesrow[:], bocur[0:1, 512:1024], start=True, stop=True)
            cPE.inc(ins)

        @block.vector
        def _(v):
            cPE.wait(v, "v")
            ins = v.tensor_copy(battnbc[:], P1[:, 0:1024])
            cV.inc(ins)


        if DBG_PREP <= 4:
            @block.gpsimd
            def _(g):
                cGD.wait(g, "gfinal")
            return nc
        def load_gcols(widx):
            nk = 8 if widx == 1 else 32
            src = dg1 if widx == 1 else dg2

            @block.sync
            def _(sy):
                cPE.wait(sy, "sy")
                ins = sy.dma_start(trowf[0:nk, 2048:2176], src.ap())
                cLD.inc(ins, 16)

            @block.tensor
            def _(t):
                cLD.wait(t, "t")
                cV.wait(t, "t")
                ins = t.transpose(P0[:, 1024:1024 + nk], trowf[0:nk, 2048:2176],
                                  identf32[0:nk, 0:nk])
                cPE.inc(ins)

            @block.vector
            def _(v):
                cPE.wait(v, "v")
                ins = v.tensor_copy(gcols[:, 0:nk], P0[:, 1024:1024 + nk])
                cV.inc(ins)

        def ternarize(widx):
            dw = dw1 if widx == 1 else dw2
            nchunks = 32 if widx == 1 else 8
            cols = 1024 if widx == 1 else 4096
            TT = T1T if widx == 1 else W2T
            nblk = cols // 128
            nk = 8 if widx == 1 else 32

            @block.vector
            def _(v):
                ins = v.memset(stat[:, 4:5], 0.0)
                cV.inc(ins)

            for c in range(nchunks):
                @block.sync
                def _(sy, c=c):
                    cV.wait(sy, "sy")
                    ins = sy.dma_start(wchunk[:, 0:cols], dw.ap()[c * 128:(c + 1) * 128, :])
                    cLD.inc(ins, 16)

                @block.vector
                def _(v, c=c):
                    cLD.wait(v, "v")
                    ins = v.reduce_sum(stat[:, 5:6], wchunk[:, 0:cols], axis=AX.X,
                                       apply_absolute_value=True)
                    cV.inc(ins)
                    cV.wait(v, "v")
                    ins = v.tensor_tensor(stat[:, 4:5], stat[:, 4:5], stat[:, 5:6], op=ALU.add)
                    cV.inc(ins)

            @block.tensor
            def _(t):
                cV.wait(t, "t")
                ins = t.matmul(P0[0:1, 0:1], onescf[:], stat[:, 4:5], start=True, stop=True)
                cPE.inc(ins)

            @block.vector
            def _(v):
                cPE.wait(v, "v")
                ins = v.tensor_scalar(ms[:], P0[0:1, 0:1], 1.0 / (4096.0 * 1024.0), 1e-5,
                                      op0=ALU.mult, op1=ALU.max)
                cV.inc(ins)

            @block.tensor
            def _(t):
                cV.wait(t, "t")
                ins = t.matmul(P0[:, 4:5], onesrow[:], ms[:], start=True, stop=True)
                cPE.inc(ins)

            @block.vector
            def _(v):
                cPE.wait(v, "v")
                ins = v.tensor_copy(mvec[:], P0[:, 4:5])
                cV.inc(ins)
                cV.wait(v, "v")
                ins = v.tensor_scalar(thrv[:], mvec[:], 0.5, None, op0=ALU.mult)
                cV.inc(ins)
                cV.wait(v, "v")
                ins = v.tensor_scalar(nthrv[:], thrv[:], -1.0, None, op0=ALU.mult)
                cV.inc(ins)
                gsl = gcols[:, 0:nk]
                ins = v.tensor_scalar(gsl, gsl, mvec[:], None, op0=ALU.mult)
                cV.inc(ins)

            for c in range(nchunks):
                @block.sync
                def _(sy, c=c):
                    cV.wait(sy, "sy")
                    cPE.wait(sy, "sy")
                    ins = sy.dma_start(wchunk[:, 0:cols], dw.ap()[c * 128:(c + 1) * 128, :])
                    cLD.inc(ins, 16)

                @block.vector
                def _(v, c=c):
                    cLD.wait(v, "v")
                    cPE.wait(v, "v")
                    ins = v.tensor_scalar(trowf[:, 0:cols], wchunk[:, 0:cols], thrv[:],
                                          nthrv[:], op0=ALU.min, op1=ALU.max)
                    cV.inc(ins)
                    cV.wait(v, "v")
                    ins = v.tensor_tensor(wchunk[:, 0:cols], wchunk[:, 0:cols],
                                          trowf[:, 0:cols], op=ALU.subtract)
                    cV.inc(ins)
                    cV.wait(v, "v")
                    ins = v.tensor_scalar(wchunk[:, 0:cols], wchunk[:, 0:cols], 1e30, 1.0,
                                          op0=ALU.mult, op1=ALU.min)
                    cV.inc(ins)
                    cV.wait(v, "v")
                    ins = v.tensor_scalar(trow[:, 0:cols], wchunk[:, 0:cols], -1.0, None,
                                          op0=ALU.max)
                    cV.inc(ins)

                @block.tensor
                def _(t, c=c):
                    for k in range(nblk):
                        cV.wait(t, "t")
                        ins = t.transpose(pbf0[:, (k % 2) * 1024:(k % 2) * 1024 + 128],
                                          trow[:, k * 128:(k + 1) * 128], identbf[:])
                        cPE.inc(ins)

                        @block.vector
                        def _(v, c=c, k=k):
                            cPE.wait(v, "v")
                            if widx == 1:
                                dst = TT[:, k * 4096 + c * 128: k * 4096 + (c + 1) * 128]
                            else:
                                dst = TT[:, k * 1024 + c * 128: k * 1024 + (c + 1) * 128]
                            ins = v.tensor_copy(dst, pbf0[:, (k % 2) * 1024:(k % 2) * 1024 + 128])
                            cV.inc(ins)

            @block.gpsimd
            def _(g):
                cV.wait(g, "g")
                src = db1ln if widx == 1 else db2ln
                ins = g.dma_start(trow[0:nk, 0:128], src.ap())
                cGD.inc(ins, 16)

            @block.tensor
            def _(t):
                cGD.wait(t, "t")
                ins = t.transpose(pbf0[:, 0:nk], trow[0:nk, 0:128], identbf[0:nk, 0:nk])
                cPE.inc(ins)

            @block.vector
            def _(v):
                cPE.wait(v, "v")
                ins = v.tensor_copy(blncols[:, 0:nk], pbf0[:, 0:nk])
                cV.inc(ins)

            ncolsO = 4096 if widx == 1 else 1024
            brow = db1 if widx == 1 else db2

            @block.sync
            def _(sy):
                cV.wait(sy, "sy")
                ins = sy.dma_start(bocur[0:1, 0:ncolsO], brow.ap())
                cLD.inc(ins, 16)

            for nb in range(ncolsO // 512):
                @block.tensor
                def _(t, nb=nb):
                    cV.wait(t, "t")
                    cLD.wait(t, "t")
                    for k in range(nk):
                        if widx == 1:
                            rhs = TT[:, k * 4096 + nb * 512: k * 4096 + nb * 512 + 512]
                        else:
                            rhs = TT[:, k * 1024 + (nb % 2) * 512: k * 1024 + (nb % 2) * 512 + 512]
                        ins = t.matmul(P0[0:1, 1024 + (nb % 2) * 512: 1536 + (nb % 2) * 512],
                                       blncols[:, k:k + 1], rhs,
                                       start=(k == 0), stop=(k == nk - 1))
                    cPE.inc(ins)

                @block.vector
                def _(v, nb=nb):
                    cPE.wait(v, "v")
                    ps = P0[0:1, 1024 + (nb % 2) * 512: 1536 + (nb % 2) * 512]
                    ins = v.tensor_scalar(brow2[0:1, 0:512], ps, ms[0:1, :], None, op0=ALU.mult)
                    cV.inc(ins)
                    cV.wait(v, "v")
                    ins = v.tensor_tensor(bocur[0:1, nb * 512:(nb + 1) * 512],
                                          bocur[0:1, nb * 512:(nb + 1) * 512],
                                          brow2[0:1, 0:512], op=ALU.add)
                    cV.inc(ins)

            if widx == 1:
                @block.tensor
                def _(t):
                    cV.wait(t, "t")
                    for nb in range(32):
                        ins = t.transpose(P0[:, nb:nb + 1], bocur[0:1, nb * 128:(nb + 1) * 128],
                                          identf32[0:1, 0:1])
                    cPE.inc(ins)

                @block.vector
                def _(v):
                    cPE.wait(v, "v")
                    ins = v.tensor_copy(beff1T[:], P0[:, 0:32])
                    cV.inc(ins)
            else:
                @block.tensor
                def _(t):
                    cV.wait(t, "t")
                    t.matmul(P0[:, 0:512], onesrow[:], bocur[0:1, 0:512], start=True, stop=True)
                    ins = t.matmul(P0[:, 512:1024], onesrow[:], bocur[0:1, 512:1024],
                                   start=True, stop=True)
                    cPE.inc(ins)

                @block.vector
                def _(v):
                    cPE.wait(v, "v")
                    ins = v.tensor_copy(b2bc[:], P0[:, 0:1024])
                    cV.inc(ins)

            @block.vector
            def _(v):
                cV.wait(v, "v")
                cPE.wait(v, "v")
                for k in range(nk):
                    sl = TT[:, k * 4096:(k + 1) * 4096] if widx == 1 else \
                        TT[:, k * 1024:(k + 1) * 1024]
                    ins = v.tensor_scalar(sl, sl, gcols[:, k:k + 1], None, op0=ALU.mult)
                    cV.inc(ins)
                    cV.wait(v, "v")

        load_gcols(1)
        ternarize(1)

        if DBG_PREP <= 5:
            @block.gpsimd
            def _(g):
                cGD.wait(g, "gfinal")
            return nc
        load_gcols(2)
        ternarize(2)

        if DBG_PREP <= 6:
            @block.gpsimd
            def _(g):
                cGD.wait(g, "gfinal")
            return nc


        @block.scalar
        def _(s):
            load_act(s, ID_NLEXP)

        for seq in range(DBG_SEQS):
            for tt in range(DBG_NT):
                xt = outbuf[:, (tt % 2) * 1024:(tt % 2) * 1024 + 1024]

                @block.sync
                def _(sy, tt=tt, xt=xt, seq=seq):
                    cV.wait(sy, "sy")
                    cS.wait(sy, "sy")
                    cGD.wait(sy, "sy")
                    ins = sy.dma_start(xt, dx.ap()[seq, tt * 128:(tt + 1) * 128, :])
                    cLD.inc(ins, 16)

                @block.vector
                def _(v, xt=xt):
                    cLD.wait(v, "v")
                    ins = v.reduce_sum(stat[:, 0:1], xt, axis=AX.X)
                    cV.inc(ins)

                @block.scalar
                def _(s, xt=xt):
                    cLD.wait(s, "s")
                    ins = s.activation(z1T[:, 0:1024], xt, AF.Square)
                    cS.inc(ins)

                @block.vector
                def _(v):
                    cS.wait(v, "v")
                    ins = v.reduce_sum(stat[:, 1:2], z1T[:, 0:1024], axis=AX.X)
                    cV.inc(ins)
                    cV.wait(v, "v")
                    ins = v.tensor_scalar(stat[:, 2:3], stat[:, 0:1], 1.0 / D, None, op0=ALU.mult)
                    cV.inc(ins)
                    cV.wait(v, "v")
                    ins = v.scalar_tensor_tensor(stat[:, 3:4], stat[:, 2:3], 0.0, stat[:, 2:3],
                                                 op0=ALU.add, op1=ALU.mult)
                    cV.inc(ins)
                    cV.wait(v, "v")
                    ins = v.scalar_tensor_tensor(stat[:, 4:5], stat[:, 1:2], 1.0 / D, stat[:, 3:4],
                                                 op0=ALU.mult, op1=ALU.subtract)
                    cV.inc(ins)

                @block.scalar
                def _(s):
                    cV.wait(s, "s")
                    ins = s.activation(stat[:, 5:6], stat[:, 4:5], AF.Ln, bias=epsv[:])
                    cS.inc(ins)
                    cS.wait(s, "s")
                    ins = s.activation(stat[:, 6:7], stat[:, 5:6], AF.Exp, scale=-0.5)
                    cS.inc(ins)

                @block.vector
                def _(v, xt=xt):
                    cS.wait(v, "v")
                    ins = v.tensor_scalar(z1T[:, 1024:2048], xt, stat[:, 2:3], stat[:, 6:7],
                                          op0=ALU.subtract, op1=ALU.mult)
                    cV.inc(ins)

                for c in range(8):
                    @block.tensor
                    def _(t, c=c):
                        cV.wait(t, "t")
                        ins = t.transpose(pbf0[:, (c % 2) * 1024:(c % 2) * 1024 + 128],
                                          z1T[:, 1024 + c * 128:1024 + (c + 1) * 128], identbf[:])
                        cPE.inc(ins)

                    @block.vector
                    def _(v, c=c):
                        cPE.wait(v, "v")
                        ins = v.tensor_copy(PT0[:, c * 128:(c + 1) * 128],
                                            pbf0[:, (c % 2) * 1024:(c % 2) * 1024 + 128])
                        cV.inc(ins)

                @block.tensor
                def _(t):
                    cV.wait(t, "t")
                    for c in range(8):
                        ins = t.matmul(P0[0:96, 1024:1152], wcat8[:, c * 96:(c + 1) * 96],
                                       PT0[:, c * 128:(c + 1) * 128],
                                       start=(c == 0), stop=(c == 7))
                    cPE.inc(ins)

                @block.scalar
                def _(s):
                    cPE.wait(s, "s")
                    ins = s.activation(tmp96[0:96, :], P0[0:96, 1024:1152], AF.Identity,
                                       bias=bconst[0:96, :])
                    cS.inc(ins)

                @block.tensor
                def _(t):
                    cS.wait(t, "t")
                    ins = t.transpose(pbf0[:, 3072:3104], tmp96[32:64, :], identbf[32:64, 32:64])
                    cPE.inc(ins)

                @block.vector
                def _(v, tt=tt):
                    cS.wait(v, "v")
                    cPE.wait(v, "v")
                    tc = slice(tt * 128, (tt + 1) * 128)
                    v.tensor_copy(cqT[:, tc], tmp96[0:32, :])
                    v.tensor_copy(CKV2[0:32, tc], tmp96[32:64, :])
                    v.tensor_copy(CKV2[32:64, tc], tmp96[64:96, :])
                    v.tensor_copy(CKV2[64:96, tc], tmp96[32:64, :])
                    v.tensor_copy(CKV2[96:128, tc], tmp96[64:96, :])
                    ins = v.tensor_copy(ckvkm[:, tt * 33:tt * 33 + 32], pbf0[:, 3072:3104])
                    cV.inc(ins)

            @block.vector
            def _(v):
                cV.wait(v, "v")
                v.memset(Mk2[:], 0.0)
                ins = v.scalar_tensor_tensor(a1g[0:64, 0:4096], CKV2[0:64, :], 0.0,
                                             CKV2[0:64, :], op0=ALU.add, op1=ALU.mult)
                cV.inc(ins)

            for nb in range(8):
                @block.tensor
                def _(t, nb=nb):
                    cV.wait(t, "t")
                    ins = t.matmul(P0[0:1, 0:512], onesbf[0:64, :],
                                   a1g[0:64, nb * 512:(nb + 1) * 512], start=True, stop=True)
                    cPE.inc(ins)

                @block.vector
                def _(v, nb=nb):
                    cPE.wait(v, "v")
                    ins = v.reduce_max(stat[0:1, 7:8], P0[0:1, 0:512], axis=AX.X)
                    cV.inc(ins)
                    cV.wait(v, "v")
                    ins = v.tensor_tensor(Mk2[:], Mk2[:], stat[0:1, 7:8], op=ALU.max)
                    cV.inc(ins)

            for qb in range(DBG_NQB):
                qsl = slice(qb * QB, (qb + 1) * QB)
                for p in range(DBG_PAIRS):
                    hA, hB = 2 * p, 2 * p + 1

                    @block.tensor
                    def _(t, p=p):
                        cV.wait(t, "t")
                        cS.wait(t, "t")
                        t.matmul(P0[0:64, 0:512], wqkr[:, hA * 64:(hA + 1) * 64],
                                 cqT[:, qsl], start=True, stop=True, tile_position=(0, 0))
                        ins = t.matmul(P0[64:128, 0:512], wqkr[:, hB * 64:(hB + 1) * 64],
                                       cqT[:, qsl], start=False, stop=True, tile_position=(0, 64))
                        cPE.inc(ins)

                    @block.scalar
                    def _(s, p=p):
                        cPE.wait(s, "s")
                        ins = s.activation(Q2[:], P0[:, 0:512], AF.Identity, bias=qbias[:, p:p + 1])
                        cS.inc(ins)

                    @block.vector
                    def _(v):
                        cS.wait(v, "v")
                        ins = v.scalar_tensor_tensor(Q2sq, Q2[:], 0.0, Q2[:],
                                                     op0=ALU.add, op1=ALU.mult)
                        cV.inc(ins)

                    @block.tensor
                    def _(t):
                        cV.wait(t, "t")
                        t.matmul(P0[0:1, 512:1024], onesbf[0:64, :], Q2sq[0:64, :],
                                 start=True, stop=True)
                        ins = t.matmul(P0[0:1, 1024:1536], onesbf[64:128, :], Q2sq[64:128, :],
                                       start=True, stop=True)
                        cPE.inc(ins)

                    @block.vector
                    def _(v):
                        cPE.wait(v, "v")
                        v.reduce_max(stat[0:1, 7:8], P0[0:1, 512:1024], axis=AX.X)
                        ins = v.reduce_max(stat[0:1, 6:7], P0[0:1, 1024:1536], axis=AX.X)
                        cV.inc(ins)
                        cV.wait(v, "v")
                        ins = v.tensor_tensor(stat[0:1, 5:6], stat[0:1, 7:8], stat[0:1, 6:7],
                                              op=ALU.max)
                        cV.inc(ins)
                        cV.wait(v, "v")
                        ins = v.tensor_scalar(stat[0:1, 5:6], stat[0:1, 5:6], Mk2[:], None,
                                              op0=ALU.max)
                        cV.inc(ins)

                    @block.tensor
                    def _(t):
                        cV.wait(t, "t")
                        ins = t.matmul(P0[:, 512:513], negsc[:], stat[0:1, 5:6],
                                       start=True, stop=True)
                        cPE.inc(ins)

                    @block.vector
                    def _(v):
                        cPE.wait(v, "v")
                        ins = v.tensor_copy(biasC[:], P0[:, 512:513])
                        cV.inc(ins)

                    mm_done = [0] * NKT
                    exp_done = [0] * NKT
                    o_done = [0] * NKT
                    vready = cV.n

                    def emit_O(kt):
                        @block.tensor
                        def _(t, kt=kt):
                            cS.wait(t, "t", at=exp_done[kt])
                            PT = PT0 if kt % 2 == 0 else PT1
                            t.matmul(P0[0:33, 1024:1536],
                                     ckvkm[:, kt * 33:kt * 33 + 33], PT[:, 0:512],
                                     start=(kt == 0), stop=(kt == NKT - 1), tile_position=(0, 0))
                            ins = t.matmul(P0[64:97, 1536:2048],
                                           ckvkm[:, kt * 33:kt * 33 + 33], PT[:, 512:1024],
                                           start=(kt == 0), stop=(kt == NKT - 1),
                                           tile_position=(0, 64))
                            o_done[kt] = cPE.inc(ins)

                    for kt in range(NKT):
                        @block.tensor
                        def _(t, kt=kt):
                            if kt >= 2:
                                cS.wait(t, "t", at=exp_done[kt - 2])
                            else:
                                cV.wait(t, "t", at=vready)
                            sl = (kt % 2) * 1024
                            t.matmul(P1[:, sl:sl + 512], CKV2[0:64, kt * 128:(kt + 1) * 128],
                                     Q2[0:64, :], start=True, stop=True, tile_position=(0, 0))
                            ins = t.matmul(P1[:, sl + 512:sl + 1024],
                                           CKV2[64:128, kt * 128:(kt + 1) * 128],
                                           Q2[64:128, :], start=True, stop=True,
                                           tile_position=(64, 0))
                            mm_done[kt] = cPE.inc(ins)

                        @block.scalar
                        def _(s, kt=kt):
                            cPE.wait(s, "s", at=mm_done[kt])
                            if kt >= 2:
                                cPE.wait(s, "s", at=o_done[kt - 2])
                            PT = PT0 if kt % 2 == 0 else PT1
                            ins = s.activation(PT[:], P1[:, (kt % 2) * 1024:(kt % 2) * 1024 + 1024],
                                               AF.Exp, bias=biasC[:], scale=SCALE)
                            exp_done[kt] = cS.inc(ins)

                        if kt >= 1:
                            emit_O(kt - 1)
                    emit_O(NKT - 1)

                    @block.vector
                    def _(v, p=p):
                        cPE.wait(v, "v")
                        v.tensor_copy(recs[0:33, :], P0[0:33, 1024:1536])
                        ins = v.tensor_copy(recs[64:97, :], P0[64:97, 1536:2048])
                        cV.inc(ins)
                        cV.wait(v, "v")
                        ins = v.reciprocal(ln2scr[0:1, 0:512], recs[32:33, :])
                        cV.inc(ins)
                        ins = v.reciprocal(rowf[0:1, 0:512], recs[96:97, :])
                        cV.inc(ins)

                    @block.tensor
                    def _(t, p=p):
                        cV.wait(t, "t")
                        t.matmul(P0[0:32, 0:512], onesrow[0:1, 0:32], ln2scr[0:1, 0:512],
                                 start=True, stop=True, tile_position=(0, 0))
                        ins = t.matmul(P0[64:96, 512:1024], onesrow[0:1, 0:32], rowf[0:1, 0:512],
                                       start=True, stop=True, tile_position=(0, 64))
                        cPE.inc(ins)

                    @block.vector
                    def _(v, p=p):
                        cPE.wait(v, "v")
                        hA, hB = 2 * p, 2 * p + 1
                        ins = v.tensor_tensor(OVT[32 * (hA % 4):32 * (hA % 4) + 32,
                                                  (hA // 4) * 512:(hA // 4) * 512 + 512],
                                              recs[0:32, :], P0[0:32, 0:512], op=ALU.mult)
                        cV.inc(ins)
                        ins = v.tensor_tensor(OVT[32 * (hB % 4):32 * (hB % 4) + 32,
                                                  (hB // 4) * 512:(hB // 4) * 512 + 512],
                                              recs[64:96, :], P0[64:96, 512:1024], op=ALU.mult)
                        cV.inc(ins)

                for h2 in range(2 if DBG_FFN else 0):
                    for qq in range(2):
                        q4 = h2 * 2 + qq
                        tok0 = qb * QB + q4 * 128
                        psX = P1[:, (qq % 2) * 1024:(qq % 2) * 1024 + 1024]
                        x1sl = x1stage[:, qq * 1024:qq * 1024 + 1024]

                        @block.sync
                        def _(sy, tok0=tok0, seq=seq):
                            cV.wait(sy, "sy")
                            cLD.inc(sy.dma_start(xping[:], dx.ap()[seq, tok0:tok0 + 128, 0:512]), 16)
                            cLD.inc(sy.dma_start(xpong[:], dx.ap()[seq, tok0:tok0 + 128, 512:1024]), 16)

                        @block.tensor
                        def _(t, q4=q4, psX=psX):
                            cV.wait(t, "t")
                            cS.wait(t, "t")
                            for cc in range(2):
                                for nn in range(2):
                                    ins = t.matmul(psX[:, nn * 512:(nn + 1) * 512],
                                                   OVT[:, cc * 512 + q4 * 128: cc * 512 + (q4 + 1) * 128],
                                                   WVO[:, cc * 1024 + nn * 512: cc * 1024 + (nn + 1) * 512],
                                                   start=(cc == 0), stop=(cc == 1))
                            cPE.inc(ins)

                        @block.vector
                        def _(v, psX=psX, x1sl=x1sl):
                            cPE.wait(v, "v")
                            cLD.wait(v, "v")
                            ins = v.tensor_tensor(x1sl, psX, battnbc[:], op=ALU.add)
                            cV.inc(ins)
                            cV.wait(v, "v")
                            v.tensor_tensor(x1sl[:, 0:512], x1sl[:, 0:512], xping[:], op=ALU.add)
                            ins = v.tensor_tensor(x1sl[:, 512:1024], x1sl[:, 512:1024], xpong[:],
                                                  op=ALU.add)
                            cV.inc(ins)
                            cV.wait(v, "v")
                            ins = v.reduce_sum(stat[:, 0:1], x1sl, axis=AX.X)
                            cV.inc(ins)

                        @block.scalar
                        def _(s, x1sl=x1sl):
                            cV.wait(s, "s")
                            ins = s.activation(a1g[:, 0:1024], x1sl, AF.Square)
                            cS.inc(ins)

                        @block.vector
                        def _(v):
                            cS.wait(v, "v")
                            ins = v.reduce_sum(stat[:, 1:2], a1g[:, 0:1024], axis=AX.X)
                            cV.inc(ins)
                            cV.wait(v, "v")
                            ins = v.tensor_scalar(stat[:, 2:3], stat[:, 0:1], 1.0 / D, None,
                                                  op0=ALU.mult)
                            cV.inc(ins)
                            cV.wait(v, "v")
                            ins = v.scalar_tensor_tensor(stat[:, 3:4], stat[:, 2:3], 0.0,
                                                         stat[:, 2:3], op0=ALU.add, op1=ALU.mult)
                            cV.inc(ins)
                            cV.wait(v, "v")
                            ins = v.scalar_tensor_tensor(stat[:, 4:5], stat[:, 1:2], 1.0 / D,
                                                         stat[:, 3:4], op0=ALU.mult, op1=ALU.subtract)
                            cV.inc(ins)

                        @block.scalar
                        def _(s):
                            cV.wait(s, "s")
                            ins = s.activation(stat[:, 5:6], stat[:, 4:5], AF.Ln, bias=epsv[:])
                            cS.inc(ins)
                            cS.wait(s, "s")
                            ins = s.activation(stat[:, 6:7], stat[:, 5:6], AF.Exp, scale=-0.5)
                            cS.inc(ins)

                        @block.vector
                        def _(v, x1sl=x1sl):
                            cS.wait(v, "v")
                            ins = v.tensor_scalar(PT1[:], x1sl, stat[:, 2:3], stat[:, 6:7],
                                                  op0=ALU.subtract, op1=ALU.mult)
                            cV.inc(ins)

                        for c in range(8):
                            @block.tensor
                            def _(t, c=c):
                                cV.wait(t, "t")
                                ins = t.transpose(pbf0[:, (c % 2) * 1024:(c % 2) * 1024 + 128],
                                                  PT1[:, c * 128:(c + 1) * 128], identbf[:])
                                cPE.inc(ins)

                            @block.vector
                            def _(v, c=c, qq=qq):
                                cPE.wait(v, "v")
                                ins = v.tensor_copy(
                                    z1T[:, c * 256 + qq * 128: c * 256 + qq * 128 + 128],
                                    pbf0[:, (c % 2) * 1024:(c % 2) * 1024 + 128])
                                cV.inc(ins)

                    @block.scalar
                    def _(s):
                        cS.wait(s, "s")
                        load_act(s, ID_GELU)

                    gel_done = [0] * 32
                    m1_done = [0] * 32
                    for mch in range(32):
                        @block.tensor
                        def _(t, mch=mch):
                            cV.wait(t, "t")
                            if mch >= 2:
                                cS.wait(t, "t", at=gel_done[mch - 2])
                            psM = P1[:, 1024 + (mch % 2) * 512: 1024 + (mch % 2) * 512 + 256]
                            for k in range(8):
                                ins = t.matmul(psM,
                                               T1T[:, k * 4096 + mch * 128: k * 4096 + (mch + 1) * 128],
                                               z1T[:, k * 256:(k + 1) * 256],
                                               start=(k == 0), stop=(k == 7))
                            m1_done[mch] = cPE.inc(ins)

                        @block.scalar
                        def _(s, mch=mch):
                            cPE.wait(s, "s", at=m1_done[mch])
                            psM = P1[:, 1024 + (mch % 2) * 512: 1024 + (mch % 2) * 512 + 256]
                            ins = s.activation(a1g[:, mch * 256:(mch + 1) * 256], psM, AF.Gelu,
                                               bias=beff1T[:, mch:mch + 1])
                            gel_done[mch] = cS.inc(ins)

                    sq_done = [0] * 32
                    for mch in range(32):
                        @block.vector
                        def _(v, mch=mch):
                            cS.wait(v, "v", at=gel_done[mch])
                            if mch >= 2:
                                cPE.wait(v, "v")
                            asl = a1g[:, mch * 256:(mch + 1) * 256]
                            ins = v.scalar_tensor_tensor(
                                z2buf[:, (mch % 2) * 256:(mch % 2) * 256 + 256],
                                asl, 0.0, asl, op0=ALU.add, op1=ALU.mult)
                            sq_done[mch] = cV.inc(ins)

                        @block.tensor
                        def _(t, mch=mch):
                            cV.wait(t, "t", at=sq_done[mch])
                            t.matmul(P0[0:1, 0:256], onesbf[:], a1g[:, mch * 256:(mch + 1) * 256],
                                     start=(mch == 0), stop=(mch == 31))
                            ins = t.matmul(P0[0:1, 256:512], onesbf[:],
                                           z2buf[:, (mch % 2) * 256:(mch % 2) * 256 + 256],
                                           start=False, stop=(mch == 31))
                            cPE.inc(ins)

                    @block.vector
                    def _(v):
                        cPE.wait(v, "v")
                        ins = v.tensor_scalar(rowf[0:1, 0:256], P0[0:1, 0:256], 1.0 / HID, None,
                                              op0=ALU.mult)
                        cV.inc(ins)
                        cV.wait(v, "v")
                        ins = v.scalar_tensor_tensor(rowf[0:1, 256:512], rowf[0:1, 0:256], 0.0,
                                                     rowf[0:1, 0:256], op0=ALU.add, op1=ALU.mult)
                        cV.inc(ins)
                        cV.wait(v, "v")
                        ins = v.scalar_tensor_tensor(ln2scr[0:1, 0:256], P0[0:1, 256:512],
                                                     1.0 / HID, rowf[0:1, 256:512],
                                                     op0=ALU.mult, op1=ALU.subtract)
                        cV.inc(ins)

                    @block.scalar
                    def _(s):
                        cV.wait(s, "s")
                        load_act(s, ID_NLEXP)
                        ins = s.activation(ln2scr[0:1, 256:512], ln2scr[0:1, 0:256], AF.Ln,
                                           bias=epsv[0:1, :])
                        cS.inc(ins)
                        cS.wait(s, "s")
                        ins = s.activation(ln2scr[0:1, 0:256], ln2scr[0:1, 256:512], AF.Exp,
                                           scale=-0.5)
                        cS.inc(ins)

                    @block.tensor
                    def _(t):
                        cS.wait(t, "t")
                        cV.wait(t, "t")
                        t.matmul(P0[:, 512:768], onesrow[:], rowf[0:1, 0:256], start=True, stop=True)
                        t.transpose(P0[:, 1024:1025], ln2scr[0:1, 0:128], identf32[0:1, 0:1])
                        ins = t.transpose(P0[:, 1025:1026], ln2scr[0:1, 128:256], identf32[0:1, 0:1])
                        cPE.inc(ins)

                    @block.vector
                    def _(v):
                        cPE.wait(v, "v")
                        v.tensor_copy(mu2bc[:], P0[:, 512:768])
                        ins = v.tensor_copy(rstd2c[:], P0[:, 1024:1026])
                        cV.inc(ins)

                    z2_done = [0] * 32
                    mm2_done = [0] * 32
                    for mch in range(32):
                        @block.vector
                        def _(v, mch=mch):
                            cV.wait(v, "v")
                            if mch >= 2:
                                cPE.wait(v, "v", at=mm2_done[mch - 2])
                            ins = v.tensor_tensor(
                                z2buf[:, (mch % 2) * 256:(mch % 2) * 256 + 256],
                                a1g[:, mch * 256:(mch + 1) * 256], mu2bc[:], op=ALU.subtract)
                            z2_done[mch] = cV.inc(ins)

                        @block.tensor
                        def _(t, mch=mch):
                            cV.wait(t, "t", at=z2_done[mch])
                            for tk in range(2):
                                for nn in range(2):
                                    ins = t.matmul(
                                        P1[:, tk * 1024 + nn * 512: tk * 1024 + (nn + 1) * 512],
                                        z2buf[:, (mch % 2) * 256 + tk * 128:
                                              (mch % 2) * 256 + tk * 128 + 128],
                                        W2T[:, mch * 1024 + nn * 512: mch * 1024 + (nn + 1) * 512],
                                        start=(mch == 0), stop=(mch == 31))
                            mm2_done[mch] = cPE.inc(ins)

                    for tk in range(2):
                        @block.vector
                        def _(v, tk=tk):
                            cPE.wait(v, "v")
                            cGD.wait(v, "v")
                            ob = outbuf[:, tk * 1024:(tk + 1) * 1024]
                            ins = v.scalar_tensor_tensor(ob, P1[:, tk * 1024:(tk + 1) * 1024],
                                                         rstd2c[:, tk:tk + 1],
                                                         x1stage[:, tk * 1024:(tk + 1) * 1024],
                                                         op0=ALU.mult, op1=ALU.add)
                            cV.inc(ins)
                            cV.wait(v, "v")
                            ins = v.tensor_tensor(ob, ob, b2bc[:], op=ALU.add)
                            cV.inc(ins)

                        @block.gpsimd
                        def _(g, tk=tk, h2=h2, qb=qb, seq=seq):
                            cV.wait(g, "g")
                            tok0 = qb * QB + h2 * 256 + tk * 128
                            ins = g.dma_start(dy.ap()[seq, tok0:tok0 + 128, :],
                                              outbuf[:, tk * 1024:(tk + 1) * 1024])
                            cGD.inc(ins, 16)

        if DBG_DUMP:
            @block.gpsimd
            def _(g):
                cV.wait(g, "g")
                cS.wait(g, "g")
                cPE.wait(g, "g")
                pass

            @block.vector
            def _(v):
                cV.wait(v, "v")
                cS.wait(v, "v")
                pass

            @block.gpsimd
            def _(g):
                cV.wait(g, "gd")
                cS.wait(g, "gd")
                cPE.wait(g, "gd")
                cGD.inc(g.dma_start(ddump.ap()[:, 0:2048].bitcast(BF16), CKV2[:]), 16)
                cGD.inc(g.dma_start(ddump.ap()[:, 2048:4096].bitcast(BF16)[0:32, :], cqT[:]), 16)
                cGD.inc(g.dma_start(ddump.ap()[:, 4096:4528].bitcast(BF16)[:, 0:528], ckvkm[:, 0:528]), 16)
                cGD.inc(g.dma_start(ddump.ap()[:, 4528:4784].bitcast(BF16)[:, 0:512], Q2[:]), 16)
                cGD.inc(g.dma_start(ddump.ap()[:, 4784:5296].bitcast(BF16)[:, 0:1024], OVT[:]), 16)
                cGD.inc(g.dma_start(ddump.ap()[:, 5296:5552].bitcast(BF16)[:, 0:512], z1T[:, 0:512]), 16)
                cGD.inc(g.dma_start(ddump.ap()[:, 5552:5553], biasC[:],
                                    allow_slow_non_contiguous=True), 16)
                cGD.inc(g.dma_start(ddump.ap()[:, 5553:5554], stat[:, 6:7],
                                    allow_slow_non_contiguous=True), 16)

        @block.gpsimd
        def _(g):
            cGD.wait(g, "gfinal")

    return nc


_NC_CACHE = {}


def kernel(**inputs):
    x = np.asarray(inputs["L0_attn_h_t"], dtype=np.float32)
    f = lambda k: np.ascontiguousarray(np.asarray(inputs[k], dtype=np.float32))
    wcat = np.ascontiguousarray(np.concatenate(
        [f("L0_attn_W_DQ"), f("L0_attn_W_DKV"), f("L0_attn_W_KR")], axis=1))
    bcat = np.ascontiguousarray(np.concatenate(
        [f("L0_attn_b_DQ"), f("L0_attn_b_DKV"), f("L0_attn_b_KR")]).reshape(96, 1))
    common = {
        "wcat": wcat, "bcat": bcat,
        "wuq": f("L0_attn_W_UQ"), "wuk": f("L0_attn_W_UK"), "wuv": f("L0_attn_W_UV"),
        "wqr": f("L0_attn_W_QR"),
        "buq8": f("L0_attn_b_UQ").reshape(8, 128),
        "bqr8": f("L0_attn_b_QR").reshape(8, 32),
        "buv8": f("L0_attn_b_UV").reshape(8, 128),
        "wo": f("L0_attn_W_O"), "bo": f("L0_attn_b_O").reshape(1, 1024),
        "g1_8": f("L0_ffn_ln1_g").reshape(8, 128),
        "b1ln8": f("L0_ffn_ln1_b").reshape(8, 128),
        "b1row": f("L0_ffn_b1").reshape(1, 4096),
        "g2_32": f("L0_ffn_ln2_g").reshape(32, 128),
        "b2ln32": f("L0_ffn_ln2_b").reshape(32, 128),
        "b2row": f("L0_ffn_b2").reshape(1, 1024),
        "w1": f("L0_ffn_w1"), "w2": f("L0_ffn_w2"),
    }
    if "nc" not in _NC_CACHE:
        _NC_CACHE["nc"] = build_program()
    nc = _NC_CACHE["nc"]
    in_maps = []
    for c in range(NCORES):
        m = dict(common)
        m["x"] = np.ascontiguousarray(x[c * BPC:(c + 1) * BPC])
        in_maps.append(m)
    res = run_bass_kernel_spmd(nc, in_maps, core_ids=list(range(NCORES)))
    return np.concatenate([res.results[c]["y"] for c in range(NCORES)], axis=0)


if __name__ == "__main__":
    build_program()
    print("build OK")


# version 02
# speedup vs baseline: 1.0555x; 1.0555x over previous
import contextlib
import numpy as np
import concourse.bass as bass
import concourse.mybir as mybir
from concourse.bass_utils import run_bass_kernel_spmd

F32 = mybir.dt.float32
BF16 = mybir.dt.bfloat16
AF = mybir.ActivationFunctionType
ALU = mybir.AluOpType
AX = mybir.AxisListType

NCORES = 8
BPC = 2
S = 4096
D = 1024
H = 8
HID = 4096
NT = S // 128
QB = 512
NQB = S // QB
NKT = S // 128
SCALE = 1.0 / float(np.sqrt(160.0))
EPS = 1e-5


class SemCtr:
    def __init__(self, nc, es, name):
        self.sem = es.enter_context(nc.semaphore(name))
        self.n = 0
        self.last = {}

    def inc(self, instr, k=1):
        instr.then_inc(self.sem, k)
        self.n += k
        return self.n

    def wait(self, eng, key, at=None):
        v = self.n if at is None else at
        if v <= 0:
            return
        if self.last.get(key, -1) >= v:
            return
        eng.wait_ge(self.sem, v)
        self.last[key] = v


def build_program():
    import os
    DBG_SEQS = int(os.environ.get("KMK_SEQS", BPC))
    DBG_NT = int(os.environ.get("KMK_NT", NT))
    DBG_NQB = int(os.environ.get("KMK_NQB", NQB))
    DBG_PAIRS = int(os.environ.get("KMK_PAIRS", 4))
    DBG_FFN = int(os.environ.get("KMK_FFN", 1))
    DBG_DUMP = int(os.environ.get("KMK_DUMP", 0))
    DBG_PREP = int(os.environ.get("KMK_PREP", 99))
    nc = bass.Bass("TRN2", target_bir_lowering=False)

    dx = nc.dram_tensor("x", [BPC, S, D], F32, kind="ExternalInput")
    dy = nc.dram_tensor("y", [BPC, S, D], F32, kind="ExternalOutput")
    dwcat = nc.dram_tensor("wcat", [D, 96], F32, kind="ExternalInput")
    dbcat = nc.dram_tensor("bcat", [96, 1], F32, kind="ExternalInput")
    dwuq = nc.dram_tensor("wuq", [32, 1024], F32, kind="ExternalInput")
    dwuk = nc.dram_tensor("wuk", [32, 1024], F32, kind="ExternalInput")
    dwuv = nc.dram_tensor("wuv", [32, 1024], F32, kind="ExternalInput")
    dwqr = nc.dram_tensor("wqr", [32, 256], F32, kind="ExternalInput")
    dbuq = nc.dram_tensor("buq8", [8, 128], F32, kind="ExternalInput")
    dbqr = nc.dram_tensor("bqr8", [8, 32], F32, kind="ExternalInput")
    dbuv = nc.dram_tensor("buv8", [8, 128], F32, kind="ExternalInput")
    dwo = nc.dram_tensor("wo", [1024, 1024], F32, kind="ExternalInput")
    dbo = nc.dram_tensor("bo", [1, 1024], F32, kind="ExternalInput")
    dg1 = nc.dram_tensor("g1_8", [8, 128], F32, kind="ExternalInput")
    db1ln = nc.dram_tensor("b1ln8", [8, 128], F32, kind="ExternalInput")
    db1 = nc.dram_tensor("b1row", [1, 4096], F32, kind="ExternalInput")
    dg2 = nc.dram_tensor("g2_32", [32, 128], F32, kind="ExternalInput")
    db2ln = nc.dram_tensor("b2ln32", [32, 128], F32, kind="ExternalInput")
    db2 = nc.dram_tensor("b2row", [1, 1024], F32, kind="ExternalInput")
    dw1 = nc.dram_tensor("w1", [HID, D], F32, kind="ExternalInput")
    dw2 = nc.dram_tensor("w2", [D, HID], F32, kind="ExternalInput")
    if DBG_DUMP:
        ddump = nc.dram_tensor("dump", [128, 8192], F32, kind="ExternalOutput")

    from concourse.hw_specs import get_activation_tables
    act_sets = list(get_activation_tables(nc.m.arch).keys())
    ID_NLEXP = act_sets.index("natural_log_exp_and_others")
    ID_GELU = act_sets.index("gelu_and_others")

    es = contextlib.ExitStack()
    with es:
        E = es.enter_context
        T1T = E(nc.sbuf_tensor("T1T", [128, 8 * 4096], BF16))
        W2T = E(nc.sbuf_tensor("W2T", [128, 32 * 1024], BF16))
        CKV2 = E(nc.sbuf_tensor("CKV2", [128, S], BF16))
        cqT = E(nc.sbuf_tensor("cqT", [32, S], BF16))
        ckvkm = E(nc.sbuf_tensor("ckvkm", [128, 33 * NKT], BF16))
        z1T = E(nc.sbuf_tensor("z1T", [128, 2048], BF16))
        PT0 = E(nc.sbuf_tensor("PT0", [128, 1024], BF16))
        PT1 = E(nc.sbuf_tensor("PT1", [128, 1024], BF16))
        Q2 = E(nc.sbuf_tensor("Q2", [128, QB], BF16))
        OVT = E(nc.sbuf_tensor("OVT", [128, 1024], BF16))
        WVO = E(nc.sbuf_tensor("WVO", [128, 2048], BF16))
        wcat8 = E(nc.sbuf_tensor("wcat8", [128, 8 * 96], BF16))
        wqkr = E(nc.sbuf_tensor("wqkr", [32, 8 * 64], BF16))
        qbias = E(nc.sbuf_tensor("qbias", [128, 4], F32))
        battnbc = E(nc.sbuf_tensor("battnbc", [128, 1024], BF16))
        b2bc = E(nc.sbuf_tensor("b2bc", [128, 1024], BF16))
        beff1T = E(nc.sbuf_tensor("beff1T", [128, 32], F32))
        identbf = E(nc.sbuf_tensor("identbf", [128, 128], BF16))
        identf32 = E(nc.sbuf_tensor("identf32", [32, 32], F32))
        arena = E(nc.sbuf_tensor("arena", [128, 17408], BF16))
        xping = E(nc.sbuf_tensor("xping", [128, 512], F32))
        xpong = E(nc.sbuf_tensor("xpong", [128, 512], F32))
        bconst = E(nc.sbuf_tensor("bconst", [128, 1], F32))
        onesbf = E(nc.sbuf_tensor("onesbf", [128, 1], BF16))
        onescf = E(nc.sbuf_tensor("onescf", [128, 1], F32))
        onesrow = E(nc.sbuf_tensor("onesrow", [1, 128], F32))
        negsc = E(nc.sbuf_tensor("negsc", [1, 128], F32))
        epsv = E(nc.sbuf_tensor("epsv", [128, 1], F32))
        biasC = E(nc.sbuf_tensor("biasC", [128, 2], F32))
        Mk2 = E(nc.sbuf_tensor("Mk2", [1, 1], F32))
        ms = E(nc.sbuf_tensor("ms", [1, 1], F32))
        thrv = E(nc.sbuf_tensor("thrv", [128, 1], F32))
        nthrv = E(nc.sbuf_tensor("nthrv", [128, 1], F32))
        mvec = E(nc.sbuf_tensor("mvec", [128, 1], F32))
        gcols = E(nc.sbuf_tensor("gcols", [128, 32], F32))
        blncols = E(nc.sbuf_tensor("blncols", [128, 32], BF16))
        stat = E(nc.sbuf_tensor("stat", [128, 8], F32))
        rstd2c = E(nc.sbuf_tensor("rstd2c", [128, 2], F32))

        tmp96 = Q2[:, 0:128]
        P0 = E(nc.psum_tensor("P0", [128, 2048], F32))
        P1 = E(nc.psum_tensor("P1", [128, 2048], F32))
        pbf0 = P0.bitcast(BF16)
        pbf1 = P1.bitcast(BF16)

        cPE = SemCtr(nc, es, "cPE")
        cV = SemCtr(nc, es, "cV")
        cS = SemCtr(nc, es, "cS")
        cG = SemCtr(nc, es, "cG")
        cLD = SemCtr(nc, es, "cLD")
        cGD = SemCtr(nc, es, "cGD")

        block = E(nc.Block())

        a1g = arena[:, 0:8192]
        x1stage = arena[:, 8192:12288].bitcast(F32)
        outbuf = arena[:, 12288:16384].bitcast(F32)
        z2buf = arena[:, 16384:16896]
        mu2bc = arena[:, 16896:17408].bitcast(F32)
        wchunk = arena[:, 0:8192].bitcast(F32)
        bocur = arena[:, 0:8192].bitcast(F32)[0:1, 0:4096]
        brow2 = arena[:, 8192:17408].bitcast(F32)[0:1, 4096:4608]
        trow = arena[:, 8192:17408]
        trowf = arena[:, 8192:17408].bitcast(F32)
        Q2sq = z1T[:, 0:512]
        recs = z1T.bitcast(F32)[:, 512:1024]
        rowf = PT1.bitcast(F32)
        ln2scr = PT0.bitcast(F32)

        def load_act(s, set_id):
            s.add_instruction(mybir.InstLoadActFuncSet(
                name=nc.get_next_instruction_name(),
                act_func_set_id=set_id, ins=[], outs=[]))

        @block.gpsimd
        def _(g):
            g.iota(wchunk[:, 0:128], pattern=[[1, 128]], base=0,
                   channel_multiplier=0, allow_small_or_imprecise_dtypes=True)
            ins = g.iota(wchunk[:, 128:129], pattern=[[0, 1]], base=0,
                         channel_multiplier=1, allow_small_or_imprecise_dtypes=True)
            cG.inc(ins)
            cGD.inc(g.dma_start(wcat8[:].rearrange("p (c n) -> p c n", c=8),
                                dwcat.ap().rearrange("(c p) n -> p c n", p=128)), 16)
            cGD.inc(g.dma_start(trow[0:32, 0:1024], dwuq.ap()), 16)
            cGD.inc(g.dma_start(trow[0:32, 1024:2048], dwuk.ap()), 16)
            cGD.inc(g.dma_start(trow[0:32, 2048:3072], dwuv.ap()), 16)
            cGD.inc(g.dma_start(trow[0:32, 3072:3328], dwqr.ap()), 16)
            cGD.inc(g.dma_start(trow[0:8, 3328:3456], dbuq.ap()), 16)
            cGD.inc(g.dma_start(trow[0:8, 3456:3584], dbuv.ap()), 16)

        @block.sync
        def _(sy):
            cLD.inc(sy.dma_start(bconst[0:96, :], dbcat.ap()), 16)
            cLD.inc(sy.dma_start(trowf[0:8, 3208:3240], dbqr.ap()), 16)

        @block.vector
        def _(v):
            cG.wait(v, "v")
            ins = v.tensor_scalar(wchunk[:, 256:384], wchunk[:, 0:128],
                                  wchunk[:, 128:129], None, op0=ALU.is_equal)
            cV.inc(ins)
            cV.wait(v, "v")
            v.tensor_copy(identbf[:], wchunk[:, 256:384])
            v.tensor_copy(identf32[:], wchunk[0:32, 256:288])
            v.memset(onesbf[:], 1.0)
            v.memset(onescf[:], 1.0)
            v.memset(onesrow[:], 1.0)
            v.memset(negsc[:], -SCALE)
            v.memset(epsv[:], EPS)
            ins = v.memset(ckvkm[:, 32:33 * NKT:33], 1.0)
            cV.inc(ins)


        if DBG_PREP <= 1:
            @block.gpsimd
            def _(g):
                cGD.wait(g, "gfinal")
            return nc
        WUQT = trow[:, 3584:3840]
        WUKT = trow[:, 3840:4096]
        WUVT = trow[:, 4096:4352]
        buqc = trow[:, 4352:4360]
        buvc = trow[:, 4360:4368]

        @block.tensor
        def _(t):
            cGD.wait(t, "t")
            for h in range(H):
                cV.wait(t, "t")
                t.transpose(pbf0[:, 0:32], trow[0:32, h * 128:(h + 1) * 128],
                            identbf[0:32, 0:32])
                t.transpose(pbf0[:, 1024:1056], trow[0:32, 1024 + h * 128:1024 + (h + 1) * 128],
                            identbf[0:32, 0:32])
                ins = t.transpose(pbf0[:, 2048:2080], trow[0:32, 2048 + h * 128:2048 + (h + 1) * 128],
                                  identbf[0:32, 0:32])
                cPE.inc(ins)

                @block.vector
                def _(v, h=h):
                    cPE.wait(v, "v")
                    v.tensor_copy(WUQT[:, h * 32:(h + 1) * 32], pbf0[:, 0:32])
                    v.tensor_copy(WUKT[:, h * 32:(h + 1) * 32], pbf0[:, 1024:1056])
                    ins = v.tensor_copy(WUVT[:, h * 32:(h + 1) * 32], pbf0[:, 2048:2080])
                    cV.inc(ins)

        @block.tensor
        def _(t):
            cV.wait(t, "t")
            cLD.wait(t, "t")
            t.transpose(pbf0[:, 0:8], trow[0:8, 3328:3456], identbf[0:8, 0:8])
            ins = t.transpose(pbf0[:, 1024:1032], trow[0:8, 3456:3584], identbf[0:8, 0:8])
            cPE.inc(ins)
            ins = t.transpose(P0[0:32, 1536:1544], trowf[0:8, 3208:3240], identf32[0:8, 0:8])
            cPE.inc(ins)

        @block.vector
        def _(v):
            cPE.wait(v, "v")
            v.tensor_copy(buqc[:], pbf0[:, 0:8])
            ins = v.tensor_copy(buvc[:], pbf0[:, 1024:1032])
            cV.inc(ins)
            for h in range(H):
                r0 = 32 if h % 2 == 0 else 96
                ins = v.tensor_copy(qbias[r0:r0 + 32, h // 2:h // 2 + 1],
                                    P0[0:32, 1536 + h:1537 + h])
            cV.inc(ins)


        if DBG_PREP <= 2:
            @block.gpsimd
            def _(g):
                cGD.wait(g, "gfinal")
            return nc
        @block.tensor
        def _(t):
            for h in range(H):
                cV.wait(t, "t")
                t.matmul(P1[0:32, 0:32], WUQT[:, h * 32:(h + 1) * 32],
                         WUKT[:, h * 32:(h + 1) * 32], start=True, stop=True)
                ins = t.matmul(P1[0:32, 512:513], WUKT[:, h * 32:(h + 1) * 32],
                               buqc[:, h:h + 1], start=True, stop=True)
                cPE.inc(ins)

                @block.vector
                def _(v, h=h):
                    cPE.wait(v, "v")
                    v.tensor_copy(wqkr[:, h * 64:h * 64 + 32], P1[0:32, 0:32])
                    r0 = 0 if h % 2 == 0 else 64
                    v.tensor_copy(qbias[r0:r0 + 32, h // 2:h // 2 + 1], P1[0:32, 512:513])
                    ins = v.tensor_copy(wqkr[:, h * 64 + 32:h * 64 + 64],
                                        trow[0:32, 3072 + h * 32:3072 + (h + 1) * 32])
                    cV.inc(ins)


        if DBG_PREP <= 3:
            @block.gpsimd
            def _(g):
                cGD.wait(g, "gfinal")
            return nc
        @block.sync
        def _(sy):
            cV.wait(sy, "sy")
            ins = sy.dma_start(bocur[0:1, 0:1024], dbo.ap())
            cLD.inc(ins, 16)

        for c in range(8):
            @block.gpsimd
            def _(g, c=c):
                cPE.wait(g, "g")
                ins = g.dma_start(trow[:, 4368 + (c % 2) * 1024: 4368 + (c % 2) * 1024 + 1024],
                                  dwo.ap()[c * 128:(c + 1) * 128, :])
                cGD.inc(ins, 16)

            @block.tensor
            def _(t, c=c):
                cGD.wait(t, "t")
                cV.wait(t, "t")
                wob = trow[:, 4368 + (c % 2) * 1024: 4368 + (c % 2) * 1024 + 1024]
                t.matmul(P1[0:32, 1024:1536], WUVT[:, c * 32:(c + 1) * 32], wob[:, 0:512],
                         start=True, stop=True)
                t.matmul(P1[0:32, 1536:2048], WUVT[:, c * 32:(c + 1) * 32], wob[:, 512:1024],
                         start=True, stop=True)
                t.matmul(P0[0:1, 0:512], buvc[:, c:c + 1], wob[:, 0:512],
                         start=(c == 0), stop=(c == 7))
                ins = t.matmul(P0[0:1, 512:1024], buvc[:, c:c + 1], wob[:, 512:1024],
                               start=(c == 0), stop=(c == 7))
                cPE.inc(ins)

            @block.vector
            def _(v, c=c):
                cPE.wait(v, "v")
                r0 = 32 * (c % 4)
                ins = v.tensor_copy(WVO[r0:r0 + 32, (c // 4) * 1024:(c // 4) * 1024 + 1024],
                                    P1[0:32, 1024:2048])
                cV.inc(ins)

        @block.vector
        def _(v):
            cPE.wait(v, "v")
            cLD.wait(v, "v")
            ins = v.tensor_tensor(bocur[0:1, 0:1024], P0[0:1, 0:1024], bocur[0:1, 0:1024],
                                  op=ALU.add)
            cV.inc(ins)

        @block.tensor
        def _(t):
            cV.wait(t, "t")
            t.matmul(P1[:, 0:512], onesrow[:], bocur[0:1, 0:512], start=True, stop=True)
            ins = t.matmul(P1[:, 512:1024], onesrow[:], bocur[0:1, 512:1024], start=True, stop=True)
            cPE.inc(ins)

        @block.vector
        def _(v):
            cPE.wait(v, "v")
            ins = v.tensor_copy(battnbc[:], P1[:, 0:1024])
            cV.inc(ins)


        if DBG_PREP <= 4:
            @block.gpsimd
            def _(g):
                cGD.wait(g, "gfinal")
            return nc
        def load_gcols(widx):
            nk = 8 if widx == 1 else 32
            src = dg1 if widx == 1 else dg2

            @block.sync
            def _(sy):
                cPE.wait(sy, "sy")
                ins = sy.dma_start(trowf[0:nk, 2048:2176], src.ap())
                cLD.inc(ins, 16)

            @block.tensor
            def _(t):
                cLD.wait(t, "t")
                cV.wait(t, "t")
                ins = t.transpose(P0[:, 1024:1024 + nk], trowf[0:nk, 2048:2176],
                                  identf32[0:nk, 0:nk])
                cPE.inc(ins)

            @block.vector
            def _(v):
                cPE.wait(v, "v")
                ins = v.tensor_copy(gcols[:, 0:nk], P0[:, 1024:1024 + nk])
                cV.inc(ins)

        def ternarize(widx):
            dw = dw1 if widx == 1 else dw2
            nchunks = 32 if widx == 1 else 8
            cols = 1024 if widx == 1 else 4096
            TT = T1T if widx == 1 else W2T
            nblk = cols // 128
            nk = 8 if widx == 1 else 32

            @block.vector
            def _(v):
                ins = v.memset(stat[:, 4:5], 0.0)
                cV.inc(ins)

            for c in range(nchunks):
                @block.sync
                def _(sy, c=c):
                    cV.wait(sy, "sy")
                    ins = sy.dma_start(wchunk[:, 0:cols], dw.ap()[c * 128:(c + 1) * 128, :])
                    cLD.inc(ins, 16)

                @block.vector
                def _(v, c=c):
                    cLD.wait(v, "v")
                    ins = v.reduce_sum(stat[:, 5:6], wchunk[:, 0:cols], axis=AX.X,
                                       apply_absolute_value=True)
                    cV.inc(ins)
                    cV.wait(v, "v")
                    ins = v.tensor_tensor(stat[:, 4:5], stat[:, 4:5], stat[:, 5:6], op=ALU.add)
                    cV.inc(ins)

            @block.tensor
            def _(t):
                cV.wait(t, "t")
                ins = t.matmul(P0[0:1, 0:1], onescf[:], stat[:, 4:5], start=True, stop=True)
                cPE.inc(ins)

            @block.vector
            def _(v):
                cPE.wait(v, "v")
                ins = v.tensor_scalar(ms[:], P0[0:1, 0:1], 1.0 / (4096.0 * 1024.0), 1e-5,
                                      op0=ALU.mult, op1=ALU.max)
                cV.inc(ins)

            @block.tensor
            def _(t):
                cV.wait(t, "t")
                ins = t.matmul(P0[:, 4:5], onesrow[:], ms[:], start=True, stop=True)
                cPE.inc(ins)

            @block.vector
            def _(v):
                cPE.wait(v, "v")
                ins = v.tensor_copy(mvec[:], P0[:, 4:5])
                cV.inc(ins)
                cV.wait(v, "v")
                ins = v.tensor_scalar(thrv[:], mvec[:], 0.5, None, op0=ALU.mult)
                cV.inc(ins)
                cV.wait(v, "v")
                ins = v.tensor_scalar(nthrv[:], thrv[:], -1.0, None, op0=ALU.mult)
                cV.inc(ins)
                gsl = gcols[:, 0:nk]
                ins = v.tensor_scalar(gsl, gsl, mvec[:], None, op0=ALU.mult)
                cV.inc(ins)

            for c in range(nchunks):
                @block.sync
                def _(sy, c=c):
                    cV.wait(sy, "sy")
                    cPE.wait(sy, "sy")
                    ins = sy.dma_start(wchunk[:, 0:cols], dw.ap()[c * 128:(c + 1) * 128, :])
                    cLD.inc(ins, 16)

                @block.vector
                def _(v, c=c):
                    cLD.wait(v, "v")
                    cPE.wait(v, "v")
                    ins = v.tensor_scalar(trowf[:, 0:cols], wchunk[:, 0:cols], thrv[:],
                                          nthrv[:], op0=ALU.min, op1=ALU.max)
                    cV.inc(ins)
                    cV.wait(v, "v")
                    ins = v.tensor_tensor(wchunk[:, 0:cols], wchunk[:, 0:cols],
                                          trowf[:, 0:cols], op=ALU.subtract)
                    cV.inc(ins)
                    cV.wait(v, "v")
                    ins = v.tensor_scalar(wchunk[:, 0:cols], wchunk[:, 0:cols], 1e30, 1.0,
                                          op0=ALU.mult, op1=ALU.min)
                    cV.inc(ins)
                    cV.wait(v, "v")
                    ins = v.tensor_scalar(trow[:, 0:cols], wchunk[:, 0:cols], -1.0, None,
                                          op0=ALU.max)
                    cV.inc(ins)

                @block.tensor
                def _(t, c=c):
                    for k in range(nblk):
                        cV.wait(t, "t")
                        ins = t.transpose(pbf0[:, (k % 2) * 1024:(k % 2) * 1024 + 128],
                                          trow[:, k * 128:(k + 1) * 128], identbf[:])
                        cPE.inc(ins)

                        @block.vector
                        def _(v, c=c, k=k):
                            cPE.wait(v, "v")
                            if widx == 1:
                                dst = TT[:, k * 4096 + c * 128: k * 4096 + (c + 1) * 128]
                            else:
                                dst = TT[:, k * 1024 + c * 128: k * 1024 + (c + 1) * 128]
                            ins = v.tensor_copy(dst, pbf0[:, (k % 2) * 1024:(k % 2) * 1024 + 128])
                            cV.inc(ins)

            @block.gpsimd
            def _(g):
                cV.wait(g, "g")
                src = db1ln if widx == 1 else db2ln
                ins = g.dma_start(trow[0:nk, 0:128], src.ap())
                cGD.inc(ins, 16)

            @block.tensor
            def _(t):
                cGD.wait(t, "t")
                ins = t.transpose(pbf0[:, 0:nk], trow[0:nk, 0:128], identbf[0:nk, 0:nk])
                cPE.inc(ins)

            @block.vector
            def _(v):
                cPE.wait(v, "v")
                ins = v.tensor_copy(blncols[:, 0:nk], pbf0[:, 0:nk])
                cV.inc(ins)

            ncolsO = 4096 if widx == 1 else 1024
            brow = db1 if widx == 1 else db2

            @block.sync
            def _(sy):
                cV.wait(sy, "sy")
                ins = sy.dma_start(bocur[0:1, 0:ncolsO], brow.ap())
                cLD.inc(ins, 16)

            for nb in range(ncolsO // 512):
                @block.tensor
                def _(t, nb=nb):
                    cV.wait(t, "t")
                    cLD.wait(t, "t")
                    for k in range(nk):
                        if widx == 1:
                            rhs = TT[:, k * 4096 + nb * 512: k * 4096 + nb * 512 + 512]
                        else:
                            rhs = TT[:, k * 1024 + (nb % 2) * 512: k * 1024 + (nb % 2) * 512 + 512]
                        ins = t.matmul(P0[0:1, 1024 + (nb % 2) * 512: 1536 + (nb % 2) * 512],
                                       blncols[:, k:k + 1], rhs,
                                       start=(k == 0), stop=(k == nk - 1))
                    cPE.inc(ins)

                @block.vector
                def _(v, nb=nb):
                    cPE.wait(v, "v")
                    ps = P0[0:1, 1024 + (nb % 2) * 512: 1536 + (nb % 2) * 512]
                    ins = v.tensor_scalar(brow2[0:1, 0:512], ps, ms[0:1, :], None, op0=ALU.mult)
                    cV.inc(ins)
                    cV.wait(v, "v")
                    ins = v.tensor_tensor(bocur[0:1, nb * 512:(nb + 1) * 512],
                                          bocur[0:1, nb * 512:(nb + 1) * 512],
                                          brow2[0:1, 0:512], op=ALU.add)
                    cV.inc(ins)

            if widx == 1:
                @block.tensor
                def _(t):
                    cV.wait(t, "t")
                    for nb in range(32):
                        ins = t.transpose(P0[:, nb:nb + 1], bocur[0:1, nb * 128:(nb + 1) * 128],
                                          identf32[0:1, 0:1])
                    cPE.inc(ins)

                @block.vector
                def _(v):
                    cPE.wait(v, "v")
                    ins = v.tensor_copy(beff1T[:], P0[:, 0:32])
                    cV.inc(ins)
            else:
                @block.tensor
                def _(t):
                    cV.wait(t, "t")
                    t.matmul(P0[:, 0:512], onesrow[:], bocur[0:1, 0:512], start=True, stop=True)
                    ins = t.matmul(P0[:, 512:1024], onesrow[:], bocur[0:1, 512:1024],
                                   start=True, stop=True)
                    cPE.inc(ins)

                @block.vector
                def _(v):
                    cPE.wait(v, "v")
                    ins = v.tensor_copy(b2bc[:], P0[:, 0:1024])
                    cV.inc(ins)

            @block.vector
            def _(v):
                cV.wait(v, "v")
                cPE.wait(v, "v")
                for k in range(nk):
                    sl = TT[:, k * 4096:(k + 1) * 4096] if widx == 1 else \
                        TT[:, k * 1024:(k + 1) * 1024]
                    ins = v.tensor_scalar(sl, sl, gcols[:, k:k + 1], None, op0=ALU.mult)
                    cV.inc(ins)
                    cV.wait(v, "v")

        load_gcols(1)
        ternarize(1)

        if DBG_PREP <= 5:
            @block.gpsimd
            def _(g):
                cGD.wait(g, "gfinal")
            return nc
        load_gcols(2)
        ternarize(2)

        if DBG_PREP <= 6:
            @block.gpsimd
            def _(g):
                cGD.wait(g, "gfinal")
            return nc


        @block.scalar
        def _(s):
            load_act(s, ID_NLEXP)

        PB = {
            "gi": 0, "mm": {}, "exp": {}, "o": {},
            "a0r": (None, 0), "a1r": (None, 0), "a2r": (None, 0), "a3r": (None, 0),
            "x1r": (None, 0), "zq": (None, 0),
            "q2sl": {0: (None, 0), 1: (None, 0)},
            "bCch": {0: (None, 0), 1: (None, 0)},
        }

        def ch_wait(eng, key, ch):
            ctr, val = ch
            if ctr is not None and val > 0:
                ctr.wait(eng, key, at=val)
        x1row = x1stage[0:1, 0:1024]

        for seq in range(DBG_SEQS):
            for tt in range(DBG_NT):
                xt = outbuf[:, (tt % 2) * 1024:(tt % 2) * 1024 + 1024]

                @block.sync
                def _(sy, tt=tt, xt=xt, seq=seq):
                    cV.wait(sy, "sy")
                    cS.wait(sy, "sy")
                    cGD.wait(sy, "sy")
                    ins = sy.dma_start(xt, dx.ap()[seq, tt * 128:(tt + 1) * 128, :])
                    cLD.inc(ins, 16)

                @block.vector
                def _(v, xt=xt):
                    cLD.wait(v, "v")
                    ins = v.reduce_sum(stat[:, 0:1], xt, axis=AX.X)
                    cV.inc(ins)

                @block.scalar
                def _(s, xt=xt):
                    cLD.wait(s, "s")
                    ins = s.activation(z1T[:, 0:1024], xt, AF.Square)
                    cS.inc(ins)

                @block.vector
                def _(v):
                    cS.wait(v, "v")
                    ins = v.reduce_sum(stat[:, 1:2], z1T[:, 0:1024], axis=AX.X)
                    cV.inc(ins)
                    cV.wait(v, "v")
                    ins = v.tensor_scalar(stat[:, 2:3], stat[:, 0:1], 1.0 / D, None, op0=ALU.mult)
                    cV.inc(ins)
                    cV.wait(v, "v")
                    ins = v.scalar_tensor_tensor(stat[:, 3:4], stat[:, 2:3], 0.0, stat[:, 2:3],
                                                 op0=ALU.add, op1=ALU.mult)
                    cV.inc(ins)
                    cV.wait(v, "v")
                    ins = v.scalar_tensor_tensor(stat[:, 4:5], stat[:, 1:2], 1.0 / D, stat[:, 3:4],
                                                 op0=ALU.mult, op1=ALU.subtract)
                    cV.inc(ins)

                @block.scalar
                def _(s):
                    cV.wait(s, "s")
                    ins = s.activation(stat[:, 5:6], stat[:, 4:5], AF.Ln, bias=epsv[:])
                    cS.inc(ins)
                    cS.wait(s, "s")
                    ins = s.activation(stat[:, 6:7], stat[:, 5:6], AF.Exp, scale=-0.5)
                    cS.inc(ins)

                @block.vector
                def _(v, xt=xt):
                    cS.wait(v, "v")
                    ins = v.tensor_scalar(z1T[:, 1024:2048], xt, stat[:, 2:3], stat[:, 6:7],
                                          op0=ALU.subtract, op1=ALU.mult)
                    cV.inc(ins)

                for c in range(8):
                    @block.tensor
                    def _(t, c=c):
                        cV.wait(t, "t")
                        ins = t.transpose(pbf0[:, (c % 2) * 1024:(c % 2) * 1024 + 128],
                                          z1T[:, 1024 + c * 128:1024 + (c + 1) * 128], identbf[:])
                        cPE.inc(ins)

                    @block.vector
                    def _(v, c=c):
                        cPE.wait(v, "v")
                        ins = v.tensor_copy(PT0[:, c * 128:(c + 1) * 128],
                                            pbf0[:, (c % 2) * 1024:(c % 2) * 1024 + 128])
                        cV.inc(ins)

                @block.tensor
                def _(t):
                    cV.wait(t, "t")
                    for c in range(8):
                        ins = t.matmul(P0[0:96, 1024:1152], wcat8[:, c * 96:(c + 1) * 96],
                                       PT0[:, c * 128:(c + 1) * 128],
                                       start=(c == 0), stop=(c == 7))
                    cPE.inc(ins)

                @block.scalar
                def _(s):
                    cPE.wait(s, "s")
                    ins = s.activation(tmp96[0:96, :], P0[0:96, 1024:1152], AF.Identity,
                                       bias=bconst[0:96, :])
                    cS.inc(ins)

                @block.tensor
                def _(t):
                    cS.wait(t, "t")
                    ins = t.transpose(pbf0[:, 3072:3104], tmp96[32:64, :], identbf[32:64, 32:64])
                    cPE.inc(ins)

                @block.vector
                def _(v, tt=tt):
                    cS.wait(v, "v")
                    cPE.wait(v, "v")
                    tc = slice(tt * 128, (tt + 1) * 128)
                    v.tensor_copy(cqT[:, tc], tmp96[0:32, :])
                    v.tensor_copy(CKV2[0:32, tc], tmp96[32:64, :])
                    v.tensor_copy(CKV2[32:64, tc], tmp96[64:96, :])
                    v.tensor_copy(CKV2[64:96, tc], tmp96[32:64, :])
                    v.tensor_copy(CKV2[96:128, tc], tmp96[64:96, :])
                    ins = v.tensor_copy(ckvkm[:, tt * 33:tt * 33 + 32], pbf0[:, 3072:3104])
                    cV.inc(ins)

            @block.vector
            def _(v):
                cV.wait(v, "v")
                v.memset(Mk2[:], 0.0)
                ins = v.scalar_tensor_tensor(a1g[0:64, 0:4096], CKV2[0:64, :], 0.0,
                                             CKV2[0:64, :], op0=ALU.add, op1=ALU.mult)
                cV.inc(ins)

            for nb in range(8):
                @block.tensor
                def _(t, nb=nb):
                    cV.wait(t, "t")
                    ins = t.matmul(P0[0:1, 0:512], onesbf[0:64, :],
                                   a1g[0:64, nb * 512:(nb + 1) * 512], start=True, stop=True)
                    cPE.inc(ins)

                @block.vector
                def _(v, nb=nb):
                    cPE.wait(v, "v")
                    ins = v.reduce_max(stat[0:1, 7:8], P0[0:1, 0:512], axis=AX.X)
                    cV.inc(ins)
                    cV.wait(v, "v")
                    ins = v.tensor_tensor(Mk2[:], Mk2[:], stat[0:1, 7:8], op=ALU.max)
                    cV.inc(ins)

            for qb in range(DBG_NQB):
                qsl = slice(qb * QB, (qb + 1) * QB)
                q2e = {}; q2sqd = {}; sqs = {}; rmx = {}; bcc = {}; bcp = {}
                oev = {}; rcp = {}; bct = {}; ovtd = {}; qeffd = {}; olast = {}

                def pro_a(p, qsl=qsl, q2e=q2e, q2sqd=q2sqd, qeffd=qeffd):
                    bnd = (p == 0)

                    @block.tensor
                    def _(t, p=p, bnd=bnd, qsl=qsl):
                        hA, hB = 2 * p, 2 * p + 1
                        if bnd:
                            cV.wait(t, "t")
                            cS.wait(t, "t")
                        else:
                            ch_wait(t, "t", PB["a0r"])
                        t.matmul(P0[0:64, 0:512], wqkr[:, hA * 64:(hA + 1) * 64],
                                 cqT[:, qsl], start=True, stop=True, tile_position=(0, 0))
                        ins = t.matmul(P0[64:128, 0:512], wqkr[:, hB * 64:(hB + 1) * 64],
                                       cqT[:, qsl], start=False, stop=True,
                                       tile_position=(0, 64))
                        qeffd[p] = cPE.inc(ins)

                    @block.vector
                    def _(v, p=p):
                        cPE.wait(v, "v", at=qeffd[p])
                        ch_wait(v, "v", PB["q2sl"][p % 2])
                        ch_wait(v, "v", PB["zq"])
                        qslc = z1T[:, 512:1024] if p % 2 == 0 else Q2[:]
                        ins = v.tensor_scalar(qslc, P0[:, 0:512], qbias[:, p:p + 1],
                                              None, op0=ALU.add)
                        q2e[p] = cV.inc(ins)
                        PB["a0r"] = (cV, q2e[p])
                        ins = v.scalar_tensor_tensor(Q2sq, qslc, 0.0, qslc,
                                                     op0=ALU.add, op1=ALU.mult)
                        q2sqd[p] = cV.inc(ins)

                def pro_b(p, q2sqd=q2sqd, sqs=sqs, rmx=rmx):
                    @block.tensor
                    def _(t, p=p):
                        cV.wait(t, "t", at=q2sqd[p])
                        ch_wait(t, "t", PB["a1r"])
                        ch_wait(t, "t", PB["a3r"])
                        t.matmul(P0[0:1, 512:1024], onesbf[0:64, :], Q2sq[0:64, :],
                                 start=True, stop=True, tile_position=(0, 0))
                        ins = t.matmul(P0[0:1, 1536:2048], onesbf[64:128, :],
                                       Q2sq[64:128, :], start=True, stop=True,
                                       tile_position=(64, 0))
                        sqs[p] = cPE.inc(ins)
                        PB["zq"] = (cPE, sqs[p])

                    @block.vector
                    def _(v, p=p):
                        cPE.wait(v, "v", at=sqs[p])
                        v.reduce_max(stat[0:1, 7:8], P0[0:1, 512:1024], axis=AX.X)
                        ins = v.reduce_max(stat[0:1, 6:7], P0[0:1, 1536:2048], axis=AX.X)
                        cV.inc(ins)
                        ins = v.tensor_tensor(stat[0:1, 5:6], stat[0:1, 7:8],
                                              stat[0:1, 6:7], op=ALU.max)
                        cV.inc(ins)
                        ins = v.tensor_scalar(stat[0:1, 5:6], stat[0:1, 5:6], Mk2[:],
                                              None, op0=ALU.max)
                        rmx[p] = cV.inc(ins)
                        PB["a1r"] = (cV, rmx[p])
                        PB["a3r"] = (cV, rmx[p])

                def pro_c(p, rmx=rmx, bcc=bcc, bcp=bcp):
                    @block.tensor
                    def _(t, p=p):
                        cV.wait(t, "t", at=rmx[p])
                        ins = t.matmul(P0[:, 512:513], negsc[:], stat[0:1, 5:6],
                                       start=True, stop=True)
                        bcc[p] = cPE.inc(ins)

                    @block.vector
                    def _(v, p=p):
                        cPE.wait(v, "v", at=bcc[p])
                        ch_wait(v, "v", PB["bCch"][p % 2])
                        ins = v.tensor_copy(biasC[:, p % 2:p % 2 + 1], P0[:, 512:513])
                        bcp[p] = cV.inc(ins)
                        PB["a1r"] = (cV, bcp[p])

                def tail_v1(p, oev=oev, rcp=rcp, olast=olast):
                    @block.vector
                    def _(v, p=p):
                        cPE.wait(v, "v", at=PB["o"][olast[p]])
                        v.tensor_copy(recs[0:33, :], P0[0:33, 1024:1536])
                        ins = v.tensor_copy(recs[64:97, :], P0[64:97, 1024:1536])
                        oev[p] = cV.inc(ins)
                        PB["a2r"] = (cV, oev[p])
                        ch_wait(v, "v", PB["x1r"])
                        ins = v.reciprocal(x1row[0:1, 0:512], recs[32:33, :])
                        cV.inc(ins)
                        ins = v.reciprocal(x1row[0:1, 512:1024], recs[96:97, :])
                        rcp[p] = cV.inc(ins)

                def tail_pe(p, rcp=rcp, bct=bct, ovtd=ovtd):
                    @block.tensor
                    def _(t, p=p):
                        cV.wait(t, "t", at=rcp[p])
                        ch_wait(t, "t", PB["a3r"])
                        t.matmul(P0[0:32, 1536:2048], onesrow[0:1, 0:32],
                                 x1row[0:1, 0:512], start=True, stop=True,
                                 tile_position=(0, 0))
                        ins = t.matmul(P0[64:96, 1536:2048], onesrow[0:1, 0:32],
                                       x1row[0:1, 512:1024], start=True, stop=True,
                                       tile_position=(0, 64))
                        bct[p] = cPE.inc(ins)
                        PB["x1r"] = (cPE, bct[p])

                    @block.vector
                    def _(v, p=p):
                        hA, hB = 2 * p, 2 * p + 1
                        cPE.wait(v, "v", at=bct[p])
                        ins = v.tensor_tensor(OVT[32 * (hA % 4):32 * (hA % 4) + 32,
                                                  (hA // 4) * 512:(hA // 4) * 512 + 512],
                                              recs[0:32, :], P0[0:32, 1536:2048],
                                              op=ALU.mult)
                        cV.inc(ins)
                        ins = v.tensor_tensor(OVT[32 * (hB % 4):32 * (hB % 4) + 32,
                                                  (hB // 4) * 512:(hB // 4) * 512 + 512],
                                              recs[64:96, :], P0[64:96, 1536:2048],
                                              op=ALU.mult)
                        ovtd[p] = cV.inc(ins)
                        PB["a3r"] = (cV, ovtd[p])

                pro_a(0)
                pro_b(0)
                pro_c(0)
                for p in range(4):
                    gi0 = PB["gi"]

                    def emit_O(kt2, p=p, gi0=gi0, olast=olast):
                        g2 = gi0 + kt2

                        @block.tensor
                        def _(t, kt2=kt2, p=p, g2=g2):
                            cS.wait(t, "t", at=PB["exp"][g2])
                            if kt2 == 0:
                                ch_wait(t, "t", PB["a2r"])
                            PT = PT0 if g2 % 2 == 0 else PT1
                            t.matmul(P0[0:33, 1024:1536],
                                     ckvkm[:, kt2 * 33:kt2 * 33 + 33], PT[:, 0:512],
                                     start=(kt2 == 0), stop=(kt2 == NKT - 1),
                                     tile_position=(0, 0))
                            ins = t.matmul(P0[64:97, 1024:1536],
                                           ckvkm[:, kt2 * 33:kt2 * 33 + 33],
                                           PT[:, 512:1024],
                                           start=(kt2 == 0), stop=(kt2 == NKT - 1),
                                           tile_position=(0, 64))
                            PB["o"][g2] = cPE.inc(ins)
                        if kt2 == NKT - 1:
                            olast[p] = g2

                    for kt in range(NKT):
                        gi = PB["gi"]
                        PB["gi"] = gi + 1

                        @block.tensor
                        def _(t, p=p, kt=kt, gi=gi):
                            if gi - 2 in PB["exp"]:
                                cS.wait(t, "t", at=PB["exp"][gi - 2])
                            if kt == 0:
                                cV.wait(t, "t", at=q2e[p])
                            sl = (gi % 2) * 1024
                            qslc = z1T[:, 512:1024] if p % 2 == 0 else Q2[:]
                            t.matmul(P1[:, sl:sl + 512],
                                     CKV2[0:64, kt * 128:(kt + 1) * 128],
                                     qslc[0:64, :], start=True, stop=True,
                                     tile_position=(0, 0))
                            ins = t.matmul(P1[:, sl + 512:sl + 1024],
                                           CKV2[64:128, kt * 128:(kt + 1) * 128],
                                           qslc[64:128, :], start=True, stop=True,
                                           tile_position=(64, 0))
                            PB["mm"][gi] = cPE.inc(ins)

                        if kt == 1 and p > 0:
                            tail_pe(p - 1)
                        if p < 3:
                            if kt == 4:
                                pro_a(p + 1)
                            elif kt == 6:
                                pro_b(p + 1)
                            elif kt == 8:
                                pro_c(p + 1)

                        @block.scalar
                        def _(s, p=p, kt=kt, gi=gi):
                            cPE.wait(s, "s", at=PB["mm"][gi])
                            if gi - 2 in PB["o"]:
                                cPE.wait(s, "s", at=PB["o"][gi - 2])
                            if kt == 0:
                                cV.wait(s, "s", at=bcp[p])
                            PT = PT0 if gi % 2 == 0 else PT1
                            ins = s.activation(PT[:],
                                               P1[:, (gi % 2) * 1024:(gi % 2) * 1024 + 1024],
                                               AF.Exp, bias=biasC[:, p % 2:p % 2 + 1],
                                               scale=SCALE)
                            PB["exp"][gi] = cS.inc(ins)
                            if kt == NKT - 1:
                                PB["bCch"][p % 2] = (cS, PB["exp"][gi])

                        if kt >= 1:
                            emit_O(kt - 1)
                    emit_O(NKT - 1)
                    PB["q2sl"][p % 2] = (cPE, PB["mm"][gi0 + NKT - 1])
                    tail_v1(p)
                    if p == 3:
                        tail_pe(3)

                for h2 in range(2 if DBG_FFN else 0):
                    for qq in range(2):
                        q4 = h2 * 2 + qq
                        tok0 = qb * QB + q4 * 128
                        psX = P1[:, (qq % 2) * 1024:(qq % 2) * 1024 + 1024]
                        x1sl = x1stage[:, qq * 1024:qq * 1024 + 1024]

                        @block.sync
                        def _(sy, tok0=tok0, seq=seq):
                            cV.wait(sy, "sy")
                            cLD.inc(sy.dma_start(xping[:], dx.ap()[seq, tok0:tok0 + 128, 0:512]), 16)
                            cLD.inc(sy.dma_start(xpong[:], dx.ap()[seq, tok0:tok0 + 128, 512:1024]), 16)

                        @block.tensor
                        def _(t, q4=q4, psX=psX):
                            cV.wait(t, "t")
                            cS.wait(t, "t")
                            for cc in range(2):
                                for nn in range(2):
                                    ins = t.matmul(psX[:, nn * 512:(nn + 1) * 512],
                                                   OVT[:, cc * 512 + q4 * 128: cc * 512 + (q4 + 1) * 128],
                                                   WVO[:, cc * 1024 + nn * 512: cc * 1024 + (nn + 1) * 512],
                                                   start=(cc == 0), stop=(cc == 1))
                            cPE.inc(ins)

                        @block.vector
                        def _(v, psX=psX, x1sl=x1sl):
                            cPE.wait(v, "v")
                            cLD.wait(v, "v")
                            ins = v.tensor_tensor(x1sl, psX, battnbc[:], op=ALU.add)
                            cV.inc(ins)
                            cV.wait(v, "v")
                            v.tensor_tensor(x1sl[:, 0:512], x1sl[:, 0:512], xping[:], op=ALU.add)
                            ins = v.tensor_tensor(x1sl[:, 512:1024], x1sl[:, 512:1024], xpong[:],
                                                  op=ALU.add)
                            cV.inc(ins)
                            cV.wait(v, "v")
                            ins = v.reduce_sum(stat[:, 0:1], x1sl, axis=AX.X)
                            cV.inc(ins)

                        @block.scalar
                        def _(s, x1sl=x1sl):
                            cV.wait(s, "s")
                            ins = s.activation(a1g[:, 0:1024], x1sl, AF.Square)
                            cS.inc(ins)

                        @block.vector
                        def _(v):
                            cS.wait(v, "v")
                            ins = v.reduce_sum(stat[:, 1:2], a1g[:, 0:1024], axis=AX.X)
                            cV.inc(ins)
                            cV.wait(v, "v")
                            ins = v.tensor_scalar(stat[:, 2:3], stat[:, 0:1], 1.0 / D, None,
                                                  op0=ALU.mult)
                            cV.inc(ins)
                            cV.wait(v, "v")
                            ins = v.scalar_tensor_tensor(stat[:, 3:4], stat[:, 2:3], 0.0,
                                                         stat[:, 2:3], op0=ALU.add, op1=ALU.mult)
                            cV.inc(ins)
                            cV.wait(v, "v")
                            ins = v.scalar_tensor_tensor(stat[:, 4:5], stat[:, 1:2], 1.0 / D,
                                                         stat[:, 3:4], op0=ALU.mult, op1=ALU.subtract)
                            cV.inc(ins)

                        @block.scalar
                        def _(s):
                            cV.wait(s, "s")
                            ins = s.activation(stat[:, 5:6], stat[:, 4:5], AF.Ln, bias=epsv[:])
                            cS.inc(ins)
                            cS.wait(s, "s")
                            ins = s.activation(stat[:, 6:7], stat[:, 5:6], AF.Exp, scale=-0.5)
                            cS.inc(ins)

                        @block.vector
                        def _(v, x1sl=x1sl):
                            cS.wait(v, "v")
                            ins = v.tensor_scalar(PT1[:], x1sl, stat[:, 2:3], stat[:, 6:7],
                                                  op0=ALU.subtract, op1=ALU.mult)
                            cV.inc(ins)

                        for c in range(8):
                            @block.tensor
                            def _(t, c=c):
                                cV.wait(t, "t")
                                ins = t.transpose(pbf0[:, (c % 2) * 1024:(c % 2) * 1024 + 128],
                                                  PT1[:, c * 128:(c + 1) * 128], identbf[:])
                                cPE.inc(ins)

                            @block.vector
                            def _(v, c=c, qq=qq):
                                cPE.wait(v, "v")
                                ins = v.tensor_copy(
                                    z1T[:, c * 256 + qq * 128: c * 256 + qq * 128 + 128],
                                    pbf0[:, (c % 2) * 1024:(c % 2) * 1024 + 128])
                                cV.inc(ins)

                    @block.scalar
                    def _(s):
                        cS.wait(s, "s")
                        load_act(s, ID_GELU)

                    gel_done = [0] * 32
                    m1_done = [0] * 32
                    for mch in range(32):
                        @block.tensor
                        def _(t, mch=mch):
                            cV.wait(t, "t")
                            if mch >= 2:
                                cS.wait(t, "t", at=gel_done[mch - 2])
                            psM = P1[:, 1024 + (mch % 2) * 512: 1024 + (mch % 2) * 512 + 256]
                            for k in range(8):
                                ins = t.matmul(psM,
                                               T1T[:, k * 4096 + mch * 128: k * 4096 + (mch + 1) * 128],
                                               z1T[:, k * 256:(k + 1) * 256],
                                               start=(k == 0), stop=(k == 7))
                            m1_done[mch] = cPE.inc(ins)

                        @block.scalar
                        def _(s, mch=mch):
                            cPE.wait(s, "s", at=m1_done[mch])
                            psM = P1[:, 1024 + (mch % 2) * 512: 1024 + (mch % 2) * 512 + 256]
                            ins = s.activation(a1g[:, mch * 256:(mch + 1) * 256], psM, AF.Gelu,
                                               bias=beff1T[:, mch:mch + 1])
                            gel_done[mch] = cS.inc(ins)
                    PB["zq"] = (cPE, m1_done[31])

                    sq_done = [0] * 32
                    for mch in range(32):
                        @block.vector
                        def _(v, mch=mch):
                            cS.wait(v, "v", at=gel_done[mch])
                            if mch >= 2:
                                cPE.wait(v, "v")
                            asl = a1g[:, mch * 256:(mch + 1) * 256]
                            ins = v.scalar_tensor_tensor(
                                z2buf[:, (mch % 2) * 256:(mch % 2) * 256 + 256],
                                asl, 0.0, asl, op0=ALU.add, op1=ALU.mult)
                            sq_done[mch] = cV.inc(ins)

                        @block.tensor
                        def _(t, mch=mch):
                            cV.wait(t, "t", at=sq_done[mch])
                            t.matmul(P0[0:1, 0:256], onesbf[:], a1g[:, mch * 256:(mch + 1) * 256],
                                     start=(mch == 0), stop=(mch == 31))
                            ins = t.matmul(P0[0:1, 256:512], onesbf[:],
                                           z2buf[:, (mch % 2) * 256:(mch % 2) * 256 + 256],
                                           start=False, stop=(mch == 31))
                            cPE.inc(ins)

                    @block.vector
                    def _(v):
                        cPE.wait(v, "v")
                        ins = v.tensor_scalar(rowf[0:1, 0:256], P0[0:1, 0:256], 1.0 / HID, None,
                                              op0=ALU.mult)
                        cV.inc(ins)
                        cV.wait(v, "v")
                        ins = v.scalar_tensor_tensor(rowf[0:1, 256:512], rowf[0:1, 0:256], 0.0,
                                                     rowf[0:1, 0:256], op0=ALU.add, op1=ALU.mult)
                        cV.inc(ins)
                        cV.wait(v, "v")
                        ins = v.scalar_tensor_tensor(ln2scr[0:1, 0:256], P0[0:1, 256:512],
                                                     1.0 / HID, rowf[0:1, 256:512],
                                                     op0=ALU.mult, op1=ALU.subtract)
                        cV.inc(ins)

                    @block.scalar
                    def _(s):
                        cV.wait(s, "s")
                        load_act(s, ID_NLEXP)
                        ins = s.activation(ln2scr[0:1, 256:512], ln2scr[0:1, 0:256], AF.Ln,
                                           bias=epsv[0:1, :])
                        cS.inc(ins)
                        cS.wait(s, "s")
                        ins = s.activation(ln2scr[0:1, 0:256], ln2scr[0:1, 256:512], AF.Exp,
                                           scale=-0.5)
                        cS.inc(ins)

                    @block.tensor
                    def _(t):
                        cS.wait(t, "t")
                        cV.wait(t, "t")
                        t.matmul(P0[:, 512:768], onesrow[:], rowf[0:1, 0:256], start=True, stop=True)
                        t.transpose(P0[:, 1024:1025], ln2scr[0:1, 0:128], identf32[0:1, 0:1])
                        ins = t.transpose(P0[:, 1025:1026], ln2scr[0:1, 128:256], identf32[0:1, 0:1])
                        cPE.inc(ins)

                    @block.vector
                    def _(v):
                        cPE.wait(v, "v")
                        v.tensor_copy(mu2bc[:], P0[:, 512:768])
                        ins = v.tensor_copy(rstd2c[:], P0[:, 1024:1026])
                        cV.inc(ins)

                    z2_done = [0] * 32
                    mm2_done = [0] * 32
                    for mch in range(32):
                        @block.vector
                        def _(v, mch=mch):
                            cV.wait(v, "v")
                            if mch >= 2:
                                cPE.wait(v, "v", at=mm2_done[mch - 2])
                            ins = v.tensor_tensor(
                                z2buf[:, (mch % 2) * 256:(mch % 2) * 256 + 256],
                                a1g[:, mch * 256:(mch + 1) * 256], mu2bc[:], op=ALU.subtract)
                            z2_done[mch] = cV.inc(ins)

                        @block.tensor
                        def _(t, mch=mch):
                            cV.wait(t, "t", at=z2_done[mch])
                            for tk in range(2):
                                for nn in range(2):
                                    ins = t.matmul(
                                        P1[:, tk * 1024 + nn * 512: tk * 1024 + (nn + 1) * 512],
                                        z2buf[:, (mch % 2) * 256 + tk * 128:
                                              (mch % 2) * 256 + tk * 128 + 128],
                                        W2T[:, mch * 1024 + nn * 512: mch * 1024 + (nn + 1) * 512],
                                        start=(mch == 0), stop=(mch == 31))
                            mm2_done[mch] = cPE.inc(ins)

                    for tk in range(2):
                        @block.vector
                        def _(v, tk=tk):
                            cPE.wait(v, "v")
                            cGD.wait(v, "v")
                            ob = outbuf[:, tk * 1024:(tk + 1) * 1024]
                            ins = v.scalar_tensor_tensor(ob, P1[:, tk * 1024:(tk + 1) * 1024],
                                                         rstd2c[:, tk:tk + 1],
                                                         x1stage[:, tk * 1024:(tk + 1) * 1024],
                                                         op0=ALU.mult, op1=ALU.add)
                            cV.inc(ins)
                            cV.wait(v, "v")
                            ins = v.tensor_tensor(ob, ob, b2bc[:], op=ALU.add)
                            cV.inc(ins)

                        @block.gpsimd
                        def _(g, tk=tk, h2=h2, qb=qb, seq=seq):
                            cV.wait(g, "g")
                            tok0 = qb * QB + h2 * 256 + tk * 128
                            ins = g.dma_start(dy.ap()[seq, tok0:tok0 + 128, :],
                                              outbuf[:, tk * 1024:(tk + 1) * 1024])
                            cGD.inc(ins, 16)

        if DBG_DUMP:
            @block.gpsimd
            def _(g):
                cV.wait(g, "g")
                cS.wait(g, "g")
                cPE.wait(g, "g")
                pass

            @block.vector
            def _(v):
                cV.wait(v, "v")
                cS.wait(v, "v")
                pass

            @block.gpsimd
            def _(g):
                cV.wait(g, "gd")
                cS.wait(g, "gd")
                cPE.wait(g, "gd")
                cGD.inc(g.dma_start(ddump.ap()[:, 0:2048].bitcast(BF16), CKV2[:]), 16)
                cGD.inc(g.dma_start(ddump.ap()[:, 2048:4096].bitcast(BF16)[0:32, :], cqT[:]), 16)
                cGD.inc(g.dma_start(ddump.ap()[:, 4096:4528].bitcast(BF16)[:, 0:528], ckvkm[:, 0:528]), 16)
                cGD.inc(g.dma_start(ddump.ap()[:, 4528:4784].bitcast(BF16)[:, 0:512], Q2[:]), 16)
                cGD.inc(g.dma_start(ddump.ap()[:, 4784:5296].bitcast(BF16)[:, 0:1024], OVT[:]), 16)
                cGD.inc(g.dma_start(ddump.ap()[:, 5296:5552].bitcast(BF16)[:, 0:512], z1T[:, 0:512]), 16)
                cGD.inc(g.dma_start(ddump.ap()[:, 5552:5553], biasC[:],
                                    allow_slow_non_contiguous=True), 16)
                cGD.inc(g.dma_start(ddump.ap()[:, 5553:5554], stat[:, 6:7],
                                    allow_slow_non_contiguous=True), 16)

        @block.gpsimd
        def _(g):
            cGD.wait(g, "gfinal")

    return nc


_NC_CACHE = {}


def kernel(**inputs):
    x = np.asarray(inputs["L0_attn_h_t"], dtype=np.float32)
    f = lambda k: np.ascontiguousarray(np.asarray(inputs[k], dtype=np.float32))
    wcat = np.ascontiguousarray(np.concatenate(
        [f("L0_attn_W_DQ"), f("L0_attn_W_DKV"), f("L0_attn_W_KR")], axis=1))
    bcat = np.ascontiguousarray(np.concatenate(
        [f("L0_attn_b_DQ"), f("L0_attn_b_DKV"), f("L0_attn_b_KR")]).reshape(96, 1))
    common = {
        "wcat": wcat, "bcat": bcat,
        "wuq": f("L0_attn_W_UQ"), "wuk": f("L0_attn_W_UK"), "wuv": f("L0_attn_W_UV"),
        "wqr": f("L0_attn_W_QR"),
        "buq8": f("L0_attn_b_UQ").reshape(8, 128),
        "bqr8": f("L0_attn_b_QR").reshape(8, 32),
        "buv8": f("L0_attn_b_UV").reshape(8, 128),
        "wo": f("L0_attn_W_O"), "bo": f("L0_attn_b_O").reshape(1, 1024),
        "g1_8": f("L0_ffn_ln1_g").reshape(8, 128),
        "b1ln8": f("L0_ffn_ln1_b").reshape(8, 128),
        "b1row": f("L0_ffn_b1").reshape(1, 4096),
        "g2_32": f("L0_ffn_ln2_g").reshape(32, 128),
        "b2ln32": f("L0_ffn_ln2_b").reshape(32, 128),
        "b2row": f("L0_ffn_b2").reshape(1, 1024),
        "w1": f("L0_ffn_w1"), "w2": f("L0_ffn_w2"),
    }
    if "nc" not in _NC_CACHE:
        _NC_CACHE["nc"] = build_program()
    nc = _NC_CACHE["nc"]
    in_maps = []
    for c in range(NCORES):
        m = dict(common)
        m["x"] = np.ascontiguousarray(x[c * BPC:(c + 1) * BPC])
        in_maps.append(m)
    res = run_bass_kernel_spmd(nc, in_maps, core_ids=list(range(NCORES)))
    return np.concatenate([res.results[c]["y"] for c in range(NCORES)], axis=0)


if __name__ == "__main__":
    build_program()
    print("build OK")


# version 05
# speedup vs baseline: 1.2279x; 1.1633x over previous
import contextlib
import numpy as np
import concourse.bass as bass
import concourse.mybir as mybir
from concourse.bass_utils import run_bass_kernel_spmd

F32 = mybir.dt.float32
BF16 = mybir.dt.bfloat16
AF = mybir.ActivationFunctionType
ALU = mybir.AluOpType
AX = mybir.AxisListType

NCORES = 8
BPC = 2
S = 4096
D = 1024
H = 8
HID = 4096
NT = S // 128
QB = 512
NQB = S // QB
NKT = S // 128
SCALE = 1.0 / float(np.sqrt(160.0))
EPS = 1e-5


class SemCtr:
    def __init__(self, nc, es, name):
        self.sem = es.enter_context(nc.semaphore(name))
        self.n = 0
        self.last = {}

    def inc(self, instr, k=1):
        instr.then_inc(self.sem, k)
        self.n += k
        return self.n

    def wait(self, eng, key, at=None):
        v = self.n if at is None else at
        if v <= 0:
            return
        if self.last.get(key, -1) >= v:
            return
        eng.wait_ge(self.sem, v)
        self.last[key] = v


def build_program():
    import os
    DBG_SEQS = int(os.environ.get("KMK_SEQS", BPC))
    DBG_NT = int(os.environ.get("KMK_NT", NT))
    DBG_NQB = int(os.environ.get("KMK_NQB", NQB))
    DBG_PAIRS = int(os.environ.get("KMK_PAIRS", 4))
    DBG_FFN = int(os.environ.get("KMK_FFN", 1))
    DBG_DUMP = int(os.environ.get("KMK_DUMP", 0))
    DBG_PREP = int(os.environ.get("KMK_PREP", 99))
    nc = bass.Bass("TRN2", target_bir_lowering=False)

    dx = nc.dram_tensor("x", [BPC, S, D], F32, kind="ExternalInput")
    dy = nc.dram_tensor("y", [BPC, S, D], F32, kind="ExternalOutput")
    dwcat = nc.dram_tensor("wcat", [D, 96], F32, kind="ExternalInput")
    dbcat = nc.dram_tensor("bcat", [96, 1], F32, kind="ExternalInput")
    dwuq = nc.dram_tensor("wuq", [32, 1024], F32, kind="ExternalInput")
    dwuk = nc.dram_tensor("wuk", [32, 1024], F32, kind="ExternalInput")
    dwuv = nc.dram_tensor("wuv", [32, 1024], F32, kind="ExternalInput")
    dwqr = nc.dram_tensor("wqr", [32, 256], F32, kind="ExternalInput")
    dbuq = nc.dram_tensor("buq8", [8, 128], F32, kind="ExternalInput")
    dbqr = nc.dram_tensor("bqr8", [8, 32], F32, kind="ExternalInput")
    dbuv = nc.dram_tensor("buv8", [8, 128], F32, kind="ExternalInput")
    dwo = nc.dram_tensor("wo", [1024, 1024], F32, kind="ExternalInput")
    dbo = nc.dram_tensor("bo", [1, 1024], F32, kind="ExternalInput")
    dg1 = nc.dram_tensor("g1_8", [8, 128], F32, kind="ExternalInput")
    db1ln = nc.dram_tensor("b1ln8", [8, 128], F32, kind="ExternalInput")
    db1 = nc.dram_tensor("b1row", [1, 4096], F32, kind="ExternalInput")
    dg2 = nc.dram_tensor("g2_32", [32, 128], F32, kind="ExternalInput")
    db2ln = nc.dram_tensor("b2ln32", [32, 128], F32, kind="ExternalInput")
    db2 = nc.dram_tensor("b2row", [1, 1024], F32, kind="ExternalInput")
    dw1 = nc.dram_tensor("w1", [HID, D], F32, kind="ExternalInput")
    dw2 = nc.dram_tensor("w2", [D, HID], F32, kind="ExternalInput")
    if DBG_DUMP:
        ddump = nc.dram_tensor("dump", [128, 8192], F32, kind="ExternalOutput")

    from concourse.hw_specs import get_activation_tables
    act_sets = list(get_activation_tables(nc.m.arch).keys())
    ID_NLEXP = act_sets.index("natural_log_exp_and_others")
    ID_GELU = act_sets.index("gelu_and_others")

    es = contextlib.ExitStack()
    with es:
        E = es.enter_context
        T1T = E(nc.sbuf_tensor("T1T", [128, 8 * 4096], BF16))
        W2T = E(nc.sbuf_tensor("W2T", [128, 32 * 1024], BF16))
        CKV2 = E(nc.sbuf_tensor("CKV2", [128, S], BF16))
        cqT = E(nc.sbuf_tensor("cqT", [32, S], BF16))
        ckvkm = E(nc.sbuf_tensor("ckvkm", [128, 33 * NKT], BF16))
        z1T = E(nc.sbuf_tensor("z1T", [128, 2048], BF16))
        PT0 = E(nc.sbuf_tensor("PT0", [128, 1024], BF16))
        PT1 = E(nc.sbuf_tensor("PT1", [128, 1024], BF16))
        Q2 = E(nc.sbuf_tensor("Q2", [128, QB], BF16))
        OVT = E(nc.sbuf_tensor("OVT", [128, 1024], BF16))
        WVO = E(nc.sbuf_tensor("WVO", [128, 2048], BF16))
        wcat8 = E(nc.sbuf_tensor("wcat8", [128, 8 * 96], BF16))
        wqkr = E(nc.sbuf_tensor("wqkr", [32, 8 * 64], BF16))
        qbias = E(nc.sbuf_tensor("qbias", [128, 4], F32))
        battnbc = E(nc.sbuf_tensor("battnbc", [128, 1024], BF16))
        b2bc = E(nc.sbuf_tensor("b2bc", [128, 1024], BF16))
        beff1T = E(nc.sbuf_tensor("beff1T", [128, 32], F32))
        identbf = E(nc.sbuf_tensor("identbf", [128, 128], BF16))
        identf32 = E(nc.sbuf_tensor("identf32", [32, 32], F32))
        arena = E(nc.sbuf_tensor("arena", [128, 17408], BF16))
        bconst = E(nc.sbuf_tensor("bconst", [128, 1], F32))
        onesbf = E(nc.sbuf_tensor("onesbf", [128, 1], BF16))
        onescf = E(nc.sbuf_tensor("onescf", [128, 1], F32))
        onesrow = E(nc.sbuf_tensor("onesrow", [1, 128], F32))
        negsc = E(nc.sbuf_tensor("negsc", [1, 128], F32))
        epsv = E(nc.sbuf_tensor("epsv", [128, 1], F32))
        biasC = E(nc.sbuf_tensor("biasC", [128, 2], F32))
        Mk2 = E(nc.sbuf_tensor("Mk2", [1, 1], F32))
        ms = E(nc.sbuf_tensor("ms", [1, 1], F32))
        thrv = E(nc.sbuf_tensor("thrv", [128, 1], F32))
        nthrv = E(nc.sbuf_tensor("nthrv", [128, 1], F32))
        mvec = E(nc.sbuf_tensor("mvec", [128, 1], F32))
        gcols = E(nc.sbuf_tensor("gcols", [128, 32], F32))
        blncols = E(nc.sbuf_tensor("blncols", [128, 32], BF16))
        stat = E(nc.sbuf_tensor("stat", [128, 16], F32))
        rstd2c = E(nc.sbuf_tensor("rstd2c", [128, 2], F32))

        tmp96 = Q2[:, 0:128]
        P0 = E(nc.psum_tensor("P0", [128, 2048], F32))
        P1 = E(nc.psum_tensor("P1", [128, 2048], F32))
        pbf0 = P0.bitcast(BF16)
        pbf1 = P1.bitcast(BF16)

        cPE = SemCtr(nc, es, "cPE")
        cV = SemCtr(nc, es, "cV")
        cS = SemCtr(nc, es, "cS")
        cG = SemCtr(nc, es, "cG")
        cLD = SemCtr(nc, es, "cLD")
        cGD = SemCtr(nc, es, "cGD")

        block = E(nc.Block())

        a1g = arena[:, 0:8192]
        x1stage = arena[:, 8192:12288].bitcast(F32)
        outbuf = arena[:, 12288:16384].bitcast(F32)
        z2buf = arena[:, 16384:16896]
        mu2bc = arena[:, 16896:17408].bitcast(F32)
        wchunk = arena[:, 0:8192].bitcast(F32)
        bocur = arena[:, 0:8192].bitcast(F32)[0:1, 0:4096]
        brow2 = arena[:, 8192:17408].bitcast(F32)[0:1, 4096:4608]
        trow = arena[:, 8192:17408]
        trowf = arena[:, 8192:17408].bitcast(F32)
        Q2sq = z1T[:, 0:512]
        recs = z1T.bitcast(F32)[:, 512:1024]
        rowf = PT1.bitcast(F32)
        ln2scr = PT0.bitcast(F32)

        def load_act(s, set_id):
            s.add_instruction(mybir.InstLoadActFuncSet(
                name=nc.get_next_instruction_name(),
                act_func_set_id=set_id, ins=[], outs=[]))

        @block.gpsimd
        def _(g):
            g.iota(wchunk[:, 0:128], pattern=[[1, 128]], base=0,
                   channel_multiplier=0, allow_small_or_imprecise_dtypes=True)
            ins = g.iota(wchunk[:, 128:129], pattern=[[0, 1]], base=0,
                         channel_multiplier=1, allow_small_or_imprecise_dtypes=True)
            cG.inc(ins)
            cGD.inc(g.dma_start(wcat8[:].rearrange("p (c n) -> p c n", c=8),
                                dwcat.ap().rearrange("(c p) n -> p c n", p=128)), 16)
            cGD.inc(g.dma_start(trow[0:32, 0:1024], dwuq.ap()), 16)
            cGD.inc(g.dma_start(trow[0:32, 1024:2048], dwuk.ap()), 16)
            cGD.inc(g.dma_start(trow[0:32, 2048:3072], dwuv.ap()), 16)
            cGD.inc(g.dma_start(trow[0:32, 3072:3328], dwqr.ap()), 16)
            cGD.inc(g.dma_start(trow[0:8, 3328:3456], dbuq.ap()), 16)
            cGD.inc(g.dma_start(trow[0:8, 3456:3584], dbuv.ap()), 16)

        @block.sync
        def _(sy):
            cLD.inc(sy.dma_start(bconst[0:96, :], dbcat.ap()), 16)
            cLD.inc(sy.dma_start(trowf[0:8, 3208:3240], dbqr.ap()), 16)

        @block.vector
        def _(v):
            cG.wait(v, "v")
            ins = v.tensor_scalar(wchunk[:, 256:384], wchunk[:, 0:128],
                                  wchunk[:, 128:129], None, op0=ALU.is_equal)
            cV.inc(ins)
            cV.wait(v, "v")
            v.tensor_copy(identbf[:], wchunk[:, 256:384])
            v.tensor_copy(identf32[:], wchunk[0:32, 256:288])
            v.memset(onesbf[:], 1.0)
            v.memset(onescf[:], 1.0)
            v.memset(onesrow[:], 1.0)
            v.memset(negsc[:], -SCALE)
            v.memset(epsv[:], EPS)
            ins = v.memset(ckvkm[:, 32:33 * NKT:33], 1.0)
            cV.inc(ins)


        if DBG_PREP <= 1:
            @block.gpsimd
            def _(g):
                cGD.wait(g, "gfinal")
            return nc
        WUQT = trow[:, 3584:3840]
        WUKT = trow[:, 3840:4096]
        WUVT = trow[:, 4096:4352]
        buqc = trow[:, 4352:4360]
        buvc = trow[:, 4360:4368]

        @block.tensor
        def _(t):
            cGD.wait(t, "t")
            for h in range(H):
                cV.wait(t, "t")
                t.transpose(pbf0[:, 0:32], trow[0:32, h * 128:(h + 1) * 128],
                            identbf[0:32, 0:32])
                t.transpose(pbf0[:, 1024:1056], trow[0:32, 1024 + h * 128:1024 + (h + 1) * 128],
                            identbf[0:32, 0:32])
                ins = t.transpose(pbf0[:, 2048:2080], trow[0:32, 2048 + h * 128:2048 + (h + 1) * 128],
                                  identbf[0:32, 0:32])
                cPE.inc(ins)

                @block.vector
                def _(v, h=h):
                    cPE.wait(v, "v")
                    v.tensor_copy(WUQT[:, h * 32:(h + 1) * 32], pbf0[:, 0:32])
                    v.tensor_copy(WUKT[:, h * 32:(h + 1) * 32], pbf0[:, 1024:1056])
                    ins = v.tensor_copy(WUVT[:, h * 32:(h + 1) * 32], pbf0[:, 2048:2080])
                    cV.inc(ins)

        @block.tensor
        def _(t):
            cV.wait(t, "t")
            cLD.wait(t, "t")
            t.transpose(pbf0[:, 0:8], trow[0:8, 3328:3456], identbf[0:8, 0:8])
            ins = t.transpose(pbf0[:, 1024:1032], trow[0:8, 3456:3584], identbf[0:8, 0:8])
            cPE.inc(ins)
            ins = t.transpose(P0[0:32, 1536:1544], trowf[0:8, 3208:3240], identf32[0:8, 0:8])
            cPE.inc(ins)

        @block.vector
        def _(v):
            cPE.wait(v, "v")
            v.tensor_copy(buqc[:], pbf0[:, 0:8])
            ins = v.tensor_copy(buvc[:], pbf0[:, 1024:1032])
            cV.inc(ins)
            for h in range(H):
                r0 = 32 if h % 2 == 0 else 96
                ins = v.tensor_copy(qbias[r0:r0 + 32, h // 2:h // 2 + 1],
                                    P0[0:32, 1536 + h:1537 + h])
            cV.inc(ins)


        if DBG_PREP <= 2:
            @block.gpsimd
            def _(g):
                cGD.wait(g, "gfinal")
            return nc
        @block.tensor
        def _(t):
            for h in range(H):
                cV.wait(t, "t")
                t.matmul(P1[0:32, 0:32], WUQT[:, h * 32:(h + 1) * 32],
                         WUKT[:, h * 32:(h + 1) * 32], start=True, stop=True)
                ins = t.matmul(P1[0:32, 512:513], WUKT[:, h * 32:(h + 1) * 32],
                               buqc[:, h:h + 1], start=True, stop=True)
                cPE.inc(ins)

                @block.vector
                def _(v, h=h):
                    cPE.wait(v, "v")
                    v.tensor_copy(wqkr[:, h * 64:h * 64 + 32], P1[0:32, 0:32])
                    r0 = 0 if h % 2 == 0 else 64
                    v.tensor_copy(qbias[r0:r0 + 32, h // 2:h // 2 + 1], P1[0:32, 512:513])
                    ins = v.tensor_copy(wqkr[:, h * 64 + 32:h * 64 + 64],
                                        trow[0:32, 3072 + h * 32:3072 + (h + 1) * 32])
                    cV.inc(ins)


        if DBG_PREP <= 3:
            @block.gpsimd
            def _(g):
                cGD.wait(g, "gfinal")
            return nc
        @block.sync
        def _(sy):
            cV.wait(sy, "sy")
            ins = sy.dma_start(bocur[0:1, 0:1024], dbo.ap())
            cLD.inc(ins, 16)

        for c in range(8):
            @block.gpsimd
            def _(g, c=c):
                cPE.wait(g, "g")
                ins = g.dma_start(trow[:, 4368 + (c % 2) * 1024: 4368 + (c % 2) * 1024 + 1024],
                                  dwo.ap()[c * 128:(c + 1) * 128, :])
                cGD.inc(ins, 16)

            @block.tensor
            def _(t, c=c):
                cGD.wait(t, "t")
                cV.wait(t, "t")
                wob = trow[:, 4368 + (c % 2) * 1024: 4368 + (c % 2) * 1024 + 1024]
                t.matmul(P1[0:32, 1024:1536], WUVT[:, c * 32:(c + 1) * 32], wob[:, 0:512],
                         start=True, stop=True)
                t.matmul(P1[0:32, 1536:2048], WUVT[:, c * 32:(c + 1) * 32], wob[:, 512:1024],
                         start=True, stop=True)
                t.matmul(P0[0:1, 0:512], buvc[:, c:c + 1], wob[:, 0:512],
                         start=(c == 0), stop=(c == 7))
                ins = t.matmul(P0[0:1, 512:1024], buvc[:, c:c + 1], wob[:, 512:1024],
                               start=(c == 0), stop=(c == 7))
                cPE.inc(ins)

            @block.vector
            def _(v, c=c):
                cPE.wait(v, "v")
                r0 = 32 * (c % 4)
                ins = v.tensor_copy(WVO[r0:r0 + 32, (c // 4) * 1024:(c // 4) * 1024 + 1024],
                                    P1[0:32, 1024:2048])
                cV.inc(ins)

        @block.vector
        def _(v):
            cPE.wait(v, "v")
            cLD.wait(v, "v")
            ins = v.tensor_tensor(bocur[0:1, 0:1024], P0[0:1, 0:1024], bocur[0:1, 0:1024],
                                  op=ALU.add)
            cV.inc(ins)

        @block.tensor
        def _(t):
            cV.wait(t, "t")
            t.matmul(P1[:, 0:512], onesrow[:], bocur[0:1, 0:512], start=True, stop=True)
            ins = t.matmul(P1[:, 512:1024], onesrow[:], bocur[0:1, 512:1024], start=True, stop=True)
            cPE.inc(ins)

        @block.vector
        def _(v):
            cPE.wait(v, "v")
            ins = v.tensor_copy(battnbc[:], P1[:, 0:1024])
            cV.inc(ins)


        if DBG_PREP <= 4:
            @block.gpsimd
            def _(g):
                cGD.wait(g, "gfinal")
            return nc
        def load_gcols(widx):
            nk = 8 if widx == 1 else 32
            src = dg1 if widx == 1 else dg2

            @block.sync
            def _(sy):
                cPE.wait(sy, "sy")
                ins = sy.dma_start(trowf[0:nk, 2048:2176], src.ap())
                cLD.inc(ins, 16)

            @block.tensor
            def _(t):
                cLD.wait(t, "t")
                cV.wait(t, "t")
                ins = t.transpose(P0[:, 1024:1024 + nk], trowf[0:nk, 2048:2176],
                                  identf32[0:nk, 0:nk])
                cPE.inc(ins)

            @block.vector
            def _(v):
                cPE.wait(v, "v")
                ins = v.tensor_copy(gcols[:, 0:nk], P0[:, 1024:1024 + nk])
                cV.inc(ins)

        def ternarize(widx):
            dw = dw1 if widx == 1 else dw2
            nchunks = 32 if widx == 1 else 8
            cols = 1024 if widx == 1 else 4096
            TT = T1T if widx == 1 else W2T
            nblk = cols // 128
            nk = 8 if widx == 1 else 32

            @block.vector
            def _(v):
                ins = v.memset(stat[:, 4:5], 0.0)
                cV.inc(ins)

            for c in range(nchunks):
                @block.sync
                def _(sy, c=c):
                    cV.wait(sy, "sy")
                    ins = sy.dma_start(wchunk[:, 0:cols], dw.ap()[c * 128:(c + 1) * 128, :])
                    cLD.inc(ins, 16)

                @block.vector
                def _(v, c=c):
                    cLD.wait(v, "v")
                    ins = v.reduce_sum(stat[:, 5:6], wchunk[:, 0:cols], axis=AX.X,
                                       apply_absolute_value=True)
                    cV.inc(ins)
                    cV.wait(v, "v")
                    ins = v.tensor_tensor(stat[:, 4:5], stat[:, 4:5], stat[:, 5:6], op=ALU.add)
                    cV.inc(ins)

            @block.tensor
            def _(t):
                cV.wait(t, "t")
                ins = t.matmul(P0[0:1, 0:1], onescf[:], stat[:, 4:5], start=True, stop=True)
                cPE.inc(ins)

            @block.vector
            def _(v):
                cPE.wait(v, "v")
                ins = v.tensor_scalar(ms[:], P0[0:1, 0:1], 1.0 / (4096.0 * 1024.0), 1e-5,
                                      op0=ALU.mult, op1=ALU.max)
                cV.inc(ins)

            @block.tensor
            def _(t):
                cV.wait(t, "t")
                ins = t.matmul(P0[:, 4:5], onesrow[:], ms[:], start=True, stop=True)
                cPE.inc(ins)

            @block.vector
            def _(v):
                cPE.wait(v, "v")
                ins = v.tensor_copy(mvec[:], P0[:, 4:5])
                cV.inc(ins)
                cV.wait(v, "v")
                ins = v.tensor_scalar(thrv[:], mvec[:], 0.5, None, op0=ALU.mult)
                cV.inc(ins)
                cV.wait(v, "v")
                ins = v.tensor_scalar(nthrv[:], thrv[:], -1.0, None, op0=ALU.mult)
                cV.inc(ins)
                gsl = gcols[:, 0:nk]
                ins = v.tensor_scalar(gsl, gsl, mvec[:], None, op0=ALU.mult)
                cV.inc(ins)

            for c in range(nchunks):
                @block.sync
                def _(sy, c=c):
                    cV.wait(sy, "sy")
                    cPE.wait(sy, "sy")
                    ins = sy.dma_start(wchunk[:, 0:cols], dw.ap()[c * 128:(c + 1) * 128, :])
                    cLD.inc(ins, 16)

                @block.vector
                def _(v, c=c):
                    cLD.wait(v, "v")
                    cPE.wait(v, "v")
                    ins = v.tensor_scalar(trowf[:, 0:cols], wchunk[:, 0:cols], thrv[:],
                                          nthrv[:], op0=ALU.min, op1=ALU.max)
                    cV.inc(ins)
                    cV.wait(v, "v")
                    ins = v.tensor_tensor(wchunk[:, 0:cols], wchunk[:, 0:cols],
                                          trowf[:, 0:cols], op=ALU.subtract)
                    cV.inc(ins)
                    cV.wait(v, "v")
                    ins = v.tensor_scalar(wchunk[:, 0:cols], wchunk[:, 0:cols], 1e30, 1.0,
                                          op0=ALU.mult, op1=ALU.min)
                    cV.inc(ins)
                    cV.wait(v, "v")
                    ins = v.tensor_scalar(trow[:, 0:cols], wchunk[:, 0:cols], -1.0, None,
                                          op0=ALU.max)
                    cV.inc(ins)

                @block.tensor
                def _(t, c=c):
                    for k in range(nblk):
                        cV.wait(t, "t")
                        ins = t.transpose(pbf0[:, (k % 2) * 1024:(k % 2) * 1024 + 128],
                                          trow[:, k * 128:(k + 1) * 128], identbf[:])
                        cPE.inc(ins)

                        @block.vector
                        def _(v, c=c, k=k):
                            cPE.wait(v, "v")
                            if widx == 1:
                                dst = TT[:, k * 4096 + c * 128: k * 4096 + (c + 1) * 128]
                            else:
                                dst = TT[:, k * 1024 + c * 128: k * 1024 + (c + 1) * 128]
                            ins = v.tensor_copy(dst, pbf0[:, (k % 2) * 1024:(k % 2) * 1024 + 128])
                            cV.inc(ins)

            @block.gpsimd
            def _(g):
                cV.wait(g, "g")
                src = db1ln if widx == 1 else db2ln
                ins = g.dma_start(trow[0:nk, 0:128], src.ap())
                cGD.inc(ins, 16)

            @block.tensor
            def _(t):
                cGD.wait(t, "t")
                ins = t.transpose(pbf0[:, 0:nk], trow[0:nk, 0:128], identbf[0:nk, 0:nk])
                cPE.inc(ins)

            @block.vector
            def _(v):
                cPE.wait(v, "v")
                ins = v.tensor_copy(blncols[:, 0:nk], pbf0[:, 0:nk])
                cV.inc(ins)

            ncolsO = 4096 if widx == 1 else 1024
            brow = db1 if widx == 1 else db2

            @block.sync
            def _(sy):
                cV.wait(sy, "sy")
                ins = sy.dma_start(bocur[0:1, 0:ncolsO], brow.ap())
                cLD.inc(ins, 16)

            for nb in range(ncolsO // 512):
                @block.tensor
                def _(t, nb=nb):
                    cV.wait(t, "t")
                    cLD.wait(t, "t")
                    for k in range(nk):
                        if widx == 1:
                            rhs = TT[:, k * 4096 + nb * 512: k * 4096 + nb * 512 + 512]
                        else:
                            rhs = TT[:, k * 1024 + (nb % 2) * 512: k * 1024 + (nb % 2) * 512 + 512]
                        ins = t.matmul(P0[0:1, 1024 + (nb % 2) * 512: 1536 + (nb % 2) * 512],
                                       blncols[:, k:k + 1], rhs,
                                       start=(k == 0), stop=(k == nk - 1))
                    cPE.inc(ins)

                @block.vector
                def _(v, nb=nb):
                    cPE.wait(v, "v")
                    ps = P0[0:1, 1024 + (nb % 2) * 512: 1536 + (nb % 2) * 512]
                    ins = v.tensor_scalar(brow2[0:1, 0:512], ps, ms[0:1, :], None, op0=ALU.mult)
                    cV.inc(ins)
                    cV.wait(v, "v")
                    ins = v.tensor_tensor(bocur[0:1, nb * 512:(nb + 1) * 512],
                                          bocur[0:1, nb * 512:(nb + 1) * 512],
                                          brow2[0:1, 0:512], op=ALU.add)
                    cV.inc(ins)

            if widx == 1:
                @block.tensor
                def _(t):
                    cV.wait(t, "t")
                    for nb in range(32):
                        ins = t.transpose(P0[:, nb:nb + 1], bocur[0:1, nb * 128:(nb + 1) * 128],
                                          identf32[0:1, 0:1])
                    cPE.inc(ins)

                @block.vector
                def _(v):
                    cPE.wait(v, "v")
                    ins = v.tensor_copy(beff1T[:], P0[:, 0:32])
                    cV.inc(ins)
            else:
                @block.tensor
                def _(t):
                    cV.wait(t, "t")
                    t.matmul(P0[:, 0:512], onesrow[:], bocur[0:1, 0:512], start=True, stop=True)
                    ins = t.matmul(P0[:, 512:1024], onesrow[:], bocur[0:1, 512:1024],
                                   start=True, stop=True)
                    cPE.inc(ins)

                @block.vector
                def _(v):
                    cPE.wait(v, "v")
                    ins = v.tensor_copy(b2bc[:], P0[:, 0:1024])
                    cV.inc(ins)

            @block.vector
            def _(v):
                cV.wait(v, "v")
                cPE.wait(v, "v")
                for k in range(nk):
                    sl = TT[:, k * 4096:(k + 1) * 4096] if widx == 1 else \
                        TT[:, k * 1024:(k + 1) * 1024]
                    ins = v.tensor_scalar(sl, sl, gcols[:, k:k + 1], None, op0=ALU.mult)
                    cV.inc(ins)
                    cV.wait(v, "v")

        load_gcols(1)
        ternarize(1)

        if DBG_PREP <= 5:
            @block.gpsimd
            def _(g):
                cGD.wait(g, "gfinal")
            return nc
        load_gcols(2)
        ternarize(2)

        if DBG_PREP <= 6:
            @block.gpsimd
            def _(g):
                cGD.wait(g, "gfinal")
            return nc


        @block.scalar
        def _(s):
            load_act(s, ID_NLEXP)

        PB = {
            "gi": 0, "mm": {}, "exp": {}, "o": {},
            "a0r": (None, 0), "a1r": (None, 0), "a2r": (None, 0), "a3r": (None, 0),
            "x1r": (None, 0), "zq": (None, 0),
            "q2sl": {0: (None, 0), 1: (None, 0)},
            "bCch": {0: (None, 0), 1: (None, 0)},
            "a2rb": (None, 0), "a3rb": (None, 0), "pt1r": (None, 0),
            "psm": {0: (None, 0), 1: (None, 0)},
            "psxr": {0: (None, 0), 1: (None, 0)},
            "xfA": {0: 0, 1: 0}, "xfB": {0: 0, 1: 0},
            "xldA": {0: 0, 1: 0}, "xldB": {0: 0, 1: 0},
            "x1dma": (None, 0), "a0ev": (None, 0), "evtk0": (None, 0),
            "z2br": (None, 0), "trslot": {0: 0, 1: 0},
        }

        def ch_wait(eng, key, ch):
            ctr, val = ch
            if ctr is not None and val > 0:
                ctr.wait(eng, key, at=val)
        x1row = outbuf[0:1, 0:1024]
        TR_E = pbf0[:, 2560:2688]
        TR_O = pbf0[:, 3584:3712]

        for seq in range(DBG_SEQS):
            for tt in range(DBG_NT):
                xt = outbuf[:, (tt % 2) * 1024:(tt % 2) * 1024 + 1024]

                @block.sync
                def _(sy, tt=tt, xt=xt, seq=seq):
                    cV.wait(sy, "sy")
                    cS.wait(sy, "sy")
                    cGD.wait(sy, "sy")
                    ins = sy.dma_start(xt, dx.ap()[seq, tt * 128:(tt + 1) * 128, :])
                    cLD.inc(ins, 16)

                @block.vector
                def _(v, xt=xt):
                    cLD.wait(v, "v")
                    ins = v.reduce_sum(stat[:, 0:1], xt, axis=AX.X)
                    cV.inc(ins)

                @block.scalar
                def _(s, xt=xt):
                    cLD.wait(s, "s")
                    ins = s.activation(z1T[:, 0:1024], xt, AF.Square)
                    cS.inc(ins)

                @block.vector
                def _(v):
                    cS.wait(v, "v")
                    ins = v.reduce_sum(stat[:, 1:2], z1T[:, 0:1024], axis=AX.X)
                    cV.inc(ins)
                    cV.wait(v, "v")
                    ins = v.tensor_scalar(stat[:, 2:3], stat[:, 0:1], 1.0 / D, None, op0=ALU.mult)
                    cV.inc(ins)
                    cV.wait(v, "v")
                    ins = v.scalar_tensor_tensor(stat[:, 3:4], stat[:, 2:3], 0.0, stat[:, 2:3],
                                                 op0=ALU.add, op1=ALU.mult)
                    cV.inc(ins)
                    cV.wait(v, "v")
                    ins = v.scalar_tensor_tensor(stat[:, 4:5], stat[:, 1:2], 1.0 / D, stat[:, 3:4],
                                                 op0=ALU.mult, op1=ALU.subtract)
                    cV.inc(ins)

                @block.scalar
                def _(s):
                    cV.wait(s, "s")
                    ins = s.activation(stat[:, 5:6], stat[:, 4:5], AF.Ln, bias=epsv[:])
                    cS.inc(ins)
                    cS.wait(s, "s")
                    ins = s.activation(stat[:, 6:7], stat[:, 5:6], AF.Exp, scale=-0.5)
                    cS.inc(ins)

                @block.vector
                def _(v, xt=xt):
                    cS.wait(v, "v")
                    ins = v.tensor_scalar(z1T[:, 1024:2048], xt, stat[:, 2:3], stat[:, 6:7],
                                          op0=ALU.subtract, op1=ALU.mult)
                    cV.inc(ins)

                for c in range(8):
                    @block.tensor
                    def _(t, c=c):
                        cV.wait(t, "t")
                        ins = t.transpose(pbf0[:, (c % 2) * 1024:(c % 2) * 1024 + 128],
                                          z1T[:, 1024 + c * 128:1024 + (c + 1) * 128], identbf[:])
                        cPE.inc(ins)

                    @block.vector
                    def _(v, c=c):
                        cPE.wait(v, "v")
                        ins = v.tensor_copy(PT0[:, c * 128:(c + 1) * 128],
                                            pbf0[:, (c % 2) * 1024:(c % 2) * 1024 + 128])
                        cV.inc(ins)

                @block.tensor
                def _(t):
                    cV.wait(t, "t")
                    for c in range(8):
                        ins = t.matmul(P0[0:96, 1024:1152], wcat8[:, c * 96:(c + 1) * 96],
                                       PT0[:, c * 128:(c + 1) * 128],
                                       start=(c == 0), stop=(c == 7))
                    cPE.inc(ins)

                @block.scalar
                def _(s):
                    cPE.wait(s, "s")
                    ins = s.activation(tmp96[0:96, :], P0[0:96, 1024:1152], AF.Identity,
                                       bias=bconst[0:96, :])
                    cS.inc(ins)

                @block.tensor
                def _(t):
                    cS.wait(t, "t")
                    ins = t.transpose(pbf0[:, 3072:3104], tmp96[32:64, :], identbf[32:64, 32:64])
                    cPE.inc(ins)

                @block.vector
                def _(v, tt=tt):
                    cS.wait(v, "v")
                    cPE.wait(v, "v")
                    tc = slice(tt * 128, (tt + 1) * 128)
                    v.tensor_copy(cqT[:, tc], tmp96[0:32, :])
                    v.tensor_copy(CKV2[0:32, tc], tmp96[32:64, :])
                    v.tensor_copy(CKV2[32:64, tc], tmp96[64:96, :])
                    v.tensor_copy(CKV2[64:96, tc], tmp96[32:64, :])
                    v.tensor_copy(CKV2[96:128, tc], tmp96[64:96, :])
                    ins = v.tensor_copy(ckvkm[:, tt * 33:tt * 33 + 32], pbf0[:, 3072:3104])
                    cV.inc(ins)

            @block.vector
            def _(v):
                cV.wait(v, "v")
                v.memset(Mk2[:], 0.0)
                ins = v.scalar_tensor_tensor(a1g[0:64, 0:4096], CKV2[0:64, :], 0.0,
                                             CKV2[0:64, :], op0=ALU.add, op1=ALU.mult)
                cV.inc(ins)

            for nb in range(8):
                @block.tensor
                def _(t, nb=nb):
                    cV.wait(t, "t")
                    ins = t.matmul(P0[0:1, 0:512], onesbf[0:64, :],
                                   a1g[0:64, nb * 512:(nb + 1) * 512], start=True, stop=True)
                    cPE.inc(ins)

                @block.vector
                def _(v, nb=nb):
                    cPE.wait(v, "v")
                    ins = v.reduce_max(stat[0:1, 7:8], P0[0:1, 0:512], axis=AX.X)
                    cV.inc(ins)
                    cV.wait(v, "v")
                    ins = v.tensor_tensor(Mk2[:], Mk2[:], stat[0:1, 7:8], op=ALU.max)
                    cV.inc(ins)

            for qb in range(DBG_NQB):
                qsl = slice(qb * QB, (qb + 1) * QB)
                q2e = {}; q2sqd = {}; sqs = {}; rmx = {}; bcc = {}; bcp = {}
                oev = {}; rcp = {}; bct = {}; ovtd = {}; qeffd = {}; olast = {}

                def pro_a(p, qsl=qsl, q2e=q2e, q2sqd=q2sqd, qeffd=qeffd):
                    bnd = (p == 0)

                    @block.tensor
                    def _(t, p=p, bnd=bnd, qsl=qsl):
                        hA, hB = 2 * p, 2 * p + 1
                        if bnd:
                            cV.wait(t, "t")
                            cS.wait(t, "t")
                        else:
                            ch_wait(t, "t", PB["a0r"])
                        t.matmul(P0[0:64, 0:512], wqkr[:, hA * 64:(hA + 1) * 64],
                                 cqT[:, qsl], start=True, stop=True, tile_position=(0, 0))
                        ins = t.matmul(P0[64:128, 0:512], wqkr[:, hB * 64:(hB + 1) * 64],
                                       cqT[:, qsl], start=False, stop=True,
                                       tile_position=(0, 64))
                        qeffd[p] = cPE.inc(ins)

                    @block.vector
                    def _(v, p=p):
                        cPE.wait(v, "v", at=qeffd[p])
                        ch_wait(v, "v", PB["q2sl"][p % 2])
                        ch_wait(v, "v", PB["zq"])
                        qslc = z1T[:, 512:1024] if p % 2 == 0 else Q2[:]
                        ins = v.tensor_scalar(qslc, P0[:, 0:512], qbias[:, p:p + 1],
                                              None, op0=ALU.add)
                        q2e[p] = cV.inc(ins)
                        PB["a0r"] = (cV, q2e[p])
                        cV.wait(v, "v", at=q2e[p])
                        ins = v.scalar_tensor_tensor(Q2sq, qslc, 0.0, qslc,
                                                     op0=ALU.add, op1=ALU.mult)
                        q2sqd[p] = cV.inc(ins)

                def pro_b(p, q2sqd=q2sqd, sqs=sqs, rmx=rmx):
                    @block.tensor
                    def _(t, p=p):
                        cV.wait(t, "t", at=q2sqd[p])
                        ch_wait(t, "t", PB["a1r"])
                        ch_wait(t, "t", PB["a3r"])
                        ch_wait(t, "t", PB["a3rb"])
                        t.matmul(P0[0:1, 512:1024], onesbf[0:64, :], Q2sq[0:64, :],
                                 start=True, stop=True, tile_position=(0, 0))
                        ins = t.matmul(P0[0:1, 1536:2048], onesbf[64:128, :],
                                       Q2sq[64:128, :], start=True, stop=True,
                                       tile_position=(64, 0))
                        sqs[p] = cPE.inc(ins)
                        PB["zq"] = (cPE, sqs[p])

                    @block.vector
                    def _(v, p=p):
                        cPE.wait(v, "v", at=sqs[p])
                        ins = v.reduce_max(stat[0:1, 7:8], P0[0:1, 512:1024], axis=AX.X)
                        cV.inc(ins)
                        ins = v.reduce_max(stat[0:1, 6:7], P0[0:1, 1536:2048], axis=AX.X)
                        c1 = cV.inc(ins)
                        cV.wait(v, "v", at=c1)
                        ins = v.tensor_tensor(stat[0:1, 5:6], stat[0:1, 7:8],
                                              stat[0:1, 6:7], op=ALU.max)
                        c2 = cV.inc(ins)
                        cV.wait(v, "v", at=c2)
                        ins = v.tensor_scalar(stat[0:1, 5:6], stat[0:1, 5:6], Mk2[:],
                                              None, op0=ALU.max)
                        rmx[p] = cV.inc(ins)
                        PB["a1r"] = (cV, rmx[p])
                        PB["a3r"] = (cV, rmx[p])

                def pro_c(p, rmx=rmx, bcc=bcc, bcp=bcp):
                    @block.tensor
                    def _(t, p=p):
                        cV.wait(t, "t", at=rmx[p])
                        ins = t.matmul(P0[:, 512:513], negsc[:], stat[0:1, 5:6],
                                       start=True, stop=True)
                        bcc[p] = cPE.inc(ins)

                    @block.vector
                    def _(v, p=p):
                        cPE.wait(v, "v", at=bcc[p])
                        ch_wait(v, "v", PB["bCch"][p % 2])
                        ins = v.tensor_copy(biasC[:, p % 2:p % 2 + 1], P0[:, 512:513])
                        bcp[p] = cV.inc(ins)
                        PB["a1r"] = (cV, bcp[p])

                def tail_v1(p, oev=oev, rcp=rcp, olast=olast):
                    @block.vector
                    def _(v, p=p):
                        cPE.wait(v, "v", at=PB["o"][olast[p]])
                        ins = v.tensor_copy(recs[0:33, :], P0[0:33, 1024:1536])
                        cV.inc(ins)
                        ins = v.tensor_copy(recs[64:97, :], P0[64:97, 1024:1536])
                        oev[p] = cV.inc(ins)
                        PB["a2r"] = (cV, oev[p])
                        cV.wait(v, "v", at=oev[p])
                        ch_wait(v, "v", PB["x1r"])
                        ch_wait(v, "v", PB["x1dma"])
                        ins = v.reciprocal(x1row[0:1, 0:512], recs[32:33, :])
                        cV.inc(ins)
                        ins = v.reciprocal(x1row[0:1, 512:1024], recs[96:97, :])
                        rcp[p] = cV.inc(ins)

                def tail_pe(p, rcp=rcp, bct=bct, ovtd=ovtd):
                    @block.tensor
                    def _(t, p=p):
                        cV.wait(t, "t", at=rcp[p])
                        ch_wait(t, "t", PB["a3r"])
                        t.matmul(P0[0:32, 1536:2048], onesrow[0:1, 0:32],
                                 x1row[0:1, 0:512], start=True, stop=True,
                                 tile_position=(0, 0))
                        ins = t.matmul(P0[64:96, 1536:2048], onesrow[0:1, 0:32],
                                       x1row[0:1, 512:1024], start=True, stop=True,
                                       tile_position=(0, 64))
                        bct[p] = cPE.inc(ins)
                        PB["x1r"] = (cPE, bct[p])

                    @block.vector
                    def _(v, p=p):
                        hA, hB = 2 * p, 2 * p + 1
                        cPE.wait(v, "v", at=bct[p])
                        cV.wait(v, "v", at=oev[p])
                        ins = v.tensor_tensor(OVT[32 * (hA % 4):32 * (hA % 4) + 32,
                                                  (hA // 4) * 512:(hA // 4) * 512 + 512],
                                              recs[0:32, :], P0[0:32, 1536:2048],
                                              op=ALU.mult)
                        cV.inc(ins)
                        ins = v.tensor_tensor(OVT[32 * (hB % 4):32 * (hB % 4) + 32,
                                                  (hB // 4) * 512:(hB // 4) * 512 + 512],
                                              recs[64:96, :], P0[64:96, 1536:2048],
                                              op=ALU.mult)
                        ovtd[p] = cV.inc(ins)
                        PB["a3r"] = (cV, ovtd[p])

                pro_a(0)
                pro_b(0)
                pro_c(0)

                @block.sync
                def _(sy, qb=qb, seq=seq):
                    for qq in range(2):
                        if PB["xfA"][qq] > 0:
                            cGD.wait(sy, "sy", at=PB["xfA"][qq])
                        tok0 = qb * QB + qq * 128
                        ins = sy.dma_start(x1stage[:, qq * 1024:(qq + 1) * 1024],
                                           dx.ap()[seq, tok0:tok0 + 128, :])
                        PB["xldA"][qq] = cLD.inc(ins, 16)

                for p in range(4):
                    gi0 = PB["gi"]

                    def emit_O(kt2, p=p, gi0=gi0, olast=olast):
                        g2 = gi0 + kt2

                        @block.tensor
                        def _(t, kt2=kt2, p=p, g2=g2):
                            cS.wait(t, "t", at=PB["exp"][g2])
                            if kt2 == 0:
                                ch_wait(t, "t", PB["a2r"])
                                ch_wait(t, "t", PB["a2rb"])
                            PT = PT0 if g2 % 2 == 0 else PT1
                            t.matmul(P0[0:33, 1024:1536],
                                     ckvkm[:, kt2 * 33:kt2 * 33 + 33], PT[:, 0:512],
                                     start=(kt2 == 0), stop=(kt2 == NKT - 1),
                                     tile_position=(0, 0))
                            ins = t.matmul(P0[64:97, 1024:1536],
                                           ckvkm[:, kt2 * 33:kt2 * 33 + 33],
                                           PT[:, 512:1024],
                                           start=(kt2 == 0), stop=(kt2 == NKT - 1),
                                           tile_position=(0, 64))
                            PB["o"][g2] = cPE.inc(ins)
                        if kt2 == NKT - 1:
                            olast[p] = g2

                    for kt in range(NKT):
                        gi = PB["gi"]
                        PB["gi"] = gi + 1

                        @block.tensor
                        def _(t, p=p, kt=kt, gi=gi):
                            if gi - 2 in PB["exp"]:
                                cS.wait(t, "t", at=PB["exp"][gi - 2])
                            if kt == 0:
                                cV.wait(t, "t", at=q2e[p])
                            sl = (gi % 2) * 1024
                            qslc = z1T[:, 512:1024] if p % 2 == 0 else Q2[:]
                            t.matmul(P1[:, sl:sl + 512],
                                     CKV2[0:64, kt * 128:(kt + 1) * 128],
                                     qslc[0:64, :], start=True, stop=True,
                                     tile_position=(0, 0))
                            ins = t.matmul(P1[:, sl + 512:sl + 1024],
                                           CKV2[64:128, kt * 128:(kt + 1) * 128],
                                           qslc[64:128, :], start=True, stop=True,
                                           tile_position=(64, 0))
                            PB["mm"][gi] = cPE.inc(ins)

                        if kt == 1 and p > 0:
                            tail_pe(p - 1)
                        if p < 3:
                            if kt == 4:
                                pro_a(p + 1)
                            elif kt == 6:
                                pro_b(p + 1)
                            elif kt == 8:
                                pro_c(p + 1)

                        @block.scalar
                        def _(s, p=p, kt=kt, gi=gi):
                            cPE.wait(s, "s", at=PB["mm"][gi])
                            if gi - 2 in PB["o"]:
                                cPE.wait(s, "s", at=PB["o"][gi - 2])
                            if kt == 0:
                                cV.wait(s, "s", at=bcp[p])
                            PT = PT0 if gi % 2 == 0 else PT1
                            ins = s.activation(PT[:],
                                               P1[:, (gi % 2) * 1024:(gi % 2) * 1024 + 1024],
                                               AF.Exp, bias=biasC[:, p % 2:p % 2 + 1],
                                               scale=SCALE)
                            PB["exp"][gi] = cS.inc(ins)
                            if kt == NKT - 1:
                                PB["bCch"][p % 2] = (cS, PB["exp"][gi])

                        if kt >= 1:
                            emit_O(kt - 1)
                    emit_O(NKT - 1)
                    PB["q2sl"][p % 2] = (cPE, PB["mm"][gi0 + NKT - 1])
                    tail_v1(p)
                    if p == 3:
                        tail_pe(3)

                opd = {}; elnd = {}; lnfin = {}; trcp = {}; normd = {}
                gel = {}; m1d = {}; sqd = {}; stm = {}; z2d = {}; mm2d = {}
                evd = {}; midd = {}

                @block.sync
                def _(sy, qb=qb, seq=seq):
                    for qq in range(2):
                        if PB["xfB"][qq] > 0:
                            cGD.wait(sy, "sy", at=PB["xfB"][qq])
                        if qq == 0:
                            cPE.wait(sy, "sy", at=bct[3])
                        tok0 = qb * QB + 256 + qq * 128
                        ins = sy.dma_start(outbuf[:, qq * 1024:(qq + 1) * 1024],
                                           dx.ap()[seq, tok0:tok0 + 128, :])
                        PB["xldB"][qq] = cLD.inc(ins, 16)

                def emit_outproj(h2, qq):
                    q4 = h2 * 2 + qq
                    psX = P1[:, qq * 1024:qq * 1024 + 1024]

                    @block.tensor
                    def _(t, q4=q4, qq=qq, psX=psX, h2=h2):
                        cV.wait(t, "t", at=ovtd[3])
                        ch_wait(t, "t", PB["psxr"][qq])
                        for cc in range(2):
                            for nn in range(2):
                                ins = t.matmul(psX[:, nn * 512:(nn + 1) * 512],
                                               OVT[:, cc * 512 + q4 * 128:
                                                   cc * 512 + (q4 + 1) * 128],
                                               WVO[:, cc * 1024 + nn * 512:
                                                   cc * 1024 + (nn + 1) * 512],
                                               start=(cc == 0), stop=(cc == 1))
                        opd[(h2, qq)] = cPE.inc(ins)

                def emit_evacln(h2, qq):
                    psX = P1[:, qq * 1024:qq * 1024 + 1024]
                    base = (x1stage if h2 == 0 else outbuf)[:, qq * 1024:qq * 1024 + 1024]
                    o = qq * 8
                    xld = (PB["xldA"] if h2 == 0 else PB["xldB"])[qq]

                    @block.vector
                    def _(v, h2=h2, qq=qq, psX=psX, base=base, o=o, xld=xld):
                        cPE.wait(v, "v", at=opd[(h2, qq)])
                        cLD.wait(v, "v", at=xld)
                        c0 = cV.inc(v.tensor_tensor(base, base, battnbc[:], op=ALU.add))
                        cV.wait(v, "v", at=c0)
                        ins = v.tensor_tensor(base, base, psX, op=ALU.add)
                        elnd[(h2, qq)] = cV.inc(ins)
                        PB["psxr"][qq] = (cV, elnd[(h2, qq)])
                        cV.wait(v, "v", at=elnd[(h2, qq)])
                        c1 = cV.inc(v.reduce_sum(stat[:, o:o + 1], base, axis=AX.X))
                        ch_wait(v, "v", PB["pt1r"])
                        ins = v.scalar_tensor_tensor(PT1[:], base, 0.0, base,
                                                     op0=ALU.add, op1=ALU.mult)
                        c2 = cV.inc(ins)
                        cV.wait(v, "v", at=c2)
                        ins = v.reduce_sum(stat[:, o + 1:o + 2], PT1[:], axis=AX.X)
                        c3 = cV.inc(ins)
                        cV.wait(v, "v", at=c1)
                        ins = v.tensor_scalar(stat[:, o + 2:o + 3], stat[:, o:o + 1],
                                              1.0 / D, None, op0=ALU.mult)
                        c4 = cV.inc(ins)
                        cV.wait(v, "v", at=c4)
                        ins = v.scalar_tensor_tensor(stat[:, o + 3:o + 4],
                                                     stat[:, o + 2:o + 3], 0.0,
                                                     stat[:, o + 2:o + 3],
                                                     op0=ALU.add, op1=ALU.mult)
                        c5 = cV.inc(ins)
                        cV.wait(v, "v", at=max(c3, c5))
                        ins = v.scalar_tensor_tensor(stat[:, o + 4:o + 5],
                                                     stat[:, o + 1:o + 2], 1.0 / D,
                                                     stat[:, o + 3:o + 4],
                                                     op0=ALU.mult, op1=ALU.subtract)
                        elnd[(h2, qq, "v")] = cV.inc(ins)

                def emit_lnfin(h2, qq):
                    o = qq * 8
                    base = (x1stage if h2 == 0 else outbuf)[:, qq * 1024:qq * 1024 + 1024]

                    @block.scalar
                    def _(s, h2=h2, qq=qq, o=o):
                        cV.wait(s, "s", at=elnd[(h2, qq, "v")])
                        ins = s.activation(stat[:, o + 5:o + 6], stat[:, o + 4:o + 5],
                                           AF.Ln, bias=epsv[:])
                        cs0 = cS.inc(ins)
                        cS.wait(s, "s", at=cs0)
                        ins = s.activation(stat[:, o + 6:o + 7], stat[:, o + 5:o + 6],
                                           AF.Exp, scale=-0.5)
                        lnfin[(h2, qq)] = cS.inc(ins)

                    @block.vector
                    def _(v, h2=h2, qq=qq, o=o, base=base):
                        cS.wait(v, "v", at=lnfin[(h2, qq)])
                        cV.wait(v, "v", at=elnd[(h2, qq, "v")])
                        ch_wait(v, "v", PB["pt1r"])
                        ins = v.tensor_scalar(PT1[:], base, stat[:, o + 2:o + 3],
                                              stat[:, o + 6:o + 7],
                                              op0=ALU.subtract, op1=ALU.mult)
                        normd[(h2, qq)] = cV.inc(ins)

                    for c in range(8):
                        @block.tensor
                        def _(t, c=c, h2=h2, qq=qq):
                            cV.wait(t, "t", at=normd[(h2, qq)])
                            if PB["trslot"][c % 2] > 0:
                                cV.wait(t, "t", at=PB["trslot"][c % 2])
                            if h2 == 0 and qq == 0 and c == 0:
                                cV.wait(t, "t", at=ovtd[3])
                            tr = TR_E if c % 2 == 0 else TR_O
                            ins = t.transpose(tr, PT1[:, c * 128:(c + 1) * 128], identbf[:])
                            trq = cPE.inc(ins)
                            if c == 7:
                                PB["pt1r"] = (cPE, trq)

                            @block.vector
                            def _(v, c=c, h2=h2, qq=qq, trq=trq):
                                cPE.wait(v, "v", at=trq)
                                tr = TR_E if c % 2 == 0 else TR_O
                                ins = v.tensor_copy(
                                    z1T[:, c * 256 + qq * 128: c * 256 + qq * 128 + 128],
                                    tr)
                                trcp[(h2, qq)] = cV.inc(ins)
                                PB["trslot"][c % 2] = trcp[(h2, qq)]

                def emit_ffn1(h2, hooks):
                    @block.scalar
                    def _(s):
                        load_act(s, ID_GELU)

                    for mch in range(32):
                        @block.tensor
                        def _(t, mch=mch, h2=h2):
                            if mch == 0:
                                cV.wait(t, "t", at=trcp[(h2, 1)])
                                if h2 == 0:
                                    cV.wait(t, "t", at=ovtd[3])
                            if mch >= 2:
                                cS.wait(t, "t", at=gel[(h2, mch - 2)])
                            else:
                                ch_wait(t, "t", PB["psm"][mch % 2])
                            psM = P0[:, 1024:1280] if mch % 2 == 0 else P0[:, 1536:1792]
                            for k in range(8):
                                ins = t.matmul(psM,
                                               T1T[:, k * 4096 + mch * 128:
                                                   k * 4096 + (mch + 1) * 128],
                                               z1T[:, k * 256:(k + 1) * 256],
                                               start=(k == 0), stop=(k == 7))
                            m1d[(h2, mch)] = cPE.inc(ins)

                        @block.scalar
                        def _(s, mch=mch, h2=h2):
                            cPE.wait(s, "s", at=m1d[(h2, mch)])
                            psM = P0[:, 1024:1280] if mch % 2 == 0 else P0[:, 1536:1792]
                            ins = s.activation(a1g[:, mch * 256:(mch + 1) * 256], psM,
                                               AF.Gelu, bias=beff1T[:, mch:mch + 1])
                            gel[(h2, mch)] = cS.inc(ins)

                        if mch >= 2:
                            emit_stat(h2, mch - 2)

                        @block.gpsimd
                        def _(g, mch=mch, h2=h2):
                            cS.wait(g, "g", at=gel[(h2, mch)])
                            if mch >= 2:
                                cPE.wait(g, "g", at=stm[(h2, mch - 2)])
                            else:
                                ch_wait(g, "g", PB["z2br"])
                            asl = a1g[:, mch * 256:(mch + 1) * 256]
                            ins = g.tensor_tensor(
                                z2buf[:, (mch % 2) * 256:(mch % 2) * 256 + 256],
                                asl, asl, op=ALU.mult)
                            sqd[(h2, mch)] = cG.inc(ins)

                        if mch in hooks:
                            hooks[mch]()
                    emit_stat(h2, 30)
                    emit_stat(h2, 31)
                    PB["psm"][0] = (cS, gel[(h2, 30)])
                    PB["psm"][1] = (cS, gel[(h2, 31)])

                def emit_stat(h2, mch):
                    @block.tensor
                    def _(t, mch=mch, h2=h2):
                        cG.wait(t, "t", at=sqd[(h2, mch)])
                        cS.wait(t, "t", at=gel[(h2, mch)])
                        if mch == 0:
                            ch_wait(t, "t", PB["a0ev"])
                        t.matmul(P0[0:1, 0:256], onesbf[:],
                                 a1g[:, mch * 256:(mch + 1) * 256],
                                 start=(mch == 0), stop=(mch == 31))
                        ins = t.matmul(P0[0:1, 256:512], onesbf[:],
                                       z2buf[:, (mch % 2) * 256:(mch % 2) * 256 + 256],
                                       start=False, stop=(mch == 31))
                        stm[(h2, mch)] = cPE.inc(ins)

                def emit_mid(h2):
                    muA = ln2scr[0:1, 0:256]
                    muB = ln2scr[0:1, 256:512]
                    mid = {}

                    @block.vector
                    def _(v, h2=h2):
                        cPE.wait(v, "v", at=stm[(h2, 31)])
                        ins = v.tensor_scalar(muA, P0[0:1, 0:256], 1.0 / HID, None,
                                              op0=ALU.mult)
                        c0 = cV.inc(ins)
                        cV.wait(v, "v", at=c0)
                        ins = v.scalar_tensor_tensor(muB, muA, 0.0, muA,
                                                     op0=ALU.add, op1=ALU.mult)
                        c1 = cV.inc(ins)
                        cV.wait(v, "v", at=c1)
                        ins = v.scalar_tensor_tensor(muB, P0[0:1, 256:512], 1.0 / HID,
                                                     muB, op0=ALU.mult, op1=ALU.subtract)
                        mid["v"] = cV.inc(ins)

                    @block.tensor
                    def _(t, h2=h2):
                        cV.wait(t, "t", at=mid["v"])
                        ch_wait(t, "t", PB["a0ev"])
                        ins = t.matmul(P0[:, 512:768], onesrow[:], muA,
                                       start=True, stop=True)
                        mid["bc"] = cPE.inc(ins)

                    @block.scalar
                    def _(s, h2=h2):
                        cV.wait(s, "s", at=mid["v"])
                        cPE.wait(s, "s", at=mid["bc"])
                        load_act(s, ID_NLEXP)
                        ins = s.activation(muA, muB, AF.Ln, bias=epsv[0:1, :])
                        cs0 = cS.inc(ins)
                        cS.wait(s, "s", at=cs0)
                        ins = s.activation(muB, muA, AF.Exp, scale=-0.5)
                        mid["s"] = cS.inc(ins)

                    @block.tensor
                    def _(t, h2=h2):
                        cS.wait(t, "t", at=mid["s"])
                        t.transpose(P0[:, 768:769], muB[0:1, 0:128], identf32[0:1, 0:1])
                        ins = t.transpose(P0[:, 769:770], muB[0:1, 128:256],
                                          identf32[0:1, 0:1])
                        mid["tr"] = cPE.inc(ins)

                    @block.vector
                    def _(v, h2=h2):
                        cPE.wait(v, "v", at=mid["tr"])
                        ins = v.tensor_copy(mu2bc[:], P0[:, 512:768])
                        cV.inc(ins)
                        ins = v.tensor_copy(rstd2c[:], P0[:, 768:770])
                        midd[h2] = cV.inc(ins)

                def emit_mm2(h2):
                    for mch in range(32):
                        @block.vector
                        def _(v, mch=mch, h2=h2):
                            if mch >= 2:
                                cPE.wait(v, "v", at=mm2d[(h2, mch - 2)])
                            else:
                                cV.wait(v, "v", at=midd[h2])
                            ins = v.tensor_tensor(
                                z2buf[:, (mch % 2) * 256:(mch % 2) * 256 + 256],
                                a1g[:, mch * 256:(mch + 1) * 256], mu2bc[:],
                                op=ALU.subtract)
                            z2d[(h2, mch)] = cV.inc(ins)

                        @block.tensor
                        def _(t, mch=mch, h2=h2):
                            cV.wait(t, "t", at=z2d[(h2, mch)])
                            if mch == 0:
                                ch_wait(t, "t", PB["evtk0"])
                                ch_wait(t, "t", PB["a0ev"])
                            for tk in range(2):
                                for nn in range(2):
                                    dst = (P1[:, 1024 + nn * 512:1024 + (nn + 1) * 512]
                                           if tk == 0
                                           else P0[:, nn * 512:(nn + 1) * 512])
                                    ins = t.matmul(
                                        dst,
                                        z2buf[:, (mch % 2) * 256 + tk * 128:
                                              (mch % 2) * 256 + tk * 128 + 128],
                                        W2T[:, mch * 1024 + nn * 512:
                                            mch * 1024 + (nn + 1) * 512],
                                        start=(mch == 0), stop=(mch == 31))
                            mm2d[(h2, mch)] = cPE.inc(ins)
                    PB["z2br"] = (cPE, mm2d[(h2, 31)])

                def emit_evac(h2):
                    for tk in range(2):
                        @block.vector
                        def _(v, tk=tk, h2=h2):
                            cPE.wait(v, "v", at=mm2d[(h2, 31)])
                            cV.wait(v, "v", at=max(midd[h2], elnd[(h2, tk)]))
                            ob = (x1stage if h2 == 0 else outbuf)[:, tk * 1024:(tk + 1) * 1024]
                            src = P1[:, 1024:2048] if tk == 0 else P0[:, 0:1024]
                            ins = v.scalar_tensor_tensor(ob, src, rstd2c[:, tk:tk + 1],
                                                         ob, op0=ALU.mult, op1=ALU.add)
                            c0 = cV.inc(ins)
                            cV.wait(v, "v", at=c0)
                            ins = v.tensor_tensor(ob, ob, b2bc[:], op=ALU.add)
                            evd[(h2, tk)] = cV.inc(ins)
                            if tk == 0:
                                PB["evtk0"] = (cV, evd[(h2, tk)])
                            else:
                                PB["a0ev"] = (cV, evd[(h2, tk)])

                        @block.gpsimd
                        def _(g, tk=tk, h2=h2, qb=qb, seq=seq):
                            cV.wait(g, "g", at=evd[(h2, tk)])
                            tok0 = qb * QB + h2 * 256 + tk * 128
                            src = (x1stage if h2 == 0 else outbuf)[:, tk * 1024:(tk + 1) * 1024]
                            ins = g.dma_start(dy.ap()[seq, tok0:tok0 + 128, :], src)
                            c = cGD.inc(ins, 16)
                            if h2 == 0:
                                PB["xfA"][tk] = c
                            else:
                                PB["xfB"][tk] = c
                                if tk == 0:
                                    PB["x1dma"] = (cGD, c)

                if DBG_FFN:
                    emit_outproj(0, 0)
                    emit_evacln(0, 0)
                    emit_lnfin(0, 0)
                    emit_outproj(0, 1)
                    emit_evacln(0, 1)
                    emit_lnfin(0, 1)
                    emit_ffn1(0, {4: lambda: (emit_outproj(1, 0), emit_evacln(1, 0)),
                                  12: lambda: (emit_outproj(1, 1), emit_evacln(1, 1))})
                    emit_mid(0)
                    emit_mm2(0)
                    emit_lnfin(1, 0)
                    emit_lnfin(1, 1)
                    emit_evac(0)
                    emit_ffn1(1, {})
                    emit_mid(1)
                    emit_mm2(1)
                    emit_evac(1)
                    PB["a2r"] = (cS, gel[(1, 30)])
                    PB["a3r"] = (cS, gel[(1, 31)])
                    PB["a2rb"] = (cV, trcp[(1, 1)])
                    PB["a3rb"] = (cV, trcp[(1, 1)])
                    PB["a0r"] = PB["a0ev"]
                    PB["a1r"] = PB["a0ev"]
                    PB["zq"] = (cPE, m1d[(1, 31)])

        if DBG_DUMP:
            @block.gpsimd
            def _(g):
                cV.wait(g, "g")
                cS.wait(g, "g")
                cPE.wait(g, "g")
                pass

            @block.vector
            def _(v):
                cV.wait(v, "v")
                cS.wait(v, "v")
                pass

            @block.gpsimd
            def _(g):
                cV.wait(g, "gd")
                cS.wait(g, "gd")
                cPE.wait(g, "gd")
                cGD.inc(g.dma_start(ddump.ap()[:, 0:2048].bitcast(BF16), CKV2[:]), 16)
                cGD.inc(g.dma_start(ddump.ap()[:, 2048:4096].bitcast(BF16)[0:32, :], cqT[:]), 16)
                cGD.inc(g.dma_start(ddump.ap()[:, 4096:4528].bitcast(BF16)[:, 0:528], ckvkm[:, 0:528]), 16)
                cGD.inc(g.dma_start(ddump.ap()[:, 4528:4784].bitcast(BF16)[:, 0:512], Q2[:]), 16)
                cGD.inc(g.dma_start(ddump.ap()[:, 4784:5296].bitcast(BF16)[:, 0:1024], OVT[:]), 16)
                cGD.inc(g.dma_start(ddump.ap()[:, 5296:5552].bitcast(BF16)[:, 0:512], z1T[:, 0:512]), 16)
                cGD.inc(g.dma_start(ddump.ap()[:, 5552:5553], biasC[:],
                                    allow_slow_non_contiguous=True), 16)
                cGD.inc(g.dma_start(ddump.ap()[:, 5553:5554], stat[:, 6:7],
                                    allow_slow_non_contiguous=True), 16)

        @block.gpsimd
        def _(g):
            cGD.wait(g, "gfinal")

    return nc


_NC_CACHE = {}


def kernel(**inputs):
    x = np.asarray(inputs["L0_attn_h_t"], dtype=np.float32)
    f = lambda k: np.ascontiguousarray(np.asarray(inputs[k], dtype=np.float32))
    wcat = np.ascontiguousarray(np.concatenate(
        [f("L0_attn_W_DQ"), f("L0_attn_W_DKV"), f("L0_attn_W_KR")], axis=1))
    bcat = np.ascontiguousarray(np.concatenate(
        [f("L0_attn_b_DQ"), f("L0_attn_b_DKV"), f("L0_attn_b_KR")]).reshape(96, 1))
    common = {
        "wcat": wcat, "bcat": bcat,
        "wuq": f("L0_attn_W_UQ"), "wuk": f("L0_attn_W_UK"), "wuv": f("L0_attn_W_UV"),
        "wqr": f("L0_attn_W_QR"),
        "buq8": f("L0_attn_b_UQ").reshape(8, 128),
        "bqr8": f("L0_attn_b_QR").reshape(8, 32),
        "buv8": f("L0_attn_b_UV").reshape(8, 128),
        "wo": f("L0_attn_W_O"), "bo": f("L0_attn_b_O").reshape(1, 1024),
        "g1_8": f("L0_ffn_ln1_g").reshape(8, 128),
        "b1ln8": f("L0_ffn_ln1_b").reshape(8, 128),
        "b1row": f("L0_ffn_b1").reshape(1, 4096),
        "g2_32": f("L0_ffn_ln2_g").reshape(32, 128),
        "b2ln32": f("L0_ffn_ln2_b").reshape(32, 128),
        "b2row": f("L0_ffn_b2").reshape(1, 1024),
        "w1": f("L0_ffn_w1"), "w2": f("L0_ffn_w2"),
    }
    if "nc" not in _NC_CACHE:
        _NC_CACHE["nc"] = build_program()
    nc = _NC_CACHE["nc"]
    in_maps = []
    for c in range(NCORES):
        m = dict(common)
        m["x"] = np.ascontiguousarray(x[c * BPC:(c + 1) * BPC])
        in_maps.append(m)
    res = run_bass_kernel_spmd(nc, in_maps, core_ids=list(range(NCORES)))
    return np.concatenate([res.results[c]["y"] for c in range(NCORES)], axis=0)


if __name__ == "__main__":
    build_program()
    print("build OK")


# version 06
# speedup vs baseline: 1.2942x; 1.0541x over previous
import contextlib
import numpy as np
import concourse.bass as bass
import concourse.mybir as mybir
from concourse.bass_utils import run_bass_kernel_spmd

F32 = mybir.dt.float32
BF16 = mybir.dt.bfloat16
AF = mybir.ActivationFunctionType
ALU = mybir.AluOpType
AX = mybir.AxisListType

NCORES = 8
BPC = 2
S = 4096
D = 1024
H = 8
HID = 4096
NT = S // 128
QB = 512
NQB = S // QB
NKT = S // 128
SCALE = 1.0 / float(np.sqrt(160.0))
EPS = 1e-5


class SemCtr:
    def __init__(self, nc, es, name):
        self.sem = es.enter_context(nc.semaphore(name))
        self.n = 0
        self.last = {}

    def inc(self, instr, k=1):
        instr.then_inc(self.sem, k)
        self.n += k
        return self.n

    def wait(self, eng, key, at=None):
        v = self.n if at is None else at
        if v <= 0:
            return
        if self.last.get(key, -1) >= v:
            return
        eng.wait_ge(self.sem, v)
        self.last[key] = v


def build_program():
    import os
    DBG_SEQS = int(os.environ.get("KMK_SEQS", BPC))
    DBG_NT = int(os.environ.get("KMK_NT", NT))
    DBG_NQB = int(os.environ.get("KMK_NQB", NQB))
    DBG_PAIRS = int(os.environ.get("KMK_PAIRS", 4))
    DBG_FFN = int(os.environ.get("KMK_FFN", 1))
    DBG_DUMP = int(os.environ.get("KMK_DUMP", 0))
    DBG_PREP = int(os.environ.get("KMK_PREP", 99))
    nc = bass.Bass("TRN2", target_bir_lowering=False)

    dx = nc.dram_tensor("x", [BPC, S, D], F32, kind="ExternalInput")
    dy = nc.dram_tensor("y", [BPC, S, D], F32, kind="ExternalOutput")
    dwcat = nc.dram_tensor("wcat", [D, 96], F32, kind="ExternalInput")
    dbcat = nc.dram_tensor("bcat", [96, 1], F32, kind="ExternalInput")
    dwuq = nc.dram_tensor("wuq", [32, 1024], F32, kind="ExternalInput")
    dwuk = nc.dram_tensor("wuk", [32, 1024], F32, kind="ExternalInput")
    dwuv = nc.dram_tensor("wuv", [32, 1024], F32, kind="ExternalInput")
    dwqr = nc.dram_tensor("wqr", [32, 256], F32, kind="ExternalInput")
    dbuq = nc.dram_tensor("buq8", [8, 128], F32, kind="ExternalInput")
    dbqr = nc.dram_tensor("bqr8", [8, 32], F32, kind="ExternalInput")
    dbuv = nc.dram_tensor("buv8", [8, 128], F32, kind="ExternalInput")
    dwo = nc.dram_tensor("wo", [1024, 1024], F32, kind="ExternalInput")
    dbo = nc.dram_tensor("bo", [1, 1024], F32, kind="ExternalInput")
    dg1 = nc.dram_tensor("g1_8", [8, 128], F32, kind="ExternalInput")
    db1ln = nc.dram_tensor("b1ln8", [8, 128], F32, kind="ExternalInput")
    db1 = nc.dram_tensor("b1row", [1, 4096], F32, kind="ExternalInput")
    dg2 = nc.dram_tensor("g2_32", [32, 128], F32, kind="ExternalInput")
    db2ln = nc.dram_tensor("b2ln32", [32, 128], F32, kind="ExternalInput")
    db2 = nc.dram_tensor("b2row", [1, 1024], F32, kind="ExternalInput")
    dw1 = nc.dram_tensor("w1", [HID, D], F32, kind="ExternalInput")
    dw2 = nc.dram_tensor("w2", [D, HID], F32, kind="ExternalInput")
    if DBG_DUMP:
        ddump = nc.dram_tensor("dump", [128, 8192], F32, kind="ExternalOutput")

    from concourse.hw_specs import get_activation_tables
    act_sets = list(get_activation_tables(nc.m.arch).keys())
    ID_NLEXP = act_sets.index("natural_log_exp_and_others")
    ID_GELU = act_sets.index("gelu_and_others")

    es = contextlib.ExitStack()
    with es:
        E = es.enter_context
        T1T = E(nc.sbuf_tensor("T1T", [128, 8 * 4096], BF16))
        W2T = E(nc.sbuf_tensor("W2T", [128, 32 * 1024], BF16))
        CKV2 = E(nc.sbuf_tensor("CKV2", [128, S], BF16))
        cqT = E(nc.sbuf_tensor("cqT", [32, S], BF16))
        ckvkm = E(nc.sbuf_tensor("ckvkm", [128, 33 * NKT], BF16))
        z1T = E(nc.sbuf_tensor("z1T", [128, 2048], BF16))
        PT0 = E(nc.sbuf_tensor("PT0", [128, 1024], BF16))
        PT1 = E(nc.sbuf_tensor("PT1", [128, 1024], BF16))
        Q2 = E(nc.sbuf_tensor("Q2", [128, QB], BF16))
        OVT = E(nc.sbuf_tensor("OVT", [128, 1024], BF16))
        WVO = E(nc.sbuf_tensor("WVO", [128, 2048], BF16))
        wcat8 = E(nc.sbuf_tensor("wcat8", [128, 8 * 96], BF16))
        wqkr = E(nc.sbuf_tensor("wqkr", [32, 8 * 64], BF16))
        qbias = E(nc.sbuf_tensor("qbias", [128, 4], F32))
        battnbc = E(nc.sbuf_tensor("battnbc", [128, 1024], BF16))
        b2bc = E(nc.sbuf_tensor("b2bc", [128, 1024], BF16))
        beff1T = E(nc.sbuf_tensor("beff1T", [128, 32], F32))
        identbf = E(nc.sbuf_tensor("identbf", [128, 128], BF16))
        identf32 = E(nc.sbuf_tensor("identf32", [32, 32], F32))
        arena = E(nc.sbuf_tensor("arena", [128, 17408], BF16))
        bconst = E(nc.sbuf_tensor("bconst", [128, 1], F32))
        onesbf = E(nc.sbuf_tensor("onesbf", [128, 1], BF16))
        onescf = E(nc.sbuf_tensor("onescf", [128, 1], F32))
        onesrow = E(nc.sbuf_tensor("onesrow", [1, 128], F32))
        negsc = E(nc.sbuf_tensor("negsc", [1, 128], F32))
        epsv = E(nc.sbuf_tensor("epsv", [128, 1], F32))
        biasC = E(nc.sbuf_tensor("biasC", [128, 2], F32))
        Mk2 = E(nc.sbuf_tensor("Mk2", [1, 1], F32))
        ms = E(nc.sbuf_tensor("ms", [1, 1], F32))
        thrv = E(nc.sbuf_tensor("thrv", [128, 1], F32))
        nthrv = E(nc.sbuf_tensor("nthrv", [128, 1], F32))
        mvec = E(nc.sbuf_tensor("mvec", [128, 1], F32))
        gcols = E(nc.sbuf_tensor("gcols", [128, 32], F32))
        blncols = E(nc.sbuf_tensor("blncols", [128, 32], BF16))
        stat = E(nc.sbuf_tensor("stat", [128, 16], F32))
        rstd2c = E(nc.sbuf_tensor("rstd2c", [128, 2], F32))

        tmp96 = Q2[:, 0:128]
        P0 = E(nc.psum_tensor("P0", [128, 2048], F32))
        P1 = E(nc.psum_tensor("P1", [128, 2048], F32))
        pbf0 = P0.bitcast(BF16)
        pbf1 = P1.bitcast(BF16)

        cPE = SemCtr(nc, es, "cPE")
        cV = SemCtr(nc, es, "cV")
        cS = SemCtr(nc, es, "cS")
        cG = SemCtr(nc, es, "cG")
        cLD = SemCtr(nc, es, "cLD")
        cGD = SemCtr(nc, es, "cGD")

        block = E(nc.Block())

        a1g = arena[:, 0:8192]
        x1stage = arena[:, 8192:12288].bitcast(F32)
        outbuf = arena[:, 12288:16384].bitcast(F32)
        z2buf = arena[:, 16384:16896]
        mu2bc = arena[:, 16896:17408].bitcast(F32)
        wchunk = arena[:, 0:8192].bitcast(F32)
        bocur = arena[:, 0:8192].bitcast(F32)[0:1, 0:4096]
        brow2 = arena[:, 8192:17408].bitcast(F32)[0:1, 4096:4608]
        trow = arena[:, 8192:17408]
        trowf = arena[:, 8192:17408].bitcast(F32)
        Q2sq = z1T[:, 0:512]
        recs = z1T.bitcast(F32)[:, 512:1024]
        rowf = PT1.bitcast(F32)
        ln2scr = PT0.bitcast(F32)

        def load_act(s, set_id):
            s.add_instruction(mybir.InstLoadActFuncSet(
                name=nc.get_next_instruction_name(),
                act_func_set_id=set_id, ins=[], outs=[]))

        @block.gpsimd
        def _(g):
            g.iota(wchunk[:, 0:128], pattern=[[1, 128]], base=0,
                   channel_multiplier=0, allow_small_or_imprecise_dtypes=True)
            ins = g.iota(wchunk[:, 128:129], pattern=[[0, 1]], base=0,
                         channel_multiplier=1, allow_small_or_imprecise_dtypes=True)
            cG.inc(ins)
            cGD.inc(g.dma_start(wcat8[:].rearrange("p (c n) -> p c n", c=8),
                                dwcat.ap().rearrange("(c p) n -> p c n", p=128)), 16)
            cGD.inc(g.dma_start(trow[0:32, 0:1024], dwuq.ap()), 16)
            cGD.inc(g.dma_start(trow[0:32, 1024:2048], dwuk.ap()), 16)
            cGD.inc(g.dma_start(trow[0:32, 2048:3072], dwuv.ap()), 16)
            cGD.inc(g.dma_start(trow[0:32, 3072:3328], dwqr.ap()), 16)
            cGD.inc(g.dma_start(trow[0:8, 3328:3456], dbuq.ap()), 16)
            cGD.inc(g.dma_start(trow[0:8, 3456:3584], dbuv.ap()), 16)

        @block.sync
        def _(sy):
            cLD.inc(sy.dma_start(bconst[0:96, :], dbcat.ap()), 16)
            cLD.inc(sy.dma_start(trowf[0:8, 3208:3240], dbqr.ap()), 16)

        @block.vector
        def _(v):
            cG.wait(v, "v")
            ins = v.tensor_scalar(wchunk[:, 256:384], wchunk[:, 0:128],
                                  wchunk[:, 128:129], None, op0=ALU.is_equal)
            cV.inc(ins)
            cV.wait(v, "v")
            v.tensor_copy(identbf[:], wchunk[:, 256:384])
            v.tensor_copy(identf32[:], wchunk[0:32, 256:288])
            v.memset(onesbf[:], 1.0)
            v.memset(onescf[:], 1.0)
            v.memset(onesrow[:], 1.0)
            v.memset(negsc[:], -SCALE)
            v.memset(epsv[:], EPS)
            ins = v.memset(ckvkm[:, 32:33 * NKT:33], 1.0)
            cV.inc(ins)


        if DBG_PREP <= 1:
            @block.gpsimd
            def _(g):
                cGD.wait(g, "gfinal")
            return nc
        WUQT = trow[:, 3584:3840]
        WUKT = trow[:, 3840:4096]
        WUVT = trow[:, 4096:4352]
        buqc = trow[:, 4352:4360]
        buvc = trow[:, 4360:4368]

        @block.tensor
        def _(t):
            cGD.wait(t, "t")
            for h in range(H):
                cV.wait(t, "t")
                t.transpose(pbf0[:, 0:32], trow[0:32, h * 128:(h + 1) * 128],
                            identbf[0:32, 0:32])
                t.transpose(pbf0[:, 1024:1056], trow[0:32, 1024 + h * 128:1024 + (h + 1) * 128],
                            identbf[0:32, 0:32])
                ins = t.transpose(pbf0[:, 2048:2080], trow[0:32, 2048 + h * 128:2048 + (h + 1) * 128],
                                  identbf[0:32, 0:32])
                cPE.inc(ins)

                @block.vector
                def _(v, h=h):
                    cPE.wait(v, "v")
                    v.tensor_copy(WUQT[:, h * 32:(h + 1) * 32], pbf0[:, 0:32])
                    v.tensor_copy(WUKT[:, h * 32:(h + 1) * 32], pbf0[:, 1024:1056])
                    ins = v.tensor_copy(WUVT[:, h * 32:(h + 1) * 32], pbf0[:, 2048:2080])
                    cV.inc(ins)

        @block.tensor
        def _(t):
            cV.wait(t, "t")
            cLD.wait(t, "t")
            t.transpose(pbf0[:, 0:8], trow[0:8, 3328:3456], identbf[0:8, 0:8])
            ins = t.transpose(pbf0[:, 1024:1032], trow[0:8, 3456:3584], identbf[0:8, 0:8])
            cPE.inc(ins)
            ins = t.transpose(P0[0:32, 1536:1544], trowf[0:8, 3208:3240], identf32[0:8, 0:8])
            cPE.inc(ins)

        @block.vector
        def _(v):
            cPE.wait(v, "v")
            v.tensor_copy(buqc[:], pbf0[:, 0:8])
            ins = v.tensor_copy(buvc[:], pbf0[:, 1024:1032])
            cV.inc(ins)
            for h in range(H):
                r0 = 32 if h % 2 == 0 else 96
                ins = v.tensor_copy(qbias[r0:r0 + 32, h // 2:h // 2 + 1],
                                    P0[0:32, 1536 + h:1537 + h])
            cV.inc(ins)


        if DBG_PREP <= 2:
            @block.gpsimd
            def _(g):
                cGD.wait(g, "gfinal")
            return nc
        @block.tensor
        def _(t):
            for h in range(H):
                cV.wait(t, "t")
                t.matmul(P1[0:32, 0:32], WUQT[:, h * 32:(h + 1) * 32],
                         WUKT[:, h * 32:(h + 1) * 32], start=True, stop=True)
                ins = t.matmul(P1[0:32, 512:513], WUKT[:, h * 32:(h + 1) * 32],
                               buqc[:, h:h + 1], start=True, stop=True)
                cPE.inc(ins)

                @block.vector
                def _(v, h=h):
                    cPE.wait(v, "v")
                    v.tensor_copy(wqkr[:, h * 64:h * 64 + 32], P1[0:32, 0:32])
                    r0 = 0 if h % 2 == 0 else 64
                    v.tensor_copy(qbias[r0:r0 + 32, h // 2:h // 2 + 1], P1[0:32, 512:513])
                    ins = v.tensor_copy(wqkr[:, h * 64 + 32:h * 64 + 64],
                                        trow[0:32, 3072 + h * 32:3072 + (h + 1) * 32])
                    cV.inc(ins)


        if DBG_PREP <= 3:
            @block.gpsimd
            def _(g):
                cGD.wait(g, "gfinal")
            return nc
        @block.sync
        def _(sy):
            cV.wait(sy, "sy")
            ins = sy.dma_start(bocur[0:1, 0:1024], dbo.ap())
            cLD.inc(ins, 16)

        for c in range(8):
            @block.gpsimd
            def _(g, c=c):
                cPE.wait(g, "g")
                ins = g.dma_start(trow[:, 4368 + (c % 2) * 1024: 4368 + (c % 2) * 1024 + 1024],
                                  dwo.ap()[c * 128:(c + 1) * 128, :])
                cGD.inc(ins, 16)

            @block.tensor
            def _(t, c=c):
                cGD.wait(t, "t")
                cV.wait(t, "t")
                wob = trow[:, 4368 + (c % 2) * 1024: 4368 + (c % 2) * 1024 + 1024]
                t.matmul(P1[0:32, 1024:1536], WUVT[:, c * 32:(c + 1) * 32], wob[:, 0:512],
                         start=True, stop=True)
                t.matmul(P1[0:32, 1536:2048], WUVT[:, c * 32:(c + 1) * 32], wob[:, 512:1024],
                         start=True, stop=True)
                t.matmul(P0[0:1, 0:512], buvc[:, c:c + 1], wob[:, 0:512],
                         start=(c == 0), stop=(c == 7))
                ins = t.matmul(P0[0:1, 512:1024], buvc[:, c:c + 1], wob[:, 512:1024],
                               start=(c == 0), stop=(c == 7))
                cPE.inc(ins)

            @block.vector
            def _(v, c=c):
                cPE.wait(v, "v")
                r0 = 32 * (c % 4)
                ins = v.tensor_copy(WVO[r0:r0 + 32, (c // 4) * 1024:(c // 4) * 1024 + 1024],
                                    P1[0:32, 1024:2048])
                cV.inc(ins)

        @block.vector
        def _(v):
            cPE.wait(v, "v")
            cLD.wait(v, "v")
            ins = v.tensor_tensor(bocur[0:1, 0:1024], P0[0:1, 0:1024], bocur[0:1, 0:1024],
                                  op=ALU.add)
            cV.inc(ins)

        @block.tensor
        def _(t):
            cV.wait(t, "t")
            t.matmul(P1[:, 0:512], onesrow[:], bocur[0:1, 0:512], start=True, stop=True)
            ins = t.matmul(P1[:, 512:1024], onesrow[:], bocur[0:1, 512:1024], start=True, stop=True)
            cPE.inc(ins)

        @block.vector
        def _(v):
            cPE.wait(v, "v")
            ins = v.tensor_copy(battnbc[:], P1[:, 0:1024])
            cV.inc(ins)


        if DBG_PREP <= 4:
            @block.gpsimd
            def _(g):
                cGD.wait(g, "gfinal")
            return nc
        def load_gcols(widx):
            nk = 8 if widx == 1 else 32
            src = dg1 if widx == 1 else dg2

            @block.sync
            def _(sy):
                cPE.wait(sy, "sy")
                ins = sy.dma_start(trowf[0:nk, 2048:2176], src.ap())
                cLD.inc(ins, 16)

            @block.tensor
            def _(t):
                cLD.wait(t, "t")
                cV.wait(t, "t")
                ins = t.transpose(P0[:, 1024:1024 + nk], trowf[0:nk, 2048:2176],
                                  identf32[0:nk, 0:nk])
                cPE.inc(ins)

            @block.vector
            def _(v):
                cPE.wait(v, "v")
                ins = v.tensor_copy(gcols[:, 0:nk], P0[:, 1024:1024 + nk])
                cV.inc(ins)

        def ternarize(widx):
            dw = dw1 if widx == 1 else dw2
            nchunks = 32 if widx == 1 else 8
            cols = 1024 if widx == 1 else 4096
            TT = T1T if widx == 1 else W2T
            nblk = cols // 128
            nk = 8 if widx == 1 else 32

            @block.vector
            def _(v):
                ins = v.memset(stat[:, 4:5], 0.0)
                cV.inc(ins)

            for c in range(nchunks):
                @block.sync
                def _(sy, c=c):
                    cV.wait(sy, "sy")
                    ins = sy.dma_start(wchunk[:, 0:cols], dw.ap()[c * 128:(c + 1) * 128, :])
                    cLD.inc(ins, 16)

                @block.vector
                def _(v, c=c):
                    cLD.wait(v, "v")
                    ins = v.reduce_sum(stat[:, 5:6], wchunk[:, 0:cols], axis=AX.X,
                                       apply_absolute_value=True)
                    cV.inc(ins)
                    cV.wait(v, "v")
                    ins = v.tensor_tensor(stat[:, 4:5], stat[:, 4:5], stat[:, 5:6], op=ALU.add)
                    cV.inc(ins)

            @block.tensor
            def _(t):
                cV.wait(t, "t")
                ins = t.matmul(P0[0:1, 0:1], onescf[:], stat[:, 4:5], start=True, stop=True)
                cPE.inc(ins)

            @block.vector
            def _(v):
                cPE.wait(v, "v")
                ins = v.tensor_scalar(ms[:], P0[0:1, 0:1], 1.0 / (4096.0 * 1024.0), 1e-5,
                                      op0=ALU.mult, op1=ALU.max)
                cV.inc(ins)

            @block.tensor
            def _(t):
                cV.wait(t, "t")
                ins = t.matmul(P0[:, 4:5], onesrow[:], ms[:], start=True, stop=True)
                cPE.inc(ins)

            @block.vector
            def _(v):
                cPE.wait(v, "v")
                ins = v.tensor_copy(mvec[:], P0[:, 4:5])
                cV.inc(ins)
                cV.wait(v, "v")
                ins = v.tensor_scalar(thrv[:], mvec[:], 0.5, None, op0=ALU.mult)
                cV.inc(ins)
                cV.wait(v, "v")
                ins = v.tensor_scalar(nthrv[:], thrv[:], -1.0, None, op0=ALU.mult)
                cV.inc(ins)
                gsl = gcols[:, 0:nk]
                ins = v.tensor_scalar(gsl, gsl, mvec[:], None, op0=ALU.mult)
                cV.inc(ins)

            for c in range(nchunks):
                @block.sync
                def _(sy, c=c):
                    cV.wait(sy, "sy")
                    cPE.wait(sy, "sy")
                    ins = sy.dma_start(wchunk[:, 0:cols], dw.ap()[c * 128:(c + 1) * 128, :])
                    cLD.inc(ins, 16)

                @block.vector
                def _(v, c=c):
                    cLD.wait(v, "v")
                    cPE.wait(v, "v")
                    ins = v.tensor_scalar(trowf[:, 0:cols], wchunk[:, 0:cols], thrv[:],
                                          nthrv[:], op0=ALU.min, op1=ALU.max)
                    cV.inc(ins)
                    cV.wait(v, "v")
                    ins = v.tensor_tensor(wchunk[:, 0:cols], wchunk[:, 0:cols],
                                          trowf[:, 0:cols], op=ALU.subtract)
                    cV.inc(ins)
                    cV.wait(v, "v")
                    ins = v.tensor_scalar(wchunk[:, 0:cols], wchunk[:, 0:cols], 1e30, 1.0,
                                          op0=ALU.mult, op1=ALU.min)
                    cV.inc(ins)
                    cV.wait(v, "v")
                    ins = v.tensor_scalar(trow[:, 0:cols], wchunk[:, 0:cols], -1.0, None,
                                          op0=ALU.max)
                    cV.inc(ins)

                @block.tensor
                def _(t, c=c):
                    for k in range(nblk):
                        cV.wait(t, "t")
                        ins = t.transpose(pbf0[:, (k % 2) * 1024:(k % 2) * 1024 + 128],
                                          trow[:, k * 128:(k + 1) * 128], identbf[:])
                        cPE.inc(ins)

                        @block.vector
                        def _(v, c=c, k=k):
                            cPE.wait(v, "v")
                            if widx == 1:
                                dst = TT[:, k * 4096 + c * 128: k * 4096 + (c + 1) * 128]
                            else:
                                dst = TT[:, k * 1024 + c * 128: k * 1024 + (c + 1) * 128]
                            ins = v.tensor_copy(dst, pbf0[:, (k % 2) * 1024:(k % 2) * 1024 + 128])
                            cV.inc(ins)

            @block.gpsimd
            def _(g):
                cV.wait(g, "g")
                src = db1ln if widx == 1 else db2ln
                ins = g.dma_start(trow[0:nk, 0:128], src.ap())
                cGD.inc(ins, 16)

            @block.tensor
            def _(t):
                cGD.wait(t, "t")
                ins = t.transpose(pbf0[:, 0:nk], trow[0:nk, 0:128], identbf[0:nk, 0:nk])
                cPE.inc(ins)

            @block.vector
            def _(v):
                cPE.wait(v, "v")
                ins = v.tensor_copy(blncols[:, 0:nk], pbf0[:, 0:nk])
                cV.inc(ins)

            ncolsO = 4096 if widx == 1 else 1024
            brow = db1 if widx == 1 else db2

            @block.sync
            def _(sy):
                cV.wait(sy, "sy")
                ins = sy.dma_start(bocur[0:1, 0:ncolsO], brow.ap())
                cLD.inc(ins, 16)

            for nb in range(ncolsO // 512):
                @block.tensor
                def _(t, nb=nb):
                    cV.wait(t, "t")
                    cLD.wait(t, "t")
                    for k in range(nk):
                        if widx == 1:
                            rhs = TT[:, k * 4096 + nb * 512: k * 4096 + nb * 512 + 512]
                        else:
                            rhs = TT[:, k * 1024 + (nb % 2) * 512: k * 1024 + (nb % 2) * 512 + 512]
                        ins = t.matmul(P0[0:1, 1024 + (nb % 2) * 512: 1536 + (nb % 2) * 512],
                                       blncols[:, k:k + 1], rhs,
                                       start=(k == 0), stop=(k == nk - 1))
                    cPE.inc(ins)

                @block.vector
                def _(v, nb=nb):
                    cPE.wait(v, "v")
                    ps = P0[0:1, 1024 + (nb % 2) * 512: 1536 + (nb % 2) * 512]
                    ins = v.tensor_scalar(brow2[0:1, 0:512], ps, ms[0:1, :], None, op0=ALU.mult)
                    cV.inc(ins)
                    cV.wait(v, "v")
                    ins = v.tensor_tensor(bocur[0:1, nb * 512:(nb + 1) * 512],
                                          bocur[0:1, nb * 512:(nb + 1) * 512],
                                          brow2[0:1, 0:512], op=ALU.add)
                    cV.inc(ins)

            if widx == 1:
                @block.tensor
                def _(t):
                    cV.wait(t, "t")
                    for nb in range(32):
                        ins = t.transpose(P0[:, nb:nb + 1], bocur[0:1, nb * 128:(nb + 1) * 128],
                                          identf32[0:1, 0:1])
                    cPE.inc(ins)

                @block.vector
                def _(v):
                    cPE.wait(v, "v")
                    ins = v.tensor_copy(beff1T[:], P0[:, 0:32])
                    cV.inc(ins)
            else:
                @block.tensor
                def _(t):
                    cV.wait(t, "t")
                    t.matmul(P0[:, 0:512], onesrow[:], bocur[0:1, 0:512], start=True, stop=True)
                    ins = t.matmul(P0[:, 512:1024], onesrow[:], bocur[0:1, 512:1024],
                                   start=True, stop=True)
                    cPE.inc(ins)

                @block.vector
                def _(v):
                    cPE.wait(v, "v")
                    ins = v.tensor_copy(b2bc[:], P0[:, 0:1024])
                    cV.inc(ins)

            @block.vector
            def _(v):
                cV.wait(v, "v")
                cPE.wait(v, "v")
                for k in range(nk):
                    sl = TT[:, k * 4096:(k + 1) * 4096] if widx == 1 else \
                        TT[:, k * 1024:(k + 1) * 1024]
                    ins = v.tensor_scalar(sl, sl, gcols[:, k:k + 1], None, op0=ALU.mult)
                    cV.inc(ins)
                    cV.wait(v, "v")

        load_gcols(1)
        ternarize(1)

        if DBG_PREP <= 5:
            @block.gpsimd
            def _(g):
                cGD.wait(g, "gfinal")
            return nc
        load_gcols(2)
        ternarize(2)

        if DBG_PREP <= 6:
            @block.gpsimd
            def _(g):
                cGD.wait(g, "gfinal")
            return nc


        @block.scalar
        def _(s):
            load_act(s, ID_NLEXP)

        PB = {
            "gi": 0, "mm": {}, "exp": {}, "o": {},
            "a0r": (None, 0), "a1r": (None, 0), "a2r": (None, 0), "a3r": (None, 0),
            "x1r": (None, 0), "zq": (None, 0),
            "q2sl": {0: (None, 0), 1: (None, 0)},
            "bCch": {0: (None, 0), 1: (None, 0)},
            "a2rb": (None, 0), "a3rb": (None, 0), "pt1r": (None, 0),
            "psm": {0: (None, 0), 1: (None, 0)},
            "psxr": {0: (None, 0), 1: (None, 0)},
            "xfA": {0: 0, 1: 0}, "xfB": {0: 0, 1: 0},
            "xldA": {0: 0, 1: 0}, "xldB": {0: 0, 1: 0},
            "x1dma": (None, 0), "a0ev": (None, 0), "evtk0": (None, 0),
            "z2br": (None, 0), "trslot": {0: 0, 1: 0},
            "zqg": (None, 0), "lnfS": (None, 0),
        }

        def ch_wait(eng, key, ch):
            ctr, val = ch
            if ctr is not None and val > 0:
                ctr.wait(eng, key, at=val)
        x1row = outbuf[0:1, 0:1024]
        TR_E = pbf0[:, 2560:2688]
        TR_O = pbf0[:, 3584:3712]

        for seq in range(DBG_SEQS):
            dmac = {}; rs = {}; sqc = {}; rsq = {}; stv = {}; lnx = {}; nrm = {}
            wc = {}; ev96 = {}; cct = {}; fcp = {}; tcp = {}
            trn_last = {}
            pbcp = {0: 0, 1: 0}
            for tt in range(DBG_NT):
                par = tt % 2
                o = par * 8
                xt = outbuf[:, par * 1024:par * 1024 + 1024]
                nbuf = z1T[:, 1024:2048] if par == 0 else PT1[:]
                tbuf = PT0 if par == 0 else a1g[:, 0:1024]
                wsl = P0[0:96, 1024 + par * 128:1152 + par * 128]
                t96 = Q2[:, par * 128:par * 128 + 128]
                ccsl = pbf0[:, 3072 + par * 32:3104 + par * 32]

                @block.sync
                def _(sy, tt=tt, xt=xt, seq=seq):
                    if tt < 2:
                        cV.wait(sy, "sy")
                        cS.wait(sy, "sy")
                        cGD.wait(sy, "sy")
                    else:
                        cS.wait(sy, "sy", at=sqc[tt - 2])
                        cV.wait(sy, "sy", at=nrm[tt - 2])
                    ins = sy.dma_start(xt, dx.ap()[seq, tt * 128:(tt + 1) * 128, :])
                    dmac[tt] = cLD.inc(ins, 16)

                @block.vector
                def _(v, tt=tt, xt=xt, o=o):
                    cLD.wait(v, "v", at=dmac[tt])
                    ins = v.reduce_sum(stat[:, o:o + 1], xt, axis=AX.X)
                    rs[tt] = cV.inc(ins)

                @block.scalar
                def _(s, tt=tt, xt=xt):
                    cLD.wait(s, "s", at=dmac[tt])
                    if tt >= 1:
                        cV.wait(s, "s", at=rsq[tt - 1])
                    ins = s.activation(z1T[:, 0:1024], xt, AF.Square)
                    sqc[tt] = cS.inc(ins)

                @block.vector
                def _(v, tt=tt, o=o):
                    cS.wait(v, "v", at=sqc[tt])
                    ins = v.reduce_sum(stat[:, o + 1:o + 2], z1T[:, 0:1024], axis=AX.X)
                    rsq[tt] = cV.inc(ins)

                @block.vector
                def _(v, tt=tt, o=o):
                    cV.wait(v, "v", at=rsq[tt])
                    ins = v.tensor_scalar(stat[:, o + 2:o + 3], stat[:, o:o + 1],
                                          1.0 / D, None, op0=ALU.mult)
                    c0 = cV.inc(ins)
                    cV.wait(v, "v", at=c0)
                    ins = v.scalar_tensor_tensor(stat[:, o + 3:o + 4], stat[:, o + 2:o + 3],
                                                 0.0, stat[:, o + 2:o + 3],
                                                 op0=ALU.add, op1=ALU.mult)
                    c1 = cV.inc(ins)
                    cV.wait(v, "v", at=c1)
                    ins = v.scalar_tensor_tensor(stat[:, o + 4:o + 5], stat[:, o + 1:o + 2],
                                                 1.0 / D, stat[:, o + 3:o + 4],
                                                 op0=ALU.mult, op1=ALU.subtract)
                    stv[tt] = cV.inc(ins)

                @block.scalar
                def _(s, tt=tt, o=o):
                    cV.wait(s, "s", at=stv[tt])
                    ins = s.activation(stat[:, o + 5:o + 6], stat[:, o + 4:o + 5],
                                       AF.Ln, bias=epsv[:])
                    c0 = cS.inc(ins)
                    cS.wait(s, "s", at=c0)
                    ins = s.activation(stat[:, o + 6:o + 7], stat[:, o + 5:o + 6],
                                       AF.Exp, scale=-0.5)
                    lnx[tt] = cS.inc(ins)

                @block.vector
                def _(v, tt=tt, xt=xt, nbuf=nbuf, o=o):
                    cS.wait(v, "v", at=lnx[tt])
                    cLD.wait(v, "v", at=dmac[tt])
                    if tt >= 2:
                        cPE.wait(v, "v", at=trn_last[tt - 2])
                    else:
                        cPE.wait(v, "v")
                    ins = v.tensor_scalar(nbuf, xt, stat[:, o + 2:o + 3],
                                          stat[:, o + 6:o + 7],
                                          op0=ALU.subtract, op1=ALU.mult)
                    nrm[tt] = cV.inc(ins)

                for c in range(8):
                    @block.tensor
                    def _(t, c=c, tt=tt, nbuf=nbuf):
                        cV.wait(t, "t", at=nrm[tt])
                        if pbcp[c % 2] > 0:
                            cV.wait(t, "t", at=pbcp[c % 2])
                        ins = t.transpose(pbf0[:, (c % 2) * 1024:(c % 2) * 1024 + 128],
                                          nbuf[:, c * 128:(c + 1) * 128], identbf[:])
                        trq = cPE.inc(ins)
                        if c == 7:
                            trn_last[tt] = trq

                        @block.vector
                        def _(v, c=c, tt=tt, tbuf=tbuf, trq=trq):
                            cPE.wait(v, "v", at=trq)
                            if c == 0 and tt >= 2:
                                cPE.wait(v, "v", at=wc[tt - 2])
                            ins = v.tensor_copy(tbuf[:, c * 128:(c + 1) * 128],
                                                pbf0[:, (c % 2) * 1024:(c % 2) * 1024 + 128])
                            tcp[tt] = cV.inc(ins)
                            pbcp[c % 2] = tcp[tt]

                @block.tensor
                def _(t, tt=tt, tbuf=tbuf, wsl=wsl):
                    cV.wait(t, "t", at=tcp[tt])
                    if tt >= 2:
                        cS.wait(t, "t", at=ev96[tt - 2])
                    else:
                        cS.wait(t, "t")
                    for c in range(8):
                        ins = t.matmul(wsl, wcat8[:, c * 96:(c + 1) * 96],
                                       tbuf[:, c * 128:(c + 1) * 128],
                                       start=(c == 0), stop=(c == 7))
                    wc[tt] = cPE.inc(ins)

                @block.scalar
                def _(s, tt=tt, wsl=wsl, t96=t96):
                    cPE.wait(s, "s", at=wc[tt])
                    if tt >= 2:
                        cV.wait(s, "s", at=fcp[tt - 2])
                        cPE.wait(s, "s", at=cct[tt - 2])
                    ins = s.activation(t96[0:96, :], wsl, AF.Identity,
                                       bias=bconst[0:96, :])
                    ev96[tt] = cS.inc(ins)

                @block.tensor
                def _(t, tt=tt, t96=t96, ccsl=ccsl):
                    cS.wait(t, "t", at=ev96[tt])
                    if tt >= 2:
                        cV.wait(t, "t", at=fcp[tt - 2])
                    ins = t.transpose(ccsl, t96[32:64, :], identbf[32:64, 32:64])
                    cct[tt] = cPE.inc(ins)

                @block.vector
                def _(v, tt=tt, t96=t96, ccsl=ccsl):
                    cS.wait(v, "v", at=ev96[tt])
                    cPE.wait(v, "v", at=cct[tt])
                    tc = slice(tt * 128, (tt + 1) * 128)
                    v.tensor_copy(cqT[:, tc], t96[0:32, :])
                    v.tensor_copy(CKV2[0:32, tc], t96[32:64, :])
                    v.tensor_copy(CKV2[32:64, tc], t96[64:96, :])
                    v.tensor_copy(CKV2[64:96, tc], t96[32:64, :])
                    v.tensor_copy(CKV2[96:128, tc], t96[64:96, :])
                    ins = v.tensor_copy(ckvkm[:, tt * 33:tt * 33 + 32], ccsl)
                    fcp[tt] = cV.inc(ins)
            PB["zqg"] = (cV, rsq[DBG_NT - 1])

            @block.vector
            def _(v):
                cV.wait(v, "v")
                cPE.wait(v, "v")
                v.memset(Mk2[:], 0.0)
                ins = v.scalar_tensor_tensor(a1g[0:64, 0:4096], CKV2[0:64, :], 0.0,
                                             CKV2[0:64, :], op0=ALU.add, op1=ALU.mult)
                cV.inc(ins)

            for nb in range(8):
                @block.tensor
                def _(t, nb=nb):
                    cV.wait(t, "t")
                    ins = t.matmul(P0[0:1, 0:512], onesbf[0:64, :],
                                   a1g[0:64, nb * 512:(nb + 1) * 512], start=True, stop=True)
                    cPE.inc(ins)

                @block.vector
                def _(v, nb=nb):
                    cPE.wait(v, "v")
                    ins = v.reduce_max(stat[0:1, 7:8], P0[0:1, 0:512], axis=AX.X)
                    cV.inc(ins)
                    cV.wait(v, "v")
                    ins = v.tensor_tensor(Mk2[:], Mk2[:], stat[0:1, 7:8], op=ALU.max)
                    cV.inc(ins)

            for qb in range(DBG_NQB):
                qsl = slice(qb * QB, (qb + 1) * QB)
                q2e = {}; q2sqd = {}; sqs = {}; rmx = {}; bcc = {}; bcp = {}
                oev = {}; rcp = {}; bct = {}; ovtd = {}; qeffd = {}; olast = {}

                def pro_a(p, qsl=qsl, q2e=q2e, q2sqd=q2sqd, qeffd=qeffd):
                    bnd = (p == 0)

                    @block.tensor
                    def _(t, p=p, bnd=bnd, qsl=qsl):
                        hA, hB = 2 * p, 2 * p + 1
                        if bnd:
                            cV.wait(t, "t")
                            cS.wait(t, "t")
                        else:
                            ch_wait(t, "t", PB["a0r"])
                        t.matmul(P0[0:64, 0:512], wqkr[:, hA * 64:(hA + 1) * 64],
                                 cqT[:, qsl], start=True, stop=True, tile_position=(0, 0))
                        ins = t.matmul(P0[64:128, 0:512], wqkr[:, hB * 64:(hB + 1) * 64],
                                       cqT[:, qsl], start=False, stop=True,
                                       tile_position=(0, 64))
                        qeffd[p] = cPE.inc(ins)

                    @block.vector
                    def _(v, p=p):
                        cPE.wait(v, "v", at=qeffd[p])
                        ch_wait(v, "v", PB["q2sl"][p % 2])
                        ch_wait(v, "v", PB["zq"])
                        ch_wait(v, "v", PB["zqg"])
                        qslc = z1T[:, 512:1024] if p % 2 == 0 else Q2[:]
                        ins = v.tensor_scalar(qslc, P0[:, 0:512], qbias[:, p:p + 1],
                                              None, op0=ALU.add)
                        q2e[p] = cV.inc(ins)
                        PB["a0r"] = (cV, q2e[p])
                        cV.wait(v, "v", at=q2e[p])
                        ins = v.scalar_tensor_tensor(Q2sq, qslc, 0.0, qslc,
                                                     op0=ALU.add, op1=ALU.mult)
                        q2sqd[p] = cV.inc(ins)

                def pro_b(p, q2sqd=q2sqd, sqs=sqs, rmx=rmx):
                    @block.tensor
                    def _(t, p=p):
                        cV.wait(t, "t", at=q2sqd[p])
                        ch_wait(t, "t", PB["a1r"])
                        ch_wait(t, "t", PB["a3r"])
                        ch_wait(t, "t", PB["a3rb"])
                        t.matmul(P0[0:1, 512:1024], onesbf[0:64, :], Q2sq[0:64, :],
                                 start=True, stop=True, tile_position=(0, 0))
                        ins = t.matmul(P0[0:1, 1536:2048], onesbf[64:128, :],
                                       Q2sq[64:128, :], start=True, stop=True,
                                       tile_position=(64, 0))
                        sqs[p] = cPE.inc(ins)
                        PB["zq"] = (cPE, sqs[p])

                    @block.vector
                    def _(v, p=p):
                        cPE.wait(v, "v", at=sqs[p])
                        ch_wait(v, "v", PB["lnfS"])
                        ins = v.reduce_max(stat[0:1, 7:8], P0[0:1, 512:1024], axis=AX.X)
                        cV.inc(ins)
                        ins = v.reduce_max(stat[0:1, 6:7], P0[0:1, 1536:2048], axis=AX.X)
                        c1 = cV.inc(ins)
                        cV.wait(v, "v", at=c1)
                        ins = v.tensor_tensor(stat[0:1, 5:6], stat[0:1, 7:8],
                                              stat[0:1, 6:7], op=ALU.max)
                        c2 = cV.inc(ins)
                        cV.wait(v, "v", at=c2)
                        ins = v.tensor_scalar(stat[0:1, 5:6], stat[0:1, 5:6], Mk2[:],
                                              None, op0=ALU.max)
                        rmx[p] = cV.inc(ins)
                        PB["a1r"] = (cV, rmx[p])
                        PB["a3r"] = (cV, rmx[p])

                def pro_c(p, rmx=rmx, bcc=bcc, bcp=bcp):
                    @block.tensor
                    def _(t, p=p):
                        cV.wait(t, "t", at=rmx[p])
                        ins = t.matmul(P0[:, 512:513], negsc[:], stat[0:1, 5:6],
                                       start=True, stop=True)
                        bcc[p] = cPE.inc(ins)

                    @block.vector
                    def _(v, p=p):
                        cPE.wait(v, "v", at=bcc[p])
                        ch_wait(v, "v", PB["bCch"][p % 2])
                        ins = v.tensor_copy(biasC[:, p % 2:p % 2 + 1], P0[:, 512:513])
                        bcp[p] = cV.inc(ins)
                        PB["a1r"] = (cV, bcp[p])

                def tail_v1(p, oev=oev, rcp=rcp, olast=olast):
                    @block.vector
                    def _(v, p=p):
                        cPE.wait(v, "v", at=PB["o"][olast[p]])
                        ins = v.tensor_copy(recs[0:33, :], P0[0:33, 1024:1536])
                        cV.inc(ins)
                        ins = v.tensor_copy(recs[64:97, :], P0[64:97, 1024:1536])
                        oev[p] = cV.inc(ins)
                        PB["a2r"] = (cV, oev[p])
                        cV.wait(v, "v", at=oev[p])
                        ch_wait(v, "v", PB["x1r"])
                        ch_wait(v, "v", PB["x1dma"])
                        ins = v.reciprocal(x1row[0:1, 0:512], recs[32:33, :])
                        cV.inc(ins)
                        ins = v.reciprocal(x1row[0:1, 512:1024], recs[96:97, :])
                        rcp[p] = cV.inc(ins)

                def tail_pe(p, rcp=rcp, bct=bct, ovtd=ovtd):
                    @block.tensor
                    def _(t, p=p):
                        cV.wait(t, "t", at=rcp[p])
                        ch_wait(t, "t", PB["a3r"])
                        t.matmul(P0[0:32, 1536:2048], onesrow[0:1, 0:32],
                                 x1row[0:1, 0:512], start=True, stop=True,
                                 tile_position=(0, 0))
                        ins = t.matmul(P0[64:96, 1536:2048], onesrow[0:1, 0:32],
                                       x1row[0:1, 512:1024], start=True, stop=True,
                                       tile_position=(0, 64))
                        bct[p] = cPE.inc(ins)
                        PB["x1r"] = (cPE, bct[p])

                    @block.vector
                    def _(v, p=p):
                        hA, hB = 2 * p, 2 * p + 1
                        cPE.wait(v, "v", at=bct[p])
                        cV.wait(v, "v", at=oev[p])
                        ins = v.tensor_tensor(OVT[32 * (hA % 4):32 * (hA % 4) + 32,
                                                  (hA // 4) * 512:(hA // 4) * 512 + 512],
                                              recs[0:32, :], P0[0:32, 1536:2048],
                                              op=ALU.mult)
                        cV.inc(ins)
                        ins = v.tensor_tensor(OVT[32 * (hB % 4):32 * (hB % 4) + 32,
                                                  (hB // 4) * 512:(hB // 4) * 512 + 512],
                                              recs[64:96, :], P0[64:96, 1536:2048],
                                              op=ALU.mult)
                        ovtd[p] = cV.inc(ins)
                        PB["a3r"] = (cV, ovtd[p])

                pro_a(0)
                pro_b(0)
                pro_c(0)

                @block.sync
                def _(sy, qb=qb, seq=seq):
                    for qq in range(2):
                        if PB["xfA"][qq] > 0:
                            cGD.wait(sy, "sy", at=PB["xfA"][qq])
                        tok0 = qb * QB + qq * 128
                        ins = sy.dma_start(x1stage[:, qq * 1024:(qq + 1) * 1024],
                                           dx.ap()[seq, tok0:tok0 + 128, :])
                        PB["xldA"][qq] = cLD.inc(ins, 16)

                for p in range(4):
                    gi0 = PB["gi"]

                    def emit_O(kt2, p=p, gi0=gi0, olast=olast):
                        g2 = gi0 + kt2

                        @block.tensor
                        def _(t, kt2=kt2, p=p, g2=g2):
                            cS.wait(t, "t", at=PB["exp"][g2])
                            if kt2 == 0:
                                ch_wait(t, "t", PB["a2r"])
                                ch_wait(t, "t", PB["a2rb"])
                            PT = PT0 if g2 % 2 == 0 else PT1
                            t.matmul(P0[0:33, 1024:1536],
                                     ckvkm[:, kt2 * 33:kt2 * 33 + 33], PT[:, 0:512],
                                     start=(kt2 == 0), stop=(kt2 == NKT - 1),
                                     tile_position=(0, 0))
                            ins = t.matmul(P0[64:97, 1024:1536],
                                           ckvkm[:, kt2 * 33:kt2 * 33 + 33],
                                           PT[:, 512:1024],
                                           start=(kt2 == 0), stop=(kt2 == NKT - 1),
                                           tile_position=(0, 64))
                            PB["o"][g2] = cPE.inc(ins)
                        if kt2 == NKT - 1:
                            olast[p] = g2

                    for kt in range(NKT):
                        gi = PB["gi"]
                        PB["gi"] = gi + 1

                        @block.tensor
                        def _(t, p=p, kt=kt, gi=gi):
                            if gi - 2 in PB["exp"]:
                                cS.wait(t, "t", at=PB["exp"][gi - 2])
                            if kt == 0:
                                cV.wait(t, "t", at=q2e[p])
                            sl = (gi % 2) * 1024
                            qslc = z1T[:, 512:1024] if p % 2 == 0 else Q2[:]
                            t.matmul(P1[:, sl:sl + 512],
                                     CKV2[0:64, kt * 128:(kt + 1) * 128],
                                     qslc[0:64, :], start=True, stop=True,
                                     tile_position=(0, 0))
                            ins = t.matmul(P1[:, sl + 512:sl + 1024],
                                           CKV2[64:128, kt * 128:(kt + 1) * 128],
                                           qslc[64:128, :], start=True, stop=True,
                                           tile_position=(64, 0))
                            PB["mm"][gi] = cPE.inc(ins)

                        if kt == 1 and p > 0:
                            tail_pe(p - 1)
                        if p < 3:
                            if kt == 4:
                                pro_a(p + 1)
                            elif kt == 6:
                                pro_b(p + 1)
                            elif kt == 8:
                                pro_c(p + 1)

                        @block.scalar
                        def _(s, p=p, kt=kt, gi=gi):
                            cPE.wait(s, "s", at=PB["mm"][gi])
                            if gi - 2 in PB["o"]:
                                cPE.wait(s, "s", at=PB["o"][gi - 2])
                            if kt == 0:
                                cV.wait(s, "s", at=bcp[p])
                            PT = PT0 if gi % 2 == 0 else PT1
                            ins = s.activation(PT[:],
                                               P1[:, (gi % 2) * 1024:(gi % 2) * 1024 + 1024],
                                               AF.Exp, bias=biasC[:, p % 2:p % 2 + 1],
                                               scale=SCALE)
                            PB["exp"][gi] = cS.inc(ins)
                            if kt == NKT - 1:
                                PB["bCch"][p % 2] = (cS, PB["exp"][gi])

                        if kt >= 1:
                            emit_O(kt - 1)
                    emit_O(NKT - 1)
                    PB["q2sl"][p % 2] = (cPE, PB["mm"][gi0 + NKT - 1])
                    tail_v1(p)
                    if p == 3:
                        tail_pe(3)

                opd = {}; elnd = {}; lnfin = {}; trcp = {}; normd = {}
                gel = {}; m1d = {}; sqd = {}; stm = {}; z2d = {}; mm2d = {}
                evd = {}; midd = {}

                @block.sync
                def _(sy, qb=qb, seq=seq):
                    for qq in range(2):
                        if PB["xfB"][qq] > 0:
                            cGD.wait(sy, "sy", at=PB["xfB"][qq])
                        if qq == 0:
                            cPE.wait(sy, "sy", at=bct[3])
                        tok0 = qb * QB + 256 + qq * 128
                        ins = sy.dma_start(outbuf[:, qq * 1024:(qq + 1) * 1024],
                                           dx.ap()[seq, tok0:tok0 + 128, :])
                        PB["xldB"][qq] = cLD.inc(ins, 16)

                def emit_outproj(h2, qq):
                    q4 = h2 * 2 + qq
                    psX = P1[:, qq * 1024:qq * 1024 + 1024]

                    @block.tensor
                    def _(t, q4=q4, qq=qq, psX=psX, h2=h2):
                        cV.wait(t, "t", at=ovtd[3])
                        ch_wait(t, "t", PB["psxr"][qq])
                        for cc in range(2):
                            for nn in range(2):
                                ins = t.matmul(psX[:, nn * 512:(nn + 1) * 512],
                                               OVT[:, cc * 512 + q4 * 128:
                                                   cc * 512 + (q4 + 1) * 128],
                                               WVO[:, cc * 1024 + nn * 512:
                                                   cc * 1024 + (nn + 1) * 512],
                                               start=(cc == 0), stop=(cc == 1))
                        opd[(h2, qq)] = cPE.inc(ins)

                def emit_evacln(h2, qq):
                    psX = P1[:, qq * 1024:qq * 1024 + 1024]
                    base = (x1stage if h2 == 0 else outbuf)[:, qq * 1024:qq * 1024 + 1024]
                    o = qq * 8
                    xld = (PB["xldA"] if h2 == 0 else PB["xldB"])[qq]

                    @block.vector
                    def _(v, h2=h2, qq=qq, psX=psX, base=base, o=o, xld=xld):
                        cPE.wait(v, "v", at=opd[(h2, qq)])
                        cLD.wait(v, "v", at=xld)
                        c0 = cV.inc(v.tensor_tensor(base, base, battnbc[:], op=ALU.add))
                        cV.wait(v, "v", at=c0)
                        ins = v.tensor_tensor(base, base, psX, op=ALU.add)
                        elnd[(h2, qq)] = cV.inc(ins)
                        PB["psxr"][qq] = (cV, elnd[(h2, qq)])
                        cV.wait(v, "v", at=elnd[(h2, qq)])
                        c1 = cV.inc(v.reduce_sum(stat[:, o:o + 1], base, axis=AX.X))
                        ch_wait(v, "v", PB["pt1r"])
                        ins = v.scalar_tensor_tensor(PT1[:], base, 0.0, base,
                                                     op0=ALU.add, op1=ALU.mult)
                        c2 = cV.inc(ins)
                        cV.wait(v, "v", at=c2)
                        ins = v.reduce_sum(stat[:, o + 1:o + 2], PT1[:], axis=AX.X)
                        c3 = cV.inc(ins)
                        cV.wait(v, "v", at=c1)
                        ins = v.tensor_scalar(stat[:, o + 2:o + 3], stat[:, o:o + 1],
                                              1.0 / D, None, op0=ALU.mult)
                        c4 = cV.inc(ins)
                        cV.wait(v, "v", at=c4)
                        ins = v.scalar_tensor_tensor(stat[:, o + 3:o + 4],
                                                     stat[:, o + 2:o + 3], 0.0,
                                                     stat[:, o + 2:o + 3],
                                                     op0=ALU.add, op1=ALU.mult)
                        c5 = cV.inc(ins)
                        cV.wait(v, "v", at=max(c3, c5))
                        ins = v.scalar_tensor_tensor(stat[:, o + 4:o + 5],
                                                     stat[:, o + 1:o + 2], 1.0 / D,
                                                     stat[:, o + 3:o + 4],
                                                     op0=ALU.mult, op1=ALU.subtract)
                        elnd[(h2, qq, "v")] = cV.inc(ins)

                def emit_lnfin(h2, qq):
                    o = qq * 8
                    base = (x1stage if h2 == 0 else outbuf)[:, qq * 1024:qq * 1024 + 1024]

                    @block.scalar
                    def _(s, h2=h2, qq=qq, o=o):
                        cV.wait(s, "s", at=elnd[(h2, qq, "v")])
                        ins = s.activation(stat[:, o + 5:o + 6], stat[:, o + 4:o + 5],
                                           AF.Ln, bias=epsv[:])
                        cs0 = cS.inc(ins)
                        cS.wait(s, "s", at=cs0)
                        ins = s.activation(stat[:, o + 6:o + 7], stat[:, o + 5:o + 6],
                                           AF.Exp, scale=-0.5)
                        lnfin[(h2, qq)] = cS.inc(ins)

                    @block.vector
                    def _(v, h2=h2, qq=qq, o=o, base=base):
                        cS.wait(v, "v", at=lnfin[(h2, qq)])
                        cV.wait(v, "v", at=elnd[(h2, qq, "v")])
                        ch_wait(v, "v", PB["pt1r"])
                        ins = v.tensor_scalar(PT1[:], base, stat[:, o + 2:o + 3],
                                              stat[:, o + 6:o + 7],
                                              op0=ALU.subtract, op1=ALU.mult)
                        normd[(h2, qq)] = cV.inc(ins)

                    for c in range(8):
                        @block.tensor
                        def _(t, c=c, h2=h2, qq=qq):
                            cV.wait(t, "t", at=normd[(h2, qq)])
                            if PB["trslot"][c % 2] > 0:
                                cV.wait(t, "t", at=PB["trslot"][c % 2])
                            if h2 == 0 and qq == 0 and c == 0:
                                cV.wait(t, "t", at=ovtd[3])
                            tr = TR_E if c % 2 == 0 else TR_O
                            ins = t.transpose(tr, PT1[:, c * 128:(c + 1) * 128], identbf[:])
                            trq = cPE.inc(ins)
                            if c == 7:
                                PB["pt1r"] = (cPE, trq)

                            @block.vector
                            def _(v, c=c, h2=h2, qq=qq, trq=trq):
                                cPE.wait(v, "v", at=trq)
                                tr = TR_E if c % 2 == 0 else TR_O
                                ins = v.tensor_copy(
                                    z1T[:, c * 256 + qq * 128: c * 256 + qq * 128 + 128],
                                    tr)
                                trcp[(h2, qq)] = cV.inc(ins)
                                PB["trslot"][c % 2] = trcp[(h2, qq)]

                def emit_ffn1(h2, hooks):
                    @block.scalar
                    def _(s):
                        load_act(s, ID_GELU)

                    for mch in range(32):
                        @block.tensor
                        def _(t, mch=mch, h2=h2):
                            if mch == 0:
                                cV.wait(t, "t", at=trcp[(h2, 1)])
                                if h2 == 0:
                                    cV.wait(t, "t", at=ovtd[3])
                            if mch >= 2:
                                cS.wait(t, "t", at=gel[(h2, mch - 2)])
                            else:
                                ch_wait(t, "t", PB["psm"][mch % 2])
                            psM = P0[:, 1024:1280] if mch % 2 == 0 else P0[:, 1536:1792]
                            for k in range(8):
                                ins = t.matmul(psM,
                                               T1T[:, k * 4096 + mch * 128:
                                                   k * 4096 + (mch + 1) * 128],
                                               z1T[:, k * 256:(k + 1) * 256],
                                               start=(k == 0), stop=(k == 7))
                            m1d[(h2, mch)] = cPE.inc(ins)

                        @block.scalar
                        def _(s, mch=mch, h2=h2):
                            cPE.wait(s, "s", at=m1d[(h2, mch)])
                            psM = P0[:, 1024:1280] if mch % 2 == 0 else P0[:, 1536:1792]
                            ins = s.activation(a1g[:, mch * 256:(mch + 1) * 256], psM,
                                               AF.Gelu, bias=beff1T[:, mch:mch + 1])
                            gel[(h2, mch)] = cS.inc(ins)

                        if mch >= 2:
                            emit_stat(h2, mch - 2)

                        @block.gpsimd
                        def _(g, mch=mch, h2=h2):
                            cS.wait(g, "g", at=gel[(h2, mch)])
                            if mch >= 2:
                                cPE.wait(g, "g", at=stm[(h2, mch - 2)])
                            else:
                                ch_wait(g, "g", PB["z2br"])
                            asl = a1g[:, mch * 256:(mch + 1) * 256]
                            ins = g.tensor_tensor(
                                z2buf[:, (mch % 2) * 256:(mch % 2) * 256 + 256],
                                asl, asl, op=ALU.mult)
                            sqd[(h2, mch)] = cG.inc(ins)

                        if mch in hooks:
                            hooks[mch]()
                    emit_stat(h2, 30)
                    emit_stat(h2, 31)
                    PB["psm"][0] = (cS, gel[(h2, 30)])
                    PB["psm"][1] = (cS, gel[(h2, 31)])

                def emit_stat(h2, mch):
                    @block.tensor
                    def _(t, mch=mch, h2=h2):
                        cG.wait(t, "t", at=sqd[(h2, mch)])
                        cS.wait(t, "t", at=gel[(h2, mch)])
                        if mch == 0:
                            ch_wait(t, "t", PB["a0ev"])
                        t.matmul(P0[0:1, 0:256], onesbf[:],
                                 a1g[:, mch * 256:(mch + 1) * 256],
                                 start=(mch == 0), stop=(mch == 31))
                        ins = t.matmul(P0[0:1, 256:512], onesbf[:],
                                       z2buf[:, (mch % 2) * 256:(mch % 2) * 256 + 256],
                                       start=False, stop=(mch == 31))
                        stm[(h2, mch)] = cPE.inc(ins)

                def emit_mid(h2):
                    muA = ln2scr[0:1, 0:256]
                    muB = ln2scr[0:1, 256:512]
                    mid = {}

                    @block.vector
                    def _(v, h2=h2):
                        cPE.wait(v, "v", at=stm[(h2, 31)])
                        ins = v.tensor_scalar(muA, P0[0:1, 0:256], 1.0 / HID, None,
                                              op0=ALU.mult)
                        c0 = cV.inc(ins)
                        cV.wait(v, "v", at=c0)
                        ins = v.scalar_tensor_tensor(muB, muA, 0.0, muA,
                                                     op0=ALU.add, op1=ALU.mult)
                        c1 = cV.inc(ins)
                        cV.wait(v, "v", at=c1)
                        ins = v.scalar_tensor_tensor(muB, P0[0:1, 256:512], 1.0 / HID,
                                                     muB, op0=ALU.mult, op1=ALU.subtract)
                        mid["v"] = cV.inc(ins)

                    @block.tensor
                    def _(t, h2=h2):
                        cV.wait(t, "t", at=mid["v"])
                        ch_wait(t, "t", PB["a0ev"])
                        ins = t.matmul(P0[:, 512:768], onesrow[:], muA,
                                       start=True, stop=True)
                        mid["bc"] = cPE.inc(ins)

                    @block.scalar
                    def _(s, h2=h2):
                        cV.wait(s, "s", at=mid["v"])
                        cPE.wait(s, "s", at=mid["bc"])
                        load_act(s, ID_NLEXP)
                        ins = s.activation(muA, muB, AF.Ln, bias=epsv[0:1, :])
                        cs0 = cS.inc(ins)
                        cS.wait(s, "s", at=cs0)
                        ins = s.activation(muB, muA, AF.Exp, scale=-0.5)
                        mid["s"] = cS.inc(ins)

                    @block.tensor
                    def _(t, h2=h2):
                        cS.wait(t, "t", at=mid["s"])
                        t.transpose(P0[:, 768:769], muB[0:1, 0:128], identf32[0:1, 0:1])
                        ins = t.transpose(P0[:, 769:770], muB[0:1, 128:256],
                                          identf32[0:1, 0:1])
                        mid["tr"] = cPE.inc(ins)

                    @block.vector
                    def _(v, h2=h2):
                        cPE.wait(v, "v", at=mid["tr"])
                        ins = v.tensor_copy(mu2bc[:], P0[:, 512:768])
                        cV.inc(ins)
                        ins = v.tensor_copy(rstd2c[:], P0[:, 768:770])
                        midd[h2] = cV.inc(ins)

                def emit_mm2(h2):
                    for mch in range(32):
                        @block.vector
                        def _(v, mch=mch, h2=h2):
                            if mch >= 2:
                                cPE.wait(v, "v", at=mm2d[(h2, mch - 2)])
                            else:
                                cV.wait(v, "v", at=midd[h2])
                            ins = v.tensor_tensor(
                                z2buf[:, (mch % 2) * 256:(mch % 2) * 256 + 256],
                                a1g[:, mch * 256:(mch + 1) * 256], mu2bc[:],
                                op=ALU.subtract)
                            z2d[(h2, mch)] = cV.inc(ins)

                        @block.tensor
                        def _(t, mch=mch, h2=h2):
                            cV.wait(t, "t", at=z2d[(h2, mch)])
                            if mch == 0:
                                ch_wait(t, "t", PB["evtk0"])
                                ch_wait(t, "t", PB["a0ev"])
                            for tk in range(2):
                                for nn in range(2):
                                    dst = (P1[:, 1024 + nn * 512:1024 + (nn + 1) * 512]
                                           if tk == 0
                                           else P0[:, nn * 512:(nn + 1) * 512])
                                    ins = t.matmul(
                                        dst,
                                        z2buf[:, (mch % 2) * 256 + tk * 128:
                                              (mch % 2) * 256 + tk * 128 + 128],
                                        W2T[:, mch * 1024 + nn * 512:
                                            mch * 1024 + (nn + 1) * 512],
                                        start=(mch == 0), stop=(mch == 31))
                            mm2d[(h2, mch)] = cPE.inc(ins)
                    PB["z2br"] = (cPE, mm2d[(h2, 31)])

                def emit_evac(h2):
                    for tk in range(2):
                        @block.vector
                        def _(v, tk=tk, h2=h2):
                            cPE.wait(v, "v", at=mm2d[(h2, 31)])
                            cV.wait(v, "v", at=max(midd[h2], elnd[(h2, tk)]))
                            ob = (x1stage if h2 == 0 else outbuf)[:, tk * 1024:(tk + 1) * 1024]
                            src = P1[:, 1024:2048] if tk == 0 else P0[:, 0:1024]
                            ins = v.scalar_tensor_tensor(ob, src, rstd2c[:, tk:tk + 1],
                                                         ob, op0=ALU.mult, op1=ALU.add)
                            c0 = cV.inc(ins)
                            cV.wait(v, "v", at=c0)
                            ins = v.tensor_tensor(ob, ob, b2bc[:], op=ALU.add)
                            evd[(h2, tk)] = cV.inc(ins)
                            if tk == 0:
                                PB["evtk0"] = (cV, evd[(h2, tk)])
                            else:
                                PB["a0ev"] = (cV, evd[(h2, tk)])

                        @block.gpsimd
                        def _(g, tk=tk, h2=h2, qb=qb, seq=seq):
                            cV.wait(g, "g", at=evd[(h2, tk)])
                            tok0 = qb * QB + h2 * 256 + tk * 128
                            src = (x1stage if h2 == 0 else outbuf)[:, tk * 1024:(tk + 1) * 1024]
                            ins = g.dma_start(dy.ap()[seq, tok0:tok0 + 128, :], src)
                            c = cGD.inc(ins, 16)
                            if h2 == 0:
                                PB["xfA"][tk] = c
                            else:
                                PB["xfB"][tk] = c
                                if tk == 0:
                                    PB["x1dma"] = (cGD, c)

                if DBG_FFN:
                    emit_outproj(0, 0)
                    emit_evacln(0, 0)
                    emit_lnfin(0, 0)
                    emit_outproj(0, 1)
                    emit_evacln(0, 1)
                    emit_lnfin(0, 1)
                    emit_ffn1(0, {4: lambda: (emit_outproj(1, 0), emit_evacln(1, 0)),
                                  12: lambda: (emit_outproj(1, 1), emit_evacln(1, 1))})
                    emit_mid(0)
                    emit_mm2(0)
                    emit_lnfin(1, 0)
                    emit_lnfin(1, 1)
                    emit_evac(0)
                    emit_ffn1(1, {})
                    emit_mid(1)
                    emit_mm2(1)
                    emit_evac(1)
                    PB["a2r"] = (cS, gel[(1, 30)])
                    PB["a3r"] = (cS, gel[(1, 31)])
                    PB["a2rb"] = (cV, trcp[(1, 1)])
                    PB["a3rb"] = (cV, trcp[(1, 1)])
                    PB["a0r"] = PB["a0ev"]
                    PB["a1r"] = PB["a0ev"]
                    PB["zq"] = (cPE, m1d[(1, 31)])
                    PB["lnfS"] = (cS, lnfin[(1, 1)])

        if DBG_DUMP:
            @block.gpsimd
            def _(g):
                cV.wait(g, "g")
                cS.wait(g, "g")
                cPE.wait(g, "g")
                pass

            @block.vector
            def _(v):
                cV.wait(v, "v")
                cS.wait(v, "v")
                pass

            @block.gpsimd
            def _(g):
                cV.wait(g, "gd")
                cS.wait(g, "gd")
                cPE.wait(g, "gd")
                cGD.inc(g.dma_start(ddump.ap()[:, 0:2048].bitcast(BF16), CKV2[:]), 16)
                cGD.inc(g.dma_start(ddump.ap()[:, 2048:4096].bitcast(BF16)[0:32, :], cqT[:]), 16)
                cGD.inc(g.dma_start(ddump.ap()[:, 4096:4528].bitcast(BF16)[:, 0:528], ckvkm[:, 0:528]), 16)
                cGD.inc(g.dma_start(ddump.ap()[:, 4528:4784].bitcast(BF16)[:, 0:512], Q2[:]), 16)
                cGD.inc(g.dma_start(ddump.ap()[:, 4784:5296].bitcast(BF16)[:, 0:1024], OVT[:]), 16)
                cGD.inc(g.dma_start(ddump.ap()[:, 5296:5552].bitcast(BF16)[:, 0:512], z1T[:, 0:512]), 16)
                cGD.inc(g.dma_start(ddump.ap()[:, 5552:5553], biasC[:],
                                    allow_slow_non_contiguous=True), 16)
                cGD.inc(g.dma_start(ddump.ap()[:, 5553:5554], stat[:, 6:7],
                                    allow_slow_non_contiguous=True), 16)

        @block.gpsimd
        def _(g):
            cGD.wait(g, "gfinal")

    return nc


_NC_CACHE = {}


def kernel(**inputs):
    x = np.asarray(inputs["L0_attn_h_t"], dtype=np.float32)
    f = lambda k: np.ascontiguousarray(np.asarray(inputs[k], dtype=np.float32))
    wcat = np.ascontiguousarray(np.concatenate(
        [f("L0_attn_W_DQ"), f("L0_attn_W_DKV"), f("L0_attn_W_KR")], axis=1))
    bcat = np.ascontiguousarray(np.concatenate(
        [f("L0_attn_b_DQ"), f("L0_attn_b_DKV"), f("L0_attn_b_KR")]).reshape(96, 1))
    common = {
        "wcat": wcat, "bcat": bcat,
        "wuq": f("L0_attn_W_UQ"), "wuk": f("L0_attn_W_UK"), "wuv": f("L0_attn_W_UV"),
        "wqr": f("L0_attn_W_QR"),
        "buq8": f("L0_attn_b_UQ").reshape(8, 128),
        "bqr8": f("L0_attn_b_QR").reshape(8, 32),
        "buv8": f("L0_attn_b_UV").reshape(8, 128),
        "wo": f("L0_attn_W_O"), "bo": f("L0_attn_b_O").reshape(1, 1024),
        "g1_8": f("L0_ffn_ln1_g").reshape(8, 128),
        "b1ln8": f("L0_ffn_ln1_b").reshape(8, 128),
        "b1row": f("L0_ffn_b1").reshape(1, 4096),
        "g2_32": f("L0_ffn_ln2_g").reshape(32, 128),
        "b2ln32": f("L0_ffn_ln2_b").reshape(32, 128),
        "b2row": f("L0_ffn_b2").reshape(1, 1024),
        "w1": f("L0_ffn_w1"), "w2": f("L0_ffn_w2"),
    }
    if "nc" not in _NC_CACHE:
        _NC_CACHE["nc"] = build_program()
    nc = _NC_CACHE["nc"]
    in_maps = []
    for c in range(NCORES):
        m = dict(common)
        m["x"] = np.ascontiguousarray(x[c * BPC:(c + 1) * BPC])
        in_maps.append(m)
    res = run_bass_kernel_spmd(nc, in_maps, core_ids=list(range(NCORES)))
    return np.concatenate([res.results[c]["y"] for c in range(NCORES)], axis=0)


if __name__ == "__main__":
    build_program()
    print("build OK")
